# version 07
# baseline (speedup 1.0000x reference)
import numpy as np
from contextlib import ExitStack
import concourse.bass as bass
import concourse.mybir as mybir
from concourse.bass_utils import run_bass_kernel_spmd

F32 = mybir.dt.float32
BF16 = mybir.dt.bfloat16
U8 = mybir.dt.uint8
I32 = mybir.dt.int32
AF = mybir.ActivationFunctionType
ALU = mybir.AluOpType
AX = mybir.AxisListType

T = 2048
D = 1024
NCORES = 8
NTT = T // 128
NDC = D // 128


QUEUES = ("pe", "dve", "act", "pool", "sync")
ENG_ATTR = {"pe": "tensor", "dve": "vector", "act": "scalar",
            "pool": "gpsimd", "sync": "sync"}


class _Op:
    __slots__ = ("fn", "deps", "dma", "used", "val", "relaxed", "idx")


class Tok:
    __slots__ = ("q", "op")

    def __init__(self, q, op):
        self.q = q
        self.op = op


def _flat(deps):
    out = []
    stack = list(deps) if isinstance(deps, (list, tuple)) else [deps]
    while stack:
        d = stack.pop()
        if d is None:
            continue
        if isinstance(d, (list, tuple)):
            stack.extend(d)
        else:
            out.append(d)
    return out


class Q:
    def __init__(self, name):
        self.name = name
        self.ops = []

    def emit(self, fn, deps=(), dma=False, relaxed=False):
        ds = _flat(deps)
        op = _Op()
        op.fn = fn
        op.deps = ds
        op.dma = dma
        op.relaxed = relaxed
        op.idx = len(self.ops)
        op.used = False
        op.val = None
        for d in ds:
            d.op.used = True
        self.ops.append(op)
        return Tok(self, op)


class Prog:
    def __init__(self):
        self.q = {n: Q(n) for n in QUEUES}

    def finalize(self, nc, stack):
        for name, q in self.q.items():
            ev, dv = 0, 0
            for op in q.ops:
                if not op.used:
                    continue
                if op.dma:
                    dv += 16
                    op.val = ("d", dv)
                else:
                    ev += 1
                    op.val = ("e", ev)
        sems = {}
        for name in QUEUES:
            sems[(name, "e")] = stack.enter_context(nc.semaphore(f"se_{name}"))
            sems[(name, "d")] = stack.enter_context(nc.semaphore(f"sd_{name}"))
        blk = stack.enter_context(nc.Block())

        def make_body(qname):
            q = self.q[qname]

            def run(eng):
                last = {}
                selfk = int(_os.environ.get("MK_SELFK", "0"))
                for op in q.ops:
                    for dep in op.deps:
                        if dep.op.val is None:
                            raise RuntimeError(
                                f"{qname}: dep on never-valued token "
                                f"(queue {dep.q.name})")
                        kind, val = dep.op.val
                        if dep.q.name == qname \
                                and not op.dma and not dep.op.dma \
                                and (op.relaxed or
                                     (selfk and op.idx - dep.op.idx > selfk)):
                            continue
                        key = (dep.q.name, kind)
                        if last.get(key, 0) >= val:
                            continue
                        eng.wait_ge(sems[key], val)
                        last[key] = val
                    inst = op.fn(eng)
                    if op.used:
                        if inst is None:
                            raise RuntimeError(f"{qname}: used op returned None")
                        kind, _ = op.val
                        inst.then_inc(sems[(qname, kind)], 16 if op.dma else 1)
            return run

        for name in QUEUES:
            getattr(blk, ENG_ATTR[name])(make_body(name))


class Region:

    def __init__(self, name=""):
        self.name = name
        self.w = []
        self.r = []

    def begin_write(self):
        deps = self.w + self.r
        self.w = []
        self.r = []
        return deps

    def end_write(self, toks):
        self.w.extend(_flat([toks]))

    def read_deps(self):
        return list(self.w)

    def add_reader(self, toks):
        self.r.extend(_flat([toks]))


def regs_begin_write(regs):
    deps = []
    for r in regs:
        deps += r.begin_write()
    return deps


def regs_end_write(regs, toks):
    for r in regs:
        r.end_write(toks)


def regs_read_deps(regs):
    deps = []
    for r in regs:
        deps += r.read_deps()
    return deps


def regs_add_reader(regs, toks):
    for r in regs:
        r.add_reader(toks)



class Ctx:
    pass


def make_ctx(nc, stack):
    C = Ctx()
    C.nc = nc
    C.stack = stack
    C.prog = Prog()
    C.pe = lambda fn, deps=(): C.prog.q["pe"].emit(fn, deps)
    C.dver = lambda fn, deps=(): C.prog.q["dve"].emit(fn, deps, relaxed=True)
    C.dve = lambda fn, deps=(): C.prog.q["dve"].emit(fn, deps)
    C.act = lambda fn, deps=(): C.prog.q["act"].emit(fn, deps)
    C.pool = lambda fn, deps=(): C.prog.q["pool"].emit(fn, deps)
    C.sync = lambda fn, deps=(): C.prog.q["sync"].emit(fn, deps)
    C.dma_sync = lambda fn, deps=(): C.prog.q["sync"].emit(fn, deps, dma=True)
    C.dma_act = lambda fn, deps=(): C.prog.q["act"].emit(fn, deps, dma=True)
    C.dma_pool = lambda fn, deps=(): C.prog.q["pool"].emit(fn, deps, dma=True)

    sb = lambda name, shape, dt: stack.enter_context(
        nc.sbuf_tensor(name, shape, dt))
    ps = lambda name, shape, dt: stack.enter_context(
        nc.psum_tensor(name, shape, dt))

    C.x_res = sb("x_res", [128, NTT * D], F32)
    C.xnT = sb("xnT", [128, NDC * T], BF16)
    C.arena = sb("arena", [128, 43008], BF16)
    C.wbuf = sb("wbuf", [128, 5120], BF16)
    C.scr = sb("scr", [128, 1024], F32)
    C.small = sb("small", [128, 512], F32)
    C.ident = sb("ident", [128, 128], BF16)
    C.identi = sb("identi", [128, 128], I32)
    C.onescol = sb("onescol", [128, 1], BF16)
    C.onesrow = sb("onesrow", [1, 128], BF16)
    C.onesrowf = sb("onesrowf", [1, 128], F32)
    C.rowp = sb("rowp", [1, 3712], BF16)
    C.rowpf = sb("rowpf", [1, 256], F32)

    C.pb = [ps(f"pb{i}", [128, 512], F32) for i in range(6)]
    C.tb = [ps(f"tb{i}", [128, 1024], BF16) for i in range(2)]
    C.pb_reg = [Region(f"pb{i}") for i in range(6)]
    C.tb_reg = [Region(f"tb{i}") for i in range(2)]

    C.xres_reg = [Region(f"xres{tt}") for tt in range(NTT)]
    C.xnT_reg = Region("xnT")
    C.ar_reg = {k: Region(k) for k in ("A", "B", "C", "D")}
    C.wslot = [Region(f"w{i}") for i in range(5)]
    C.scr_reg = Region("scr")
    C.rowp_reg = Region("rowp")

    C.identJ = sb("identJ", [128, 128], BF16)
    t_iota = C.pool(lambda g: g.iota(C.identi[:], [[1, 128]], base=0,
                                     channel_multiplier=-1))
    C.t_ident = C.dve(lambda v: v.tensor_single_scalar(
        C.ident[:], C.identi[:], 0, ALU.is_equal), deps=[t_iota])
    t_iota2 = C.pool(lambda g: g.iota(C.identi[:], [[1, 128]], base=-127,
                                      channel_multiplier=1), deps=[C.t_ident])
    C.t_identJ = C.dve(lambda v: v.tensor_single_scalar(
        C.identJ[:], C.identi[:], 0, ALU.is_equal), deps=[t_iota2])
    C.zmm = sb("zmm", [128, 128], BF16)
    C.t_zmm = C.pool(lambda g: g.memset(C.zmm[:, :], 0.0))
    C.t_ones = C.pool(lambda g: g.memset(C.onescol[:], 1.0))
    C.t_onesr = C.pool(lambda g: g.memset(C.onesrow[:], 1.0))
    C.t_onesrf = C.pool(lambda g: g.memset(C.onesrowf[:], 1.0))
    C.small_next = 0
    C.rowp_next = 0
    C.rowpf_next = 0
    C.qcols = C.small[:, 416:512]
    C.qreg = Region("qcols")
    return C


def mm_close(C, out_ap, ncols, deps):
    return C.pe(lambda t: t.matmul(
        out_ap, C.zmm[:, :], C.xnT[:, 0:ncols],
        start=False, stop=True), deps=_flat(deps) + [C.t_zmm])


AR_A = 0
AR_B = 16384
AR_C = 32768
AR_D = 40960


def ar(C, base, off, n):
    return C.arena[:, base + off: base + off + n]


def wb(C, off, n):
    return C.wbuf[:, off:off + n]


def wb_slots(C, off, n):
    return [C.wslot[i] for i in range(off // 1024, (off + n + 1023) // 1024)]


def smcol(C, n=1):
    i = C.small_next
    C.small_next += n
    assert C.small_next <= 416, "small pool overflow"
    return C.small[:, i:i + n]


def rowb_alloc(C, n):
    i = C.rowp_next
    C.rowp_next += n
    assert C.rowp_next <= 3712, "row pool overflow"
    return C.rowp[0:1, i:i + n]


def rowf_alloc(C, n):
    i = C.rowpf_next
    C.rowpf_next += n
    assert C.rowpf_next <= 256, "rowf pool overflow"
    return C.rowpf[0:1, i:i + n]


def layer_reset_pools(C):
    C.small_next = 0
    C.rowp_next = 0
    C.rowpf_next = 0



def tb_transpose_batch(C, src_fns, dst_fn, deps, tbi=0, src_rows=128,
                       src_cols=128):
    assert len(src_fns) <= 8
    tbt = C.tb[tbi]
    reg = C.tb_reg[tbi]
    wdeps = reg.begin_write() + _flat(deps) + [C.t_ident]
    mms = []
    for b, src_fn in enumerate(src_fns):
        mm = C.pe(lambda t, b=b, src_fn=src_fn: t.transpose(
            tbt[0:src_cols, b * src_rows:(b + 1) * src_rows], src_fn(),
            C.ident[0:src_rows, 0:src_rows]),
            deps=(wdeps if b == 0 else []))
        mms.append(mm)
    reg.end_write(mms)
    nb = len(src_fns)
    ev = C.dve(lambda v: v.tensor_copy(dst_fn(nb),
                                       tbt[0:src_cols, 0:nb * src_rows]),
               deps=[mms[-1]])
    reg.add_reader(ev)
    return ev



def dbg_dump(C, tag, deps):
    if _os.environ.get("MK_DBG_AT") != tag:
        return []
    nc = C.nc
    da = nc.dram_tensor("dbg_arena", [128, 43008], BF16, kind="ExternalOutput").ap()
    ds = nc.dram_tensor("dbg_small", [128, 512], F32, kind="ExternalOutput").ap()
    dc = nc.dram_tensor("dbg_scr", [128, 1024], F32, kind="ExternalOutput").ap()
    t1 = C.dma_sync(lambda s: s.dma_start(da, C.arena[:, :]), deps=deps)
    t2 = C.dma_sync(lambda s: s.dma_start(ds, C.small[:, :]), deps=deps)
    t3 = C.dma_sync(lambda s: s.dma_start(dc, C.scr[:, :]), deps=deps)
    for k in C.ar_reg:
        C.ar_reg[k].add_reader([t1])
    C.scr_reg.add_reader([t3])
    C.sync(lambda s: s.nop(), deps=[t1, t2, t3])
    return [t1, t2, t3]

def bcast_col(C, src_row_fn, n, dst_f32_ap, deps, pbi=5):
    assert n <= 512
    reg = C.pb_reg[pbi]
    wd = reg.begin_write()
    mm = C.pe(lambda t: t.matmul(C.pb[pbi][:, 0:n], C.onesrow[:],
                                 src_row_fn(), start=True, stop=True),
              deps=wd + _flat(deps) + [C.t_onesr])
    reg.end_write(mm)
    ev = C.dve(lambda v: v.tensor_copy(dst_f32_ap, C.pb[pbi][:, 0:n]),
               deps=[mm])
    reg.add_reader(ev)
    return ev


def load_cast(C, dst_ap, src_ap, deps=()):
    return C.dma_pool(lambda g: g.dma_start(dst_ap, src_ap), deps=deps)


def load_wT_rowblock(C, Wap, rb, K, dst_off, stage_off, deps=(), tbi=1):
    nkc = K // 128
    stg = wb(C, stage_off, K)
    sregs = wb_slots(C, stage_off, K)
    dregs = wb_slots(C, dst_off, nkc * 128)
    t_ld = load_cast(C, stg, Wap[rb * 128:(rb + 1) * 128, 0:K],
                     deps=_flat(deps) + regs_begin_write(sregs))
    regs_end_write(sregs, t_ld)
    wdeps = regs_begin_write(dregs)
    evs = []
    for k0 in range(0, nkc, 8):
        nb = min(8, nkc - k0)
        ev = tb_transpose_batch(
            C,
            [(lambda kc=kc: stg[:, kc * 128:(kc + 1) * 128])
             for kc in range(k0, k0 + nb)],
            lambda nb, k0=k0: wb(C, dst_off + k0 * 128, nb * 128),
            deps=[t_ld] + (wdeps if k0 == 0 else []), tbi=tbi)
        evs.append(ev)
    regs_add_reader(sregs, evs)
    regs_end_write(dregs, evs)
    return evs



def emit_load_x(C, x_in):
    xv = C.x_res[:, :].rearrange("p (tt d) -> p tt d", d=D)
    src = x_in.rearrange("(tt p) d -> p tt d", p=128)
    tok = C.dma_sync(lambda s: s.dma_start(xv, src))
    for tt in range(NTT):
        C.xres_reg[tt].begin_write()
        C.xres_reg[tt].end_write(tok)
    return tok


def emit_store_x(C, x_out):
    xv = C.x_res[:, :].rearrange("p (tt d) -> p tt d", d=D)
    dst = x_out.rearrange("(tt p) d -> p tt d", p=128)
    deps = []
    for tt in range(NTT):
        deps += C.xres_reg[tt].read_deps()
        deps += C.xres_reg[tt].r
    tok = C.dma_sync(lambda s: s.dma_start(dst, xv), deps=deps)
    C.sync(lambda s: s.nop(), deps=[tok])
    return tok


def emit_ln(C, tile_hook=None):
    xnT_wdeps = C.xnT_reg.begin_write()
    ev_toks = []
    regD = C.ar_reg["D"]
    dD = regD.begin_write()
    xbuf = [ar(C, AR_D, 0, 1024), ar(C, AR_D, 1024, 1024)]
    xbuf_tok = [dD, dD]
    xnT_v = C.xnT[:, :].rearrange("p (dc t) -> p dc t", t=T)
    scr_wd = C.scr_reg.begin_write()
    sq_toks = []
    for tt in range(NTT):
        xt = C.x_res[:, tt * D:(tt + 1) * D]
        xdeps = C.xres_reg[tt].read_deps()
        cols = smcol(C, 6)
        s_c, q_c, m_c, e_c, n_c, r_c = (cols[:, i:i + 1] for i in range(6))
        sqscr = C.scr[:, 0:1024]
        t_s = C.dve(lambda v, xt=xt, s_c=s_c: v.reduce_sum(s_c, xt, AX.X),
                    deps=xdeps)
        t_sq = C.act(lambda s, xt=xt, q_c=q_c, sqscr=sqscr: s.activation(
            sqscr, xt, AF.Square, accum_out=q_c),
            deps=xdeps + (scr_wd if tt == 0 else []))
        sq_toks.append(t_sq)
        t_m = C.dve(lambda v, s_c=s_c, m_c=m_c: v.tensor_single_scalar(
            m_c, s_c, 1.0 / D, ALU.mult), deps=[t_s])
        t_e = C.dve(lambda v, q_c=q_c, e_c=e_c: v.tensor_scalar(
            e_c, q_c, 1.0 / D, 1e-5, ALU.mult, ALU.add), deps=[t_sq])
        t_n = C.dve(lambda v, m_c=m_c, n_c=n_c: v.scalar_tensor_tensor(
            n_c, m_c, -1.0, m_c, ALU.mult, ALU.mult), deps=[t_m])
        t_v = C.dve(lambda v, n_c=n_c, e_c=e_c: v.tensor_add(
            n_c, n_c, e_c), deps=[t_n, t_e])
        t_r = C.dve(lambda v, n_c=n_c, r_c=r_c: v.reciprocal(r_c, n_c),
                    deps=[t_v])
        t_rs = C.act(lambda s, r_c=r_c: s.activation(r_c, r_c, AF.Sqrt),
                     deps=[t_r])
        xb = xbuf[tt % 2]
        t_xn = C.dve(lambda v, xb=xb, xt=xt, m_c=m_c, r_c=r_c: v.tensor_scalar(
            xb, xt, m_c, r_c, ALU.subtract, ALU.mult),
            deps=[t_m, t_rs] + xdeps + xbuf_tok[tt % 2])
        C.xres_reg[tt].add_reader(t_xn)
        ev = tb_transpose_batch(
            C,
            [(lambda xb=xb, dc=dc: xb[:, dc * 128:(dc + 1) * 128])
             for dc in range(NDC)],
            lambda nb, tt=tt: xnT_v[:, :, tt * 128:(tt + 1) * 128],
            deps=[t_xn] + (xnT_wdeps if tt == 0 else []), tbi=tt % 2)
        xbuf_tok[tt % 2] = [ev]
        ev_toks.append(ev)
        if tile_hook is not None:
            tile_hook(tt, xt, m_c, r_c, [t_m, t_rs] + xdeps)
    C.scr_reg.end_write(sq_toks)
    C.xnT_reg.end_write(ev_toks)
    regD.end_write(ev_toks)
    return ev_toks


def xnT_tile(C, dc, t0, n):
    return C.xnT[:, dc * T + t0: dc * T + t0 + n]



def l1_declare(nc):
    W = {}
    W["W_qkv"] = nc.dram_tensor("L1_attn_W_qkv", [3 * D, D], F32,
                                kind="ExternalInput").ap()
    W["W_out"] = nc.dram_tensor("L1_attn_W_out", [D, D], F32,
                                kind="ExternalInput").ap()
    W["b_out"] = nc.dram_tensor("L1_attn_b_out", [D], F32,
                                kind="ExternalInput").ap()
    W["k0"] = nc.dram_tensor("L1_ffn_k0", [2 * D, 4096], U8,
                             kind="ExternalInput").ap()
    W["k1"] = nc.dram_tensor("L1_ffn_k1", [4096, 4096], U8,
                             kind="ExternalInput").ap()
    W["k2"] = nc.dram_tensor("L1_ffn_k2", [4096, D], U8,
                             kind="ExternalInput").ap()
    W["s0"] = nc.dram_tensor("L1_ffn_s0", [1], F32, kind="ExternalInput").ap()
    W["s1"] = nc.dram_tensor("L1_ffn_s1", [1], F32, kind="ExternalInput").ap()
    W["s2"] = nc.dram_tensor("L1_ffn_s2", [1], F32, kind="ExternalInput").ap()
    W["k0w"] = nc.dram_tensor("l1_k0w", [D, 4096], BF16).ap()
    W["k1w"] = nc.dram_tensor("l1_k1w", [4096, 4096], BF16).ap()
    W["k2w"] = nc.dram_tensor("l1_k2w", [4096, D], BF16).ap()
    return W


def emit_l1_prep(C, W):
    u8a = wb(C, 0, 1024).bitcast(U8)
    u8b = wb(C, 1024, 1024).bitcast(U8)
    bfo = wb(C, 2048, 2048)
    rA = wb_slots(C, 0, 1024)
    rB = wb_slots(C, 1024, 1024)
    rO = wb_slots(C, 2048, 2048)
    out_toks = []
    for dc in range(8):
        for h in range(2):
            t_a = C.dma_sync(lambda s, dc=dc, h=h: s.dma_start(
                u8a, W["k0"][dc * 128:(dc + 1) * 128,
                             h * 2048:(h + 1) * 2048]),
                deps=regs_begin_write(rA))
            regs_end_write(rA, t_a)
            t_b = C.dma_sync(lambda s, dc=dc, h=h: s.dma_start(
                u8b, W["k0"][D + dc * 128: D + (dc + 1) * 128,
                             h * 2048:(h + 1) * 2048]),
                deps=regs_begin_write(rB))
            regs_end_write(rB, t_b)
            t_c = C.dve(lambda g: g.tensor_tensor(bfo, u8a, u8b,
                                                  ALU.subtract),
                        deps=[t_a, t_b] + regs_begin_write(rO))
            regs_end_write(rO, t_c)
            regs_add_reader(rA, t_c)
            regs_add_reader(rB, t_c)
            t_o = C.dma_sync(lambda s, dc=dc, h=h: s.dma_start(
                W["k0w"][dc * 128:(dc + 1) * 128, h * 2048:(h + 1) * 2048],
                bfo), deps=[t_c])
            regs_add_reader(rO, t_o)
            out_toks.append(t_o)
    for (src, dst, rows, cols) in ((W["k1"], W["k1w"], 4096, 4096),
                                   (W["k2"], W["k2w"], 4096, D)):
        nh = max(1, cols // 2048)
        w = cols // nh
        for dc in range(rows // 128):
            for h in range(nh):
                t_a = C.dma_sync(lambda s, src=src, dc=dc, h=h, w=w:
                                 s.dma_start(u8a[:, 0:w],
                                             src[dc * 128:(dc + 1) * 128,
                                                 h * w:(h + 1) * w]),
                                 deps=regs_begin_write(rA))
                regs_end_write(rA, t_a)
                t_c = C.dve(lambda g, w=w: g.tensor_scalar(
                    bfo[:, 0:w], u8a[:, 0:w], 2.0, -1.0, ALU.mult, ALU.add),
                    deps=[t_a] + regs_begin_write(rO))
                regs_end_write(rO, t_c)
                regs_add_reader(rA, t_c)
                t_o = C.dma_sync(lambda s, dst=dst, dc=dc, h=h, w=w:
                                 s.dma_start(dst[dc * 128:(dc + 1) * 128,
                                                 h * w:(h + 1) * w],
                                             bfo[:, 0:w]), deps=[t_c])
                regs_add_reader(rO, t_o)
                out_toks.append(t_o)
    return out_toks


def scalar_cols(C, rows_f32_dram, deps=()):
    n = len(rows_f32_dram)
    rf = rowf_alloc(C, n)
    ts = []
    for i, ap in enumerate(rows_f32_dram):
        ts.append(C.dma_sync(lambda s, i=i, ap=ap: s.dma_start(
            rf[:, i:i + 1], ap.unsqueeze(0)), deps=deps))
    rb = rowb_alloc(C, n)
    t_rb = C.dve(lambda v: v.tensor_copy(rb, rf), deps=ts)
    cols = smcol(C, n)
    t_bc = bcast_col(C, lambda: rb, n, cols, [t_rb])
    return cols, t_bc


def emit_l1_ffn(C, W, prep_toks):
    scol, t_sc0 = scalar_cols(C, [W["s0"], W["s1"], W["s2"]])
    scol2 = smcol(C, 3)
    t_sc = C.dve(lambda v: v.tensor_single_scalar(
        scol2, scol, 2.0, ALU.mult), deps=[t_sc0])
    s0c, s1c, s2c = (scol2[:, i:i + 1] for i in range(3))

    xn_deps = C.xnT_reg.read_deps()
    regA = C.ar_reg["A"]
    regB = C.ar_reg["B"]
    wbs = [(wb(C, 0, 1024), wb_slots(C, 0, 1024)),
           (wb(C, 1024, 1024), wb_slots(C, 1024, 1024))]
    add_toks = []
    for ts in range(4):
        t0 = ts * 512
        h1_toks = []
        wrA = regA.begin_write()
        for g in range(8):
            mm_last = [None] * 4
            for dc in range(8):
                wt, wregs = wbs[dc % 2]
                t_w = load_cast(
                    C, wt[:, 0:512],
                    W["k0w"][dc * 128:(dc + 1) * 128, g * 512:(g + 1) * 512],
                    deps=regs_begin_write(wregs) +
                    (prep_toks if ts == 0 and g == 0 and dc < 2 else []))
                regs_end_write(wregs, t_w)
                for f in range(4):
                    bankd = C.pb_reg[f].begin_write() if dc == 0 else ()
                    mm = C.pe(lambda t, wt=wt, f=f, dc=dc, t0=t0: t.matmul(
                        C.pb[f][:, :],
                        wt[:, f * 128:(f + 1) * 128],
                        xnT_tile(C, dc, t0, 512),
                        start=(dc == 0), stop=(dc == 7)),
                        deps=[t_w] + list(bankd) +
                        (xn_deps if dc == 0 and f == 0 else []))
                    regs_add_reader(wregs, mm)
                    mm_last[f] = mm
            for f in range(4):
                fc = g * 4 + f
                C.pb_reg[f].end_write(mm_last[f])
                ev = C.act(lambda s, f=f, fc=fc: s.activation(
                    ar(C, AR_A, fc * 512, 512), C.pb[f][:, :], AF.Relu,
                    scale=s0c),
                    deps=[mm_last[f], t_sc] + (wrA if g == 0 and f == 0 else []))
                C.pb_reg[f].add_reader(ev)
                h1_toks.append(ev)
        regA.end_write(h1_toks)
        h2_toks = []
        wrB = regB.begin_write()
        for g in range(8):
            mm_last = [None] * 4
            for kc in range(32):
                wt, wregs = wbs[kc % 2]
                t_w = load_cast(
                    C, wt[:, 0:512],
                    W["k1w"][kc * 128:(kc + 1) * 128, g * 512:(g + 1) * 512],
                    deps=regs_begin_write(wregs))
                regs_end_write(wregs, t_w)
                for f in range(4):
                    bankd = C.pb_reg[f].begin_write() if kc == 0 else ()
                    mm = C.pe(lambda t, wt=wt, f=f, kc=kc: t.matmul(
                        C.pb[f][:, :],
                        wt[:, f * 128:(f + 1) * 128],
                        ar(C, AR_A, kc * 512, 512),
                        start=(kc == 0), stop=(kc == 31)),
                        deps=[t_w] + list(bankd) +
                        (h1_toks if kc == 0 and f == 0 else []))
                    regs_add_reader(wregs, mm)
                    mm_last[f] = mm
            for f in range(4):
                fc = g * 4 + f
                C.pb_reg[f].end_write(mm_last[f])
                regA.add_reader(mm_last[f])
                ev = C.act(lambda s, f=f, fc=fc: s.activation(
                    ar(C, AR_B, fc * 512, 512), C.pb[f][:, :], AF.Relu,
                    scale=s1c),
                    deps=[mm_last[f], t_sc] + (wrB if g == 0 and f == 0 else []))
                C.pb_reg[f].add_reader(ev)
                h2_toks.append(ev)
        regB.end_write(h2_toks)
        for it in range(4):
            tt = ts * 4 + it
            mm_last = [None, None]
            for kc in range(32):
                wt, wregs = wbs[kc % 2]
                t_w = load_cast(
                    C, wt[:, 0:1024],
                    W["k2w"][kc * 128:(kc + 1) * 128, :],
                    deps=regs_begin_write(wregs))
                regs_end_write(wregs, t_w)
                for f in range(2):
                    bankd = C.pb_reg[f].begin_write() if kc == 0 else ()
                    mm = C.pe(lambda t, wt=wt, f=f, kc=kc, it=it: t.matmul(
                        C.pb[f][:, :],
                        ar(C, AR_B, kc * 512 + it * 128, 128),
                        wt[:, f * 512:(f + 1) * 512],
                        start=(kc == 0), stop=(kc == 31)),
                        deps=[t_w] + list(bankd) +
                        (h2_toks if kc == 0 and f == 0 else []))
                    regs_add_reader(wregs, mm)
                    mm_last[f] = mm
            for f in range(2):
                C.pb_reg[f].end_write(mm_last[f])
                regB.add_reader(mm_last[f])
                xd = C.xres_reg[tt].begin_write()
                sl = slice(tt * D + f * 512, tt * D + (f + 1) * 512)
                ev = C.dve(lambda v, f=f, sl=sl: v.scalar_tensor_tensor(
                    C.x_res[:, sl], C.pb[f][:, :], s2c, C.x_res[:, sl],
                    ALU.mult, ALU.add),
                    deps=[mm_last[f], t_sc] + xd)
                C.pb_reg[f].add_reader(ev)
                C.xres_reg[tt].end_write(ev)
                add_toks.append(ev)
    return add_toks


def emit_l1_attn(C, W):
    xn_deps = C.xnT_reg.read_deps()
    regA, regB, regC, regD = (C.ar_reg[k] for k in "ABCD")

    kv_evt = {"k": [], "v": []}
    for name, base_eb in (("k", 8), ("v", 16)):
        dst_base = AR_A if name == "k" else AR_B
        reg = regA if name == "k" else regB
        wr_dst = reg.begin_write()
        for g in range(2):
            tev = []
            for j in range(4):
                evs = load_wT_rowblock(C, W["W_qkv"], base_eb + g * 4 + j, D,
                                       j * 1024, 4096)
                tev += evs
            slab = wb(C, 0, 4096).rearrange("p (eb x) -> p eb x", x=1024)
            sregs = wb_slots(C, 0, 4096)
            for tt in range(NTT):
                bi = tt % 2
                bankd = C.pb_reg[bi].begin_write()
                mml = None
                for dc in range(NDC):
                    mm = C.pe(lambda t, bi=bi, dc=dc, tt=tt, slab=slab:
                              t.matmul(
                                  C.pb[bi][:, :],
                                  xnT_tile(C, dc, tt * 128, 128),
                                  slab[:, :, dc * 128:(dc + 1) * 128],
                                  start=(dc == 0), stop=(dc == 7)),
                              deps=(tev + list(bankd) + xn_deps)
                              if dc == 0 else [tev[-1]])
                    mml = mm
                C.pb_reg[bi].end_write(mml)
                regs_add_reader(sregs, mml)
                if name == "k":
                    ev = C.dve(lambda v, bi=bi, tt=tt, g=g, dst_base=dst_base:
                               v.tensor_scalar(
                                   ar(C, dst_base, tt * 1024 + g * 512, 512),
                                   C.pb[bi][:, :], 0.0, 1e-6, ALU.max, ALU.add),
                               deps=[mml] + (wr_dst if tt == 0 and g == 0 else []))
                else:
                    ev = C.dve(lambda v, bi=bi, tt=tt, g=g, dst_base=dst_base:
                               v.tensor_copy(
                                   ar(C, dst_base, tt * 1024 + g * 512, 512),
                                   C.pb[bi][:, :]),
                               deps=[mml] + (wr_dst if tt == 0 and g == 0 else []))
                C.pb_reg[bi].add_reader(ev)
                kv_evt[name].append(ev)
        reg.end_write(kv_evt[name])

    ksum_toks = []
    for g in range(2):
        bi = 2 + g
        bankd = C.pb_reg[bi].begin_write()
        mml = None
        for tt in range(NTT):
            mm = C.pe(lambda t, bi=bi, tt=tt, g=g: t.matmul(
                C.pb[bi][0:1, :], C.onescol[:, :],
                ar(C, AR_A, tt * 1024 + g * 512, 512),
                start=(tt == 0), stop=(tt == 15)),
                deps=(list(bankd) + kv_evt["k"] + [C.t_ones])
                if tt == 0 else ())
            mml = mm
        C.pb_reg[bi].end_write(mml)
        regA.add_reader(mml)
        ksum_toks.append(mml)
    ksrow = rowb_alloc(C, 1024)
    t_kr = []
    for g in range(2):
        tok = C.dve(lambda v, g=g: v.tensor_copy(
            ksrow[:, g * 512:(g + 1) * 512], C.pb[2 + g][0:1, :]),
            deps=[ksum_toks[g]])
        C.pb_reg[2 + g].add_reader(tok)
        t_kr.append(tok)
    scr_wd = C.scr_reg.begin_write()
    ksumB = C.scr[:, 0:1024]
    t_kb = []
    for g in range(2):
        t_kb.append(bcast_col(C, lambda g=g: ksrow[:, g * 512:(g + 1) * 512],
                              512, ksumB[:, g * 512:(g + 1) * 512],
                              [t_kr[g]] + (scr_wd if g == 0 else []), pbi=4))
    C.scr_reg.end_write(t_kb)

    regC_w = regC.begin_write()
    kv_toks = []
    for dkc in range(8):
        for dvh in range(2):
            bi = (dkc * 2 + dvh) % 2
            bankd = C.pb_reg[bi].begin_write()
            mml = None
            for tt in range(NTT):
                mm = C.pe(lambda t, bi=bi, dkc=dkc, dvh=dvh, tt=tt: t.matmul(
                    C.pb[bi][:, :],
                    ar(C, AR_A, tt * 1024 + dkc * 128, 128),
                    ar(C, AR_B, tt * 1024 + dvh * 512, 512),
                    start=(tt == 0), stop=(tt == 15)),
                    deps=(list(bankd) + kv_evt["k"] + kv_evt["v"])
                    if tt == 0 else ())
                mml = mm
            C.pb_reg[bi].end_write(mml)
            regA.add_reader(mml)
            regB.add_reader(mml)
            ev = C.dve(lambda v, bi=bi, dkc=dkc, dvh=dvh: v.tensor_copy(
                ar(C, AR_C, dkc * 1024 + dvh * 512, 512), C.pb[bi][:, :]),
                deps=[mml] + (regC_w if dkc == 0 and dvh == 0 else []))
            C.pb_reg[bi].add_reader(ev)
            kv_toks.append(ev)
    regC.end_write(kv_toks)

    wrA2 = regA.begin_write()
    q_toks = []
    for g in range(2):
        tev = []
        for j in range(4):
            evs = load_wT_rowblock(C, W["W_qkv"], g * 4 + j, D,
                                   j * 1024, 4096)
            tev += evs
        slab = wb(C, 0, 4096).rearrange("p (eb x) -> p eb x", x=1024)
        sregs = wb_slots(C, 0, 4096)
        for tt in range(NTT):
            bi = tt % 2
            bankd = C.pb_reg[bi].begin_write()
            mml = None
            for dc in range(NDC):
                mm = C.pe(lambda t, bi=bi, dc=dc, tt=tt, slab=slab: t.matmul(
                    C.pb[bi][:, :],
                    xnT_tile(C, dc, tt * 128, 128),
                    slab[:, :, dc * 128:(dc + 1) * 128],
                    start=(dc == 0), stop=(dc == 7)),
                    deps=(tev + list(bankd)) if dc == 0 else [tev[-1]])
                mml = mm
            C.pb_reg[bi].end_write(mml)
            regs_add_reader(sregs, mml)
            ev = C.dve(lambda v, bi=bi, tt=tt, g=g: v.tensor_scalar(
                ar(C, AR_A, tt * 1024 + g * 512, 512),
                C.pb[bi][:, :], 0.0, 1e-6, ALU.max, ALU.add),
                deps=[mml] + (wrA2 if tt == 0 and g == 0 else []))
            C.pb_reg[bi].add_reader(ev)
            q_toks.append(ev)
    regA.end_write(q_toks)

    kb_deps = C.scr_reg.read_deps()
    dummy = wb(C, 4096, 1024)
    dregs = wb_slots(C, 4096, 1024)
    dum_wd = regs_begin_write(dregs)
    qz_toks = []
    zc_all = smcol(C, 16)
    for tt in range(NTT):
        zc = zc_all[:, tt:tt + 1]
        t_z = C.dve(lambda v, tt=tt, zc=zc: v.scalar_tensor_tensor(
            dummy, ar(C, AR_A, tt * 1024, 1024), 1.0, ksumB,
            ALU.mult, ALU.mult, accum_out=zc),
            deps=q_toks + kb_deps + (dum_wd if tt == 0 else []))
        t_zr = C.dve(lambda v, zc=zc: v.tensor_scalar_add(zc, zc, 1e-6),
                     deps=[t_z])
        t_zi = C.dve(lambda v, zc=zc: v.reciprocal(zc, zc), deps=[t_zr])
        t_qz = C.dve(lambda v, tt=tt, zc=zc: v.tensor_single_scalar(
            ar(C, AR_A, tt * 1024, 1024), ar(C, AR_A, tt * 1024, 1024),
            zc, ALU.mult), deps=[t_zi])
        qz_toks.append(t_qz)
    regs_end_write(dregs, qz_toks)
    regA.end_write(qz_toks)
    C.scr_reg.add_reader(qz_toks)

    wrB2 = regB.begin_write()
    attn_toks = []
    for tsl in range(8):
        regD_w = regD.begin_write()
        tev = []
        for dh in range(2):
            ev = tb_transpose_batch(
                C,
                [(lambda it=it, dc=dc, tsl=tsl: ar(
                    C, AR_A, (tsl * 2 + it) * 1024 + dc * 128, 128))
                 for dc in range(dh * 4, dh * 4 + 4) for it in range(2)],
                lambda nb, dh=dh: ar(C, AR_D, dh * 1024, 1024),
                deps=qz_toks + (regD_w if dh == 0 else []), tbi=dh)
            tev.append(ev)
        regD.end_write(tev)
        regA.add_reader(tev)
        for dvc in range(8):
            bi = dvc % 2
            bankd = C.pb_reg[bi].begin_write()
            mml = None
            for dkc in range(8):
                mm = C.pe(lambda t, bi=bi, dvc=dvc, dkc=dkc: t.matmul(
                    C.pb[bi][:, 0:256],
                    ar(C, AR_C, dkc * 1024 + dvc * 128, 128),
                    ar(C, AR_D, dkc * 256, 256),
                    start=(dkc == 0), stop=(dkc == 7)),
                    deps=(list(bankd) + tev + kv_toks)
                    if dkc == 0 else [tev[-1]])
                mml = mm
            C.pb_reg[bi].end_write(mml)
            regD.add_reader(mml)
            regC.add_reader(mml)
            ev = C.dve(lambda v, bi=bi, dvc=dvc, tsl=tsl: v.tensor_copy(
                ar(C, AR_B, dvc * T + tsl * 256, 256), C.pb[bi][:, 0:256]),
                deps=[mml] + (wrB2 if tsl == 0 and dvc == 0 else []))
            C.pb_reg[bi].add_reader(ev)
            attn_toks.append(ev)
    regB.end_write(attn_toks)

    bo_row = rowb_alloc(C, 1024)
    t_bo = load_cast(C, bo_row, W["b_out"].unsqueeze(0))
    add_toks = []
    for fh in range(2):
        tev = []
        for j in range(4):
            evs = load_wT_rowblock(C, W["W_out"], fh * 4 + j, D,
                                   j * 1024, 4096)
            tev += evs
        slab = wb(C, 0, 4096).rearrange("p (fb x) -> p fb x", x=1024)
        sregs = wb_slots(C, 0, 4096)
        for tt in range(NTT):
            bi = tt % 2
            bankd = C.pb_reg[bi].begin_write()
            mml = None
            for ec in range(8):
                mm = C.pe(lambda t, bi=bi, ec=ec, tt=tt, slab=slab: t.matmul(
                    C.pb[bi][:, :],
                    ar(C, AR_B, ec * T + tt * 128, 128),
                    slab[:, :, ec * 128:(ec + 1) * 128],
                    start=(ec == 0), stop=False),
                    deps=(tev + list(bankd) + attn_toks)
                    if ec == 0 else [tev[-1]])
                mml = mm
            mmb = C.pe(lambda t, bi=bi, fh=fh: t.matmul(
                C.pb[bi][:, :], C.onesrow[:],
                bo_row[:, fh * 512:(fh + 1) * 512],
                start=False, stop=True), deps=[t_bo, C.t_onesr])
            C.pb_reg[bi].end_write(mmb)
            regs_add_reader(sregs, mmb)
            regB.add_reader(mmb)
            xd = C.xres_reg[tt].begin_write()
            sl = slice(tt * D + fh * 512, tt * D + (fh + 1) * 512)
            ev = C.dve(lambda v, bi=bi, sl=sl: v.tensor_add(
                C.x_res[:, sl], C.x_res[:, sl], C.pb[bi][:, :]),
                deps=[mmb] + xd)
            C.pb_reg[bi].add_reader(ev)
            C.xres_reg[tt].end_write(ev)
            add_toks.append(ev)
    return add_toks


import os as _os


def emit_layer1(C, W):
    parts = _os.environ.get("MK_L1_PARTS", "fa")
    prep = emit_l1_prep(C, W) if "f" in parts else []
    emit_ln(C)
    if "f" in parts:
        emit_l1_ffn(C, W, prep)
    if "a" in parts:
        emit_l1_attn(C, W)
    layer_reset_pools(C)




def l0_declare(nc):
    W = {}
    W["phase"] = nc.dram_tensor("L0_attn_phase", [4, D], F32, kind="ExternalInput").ap()
    for n in ("Wq", "Wk", "Wv"):
        W[n] = nc.dram_tensor(f"L0_attn_{n}", [D, 4096], F32, kind="ExternalInput").ap()
    for n in ("bq", "bk", "bv"):
        W[n] = nc.dram_tensor(f"L0_attn_{n}", [4096], F32, kind="ExternalInput").ap()
    W["Wo"] = nc.dram_tensor("L0_attn_Wo", [4096, D], F32, kind="ExternalInput").ap()
    W["bo"] = nc.dram_tensor("L0_attn_bo", [D], F32, kind="ExternalInput").ap()
    W["wi"] = nc.dram_tensor("L0_ffn_wi", [4096, D], F32, kind="ExternalInput").ap()
    W["bi"] = nc.dram_tensor("L0_ffn_bi", [4096], F32, kind="ExternalInput").ap()
    W["wo2"] = nc.dram_tensor("L0_ffn_wo", [D, 4096], F32, kind="ExternalInput").ap()
    W["bo2"] = nc.dram_tensor("L0_ffn_bo", [D], F32, kind="ExternalInput").ap()
    W["lora_A"] = nc.dram_tensor("L0_ffn_lora_A", [8, 16, D], F32, kind="ExternalInput").ap()
    W["lora_B"] = nc.dram_tensor("L0_ffn_lora_B", [8, 4096, 16], F32, kind="ExternalInput").ap()
    W["gate_w"] = nc.dram_tensor("L0_ffn_gate_w", [8, D], F32, kind="ExternalInput").ap()
    W["gate_b"] = nc.dram_tensor("L0_ffn_gate_b", [8], F32, kind="ExternalInput").ap()
    dbg = {}
    if _os.environ.get("MK_DBG"):
        dbg = dict(kind="ExternalOutput")
    W["wiT"] = nc.dram_tensor("l0_wiT", [D, 4096], BF16).ap()
    W["woT"] = nc.dram_tensor("l0_woT", [4096, D], BF16).ap()
    W["BT"] = nc.dram_tensor("l0_BT", [8, 16, 4096], BF16).ap()
    W["qscr"] = nc.dram_tensor("l0_qscr", [T, 4096], BF16, **dbg).ap()
    W["kscr"] = nc.dram_tensor("l0_kscr", [T, 4096], BF16, **dbg).ap()
    W["vscr"] = nc.dram_tensor("l0_vscr", [T, 4096], BF16, **dbg).ap()
    W["otscr"] = nc.dram_tensor("l0_otscr", [8, 32, 128, 256], BF16, **dbg).ap()
    W["rowscr"] = nc.dram_tensor("l0_rowscr", [16, 128], BF16).ap()
    return W


def emit_l0_prep(C, W):
    out_toks = []
    stg = wb(C, 0, 4096)
    sregs = wb_slots(C, 0, 4096)
    ost = wb(C, 4096, 1024)
    oregs = wb_slots(C, 4096, 1024)
    for (src, dst, R, K) in ((W["wi"], W["wiT"], 4096, D),
                             (W["wo2"], W["woT"], D, 4096)):
        dv = dst.rearrange("(kc p) r -> p kc r", p=128)
        for rb in range(R // 128):
            t_ld = load_cast(C, stg[:, 0:K], src[rb * 128:(rb + 1) * 128, 0:K],
                             deps=regs_begin_write(sregs))
            regs_end_write(sregs, t_ld)
            for k0 in range(0, K // 128, 8):
                nb = min(8, K // 128 - k0)
                ev = tb_transpose_batch(
                    C,
                    [(lambda kc=kc: stg[:, kc * 128:(kc + 1) * 128])
                     for kc in range(k0, k0 + nb)],
                    lambda nb2, k0=k0: ost[:, 0:nb2 * 128],
                    deps=[t_ld] + regs_begin_write(oregs), tbi=1)
                regs_end_write(oregs, ev)
                regs_add_reader(sregs, ev)
                t_o = C.dma_sync(lambda s, dv=dv, rb=rb, k0=k0, nb=nb:
                                 s.dma_start(
                                     dv[:, k0:k0 + nb,
                                        rb * 128:(rb + 1) * 128],
                                     ost[:, 0:nb * 128].rearrange(
                                         "p (kc r) -> p kc r", r=128)),
                                 deps=[ev])
                regs_add_reader(oregs, t_o)
                out_toks.append(t_o)
    for e in range(8):
        for b0 in range(0, 32, 8):
            t_ld = load_cast(C, stg[:, 0:128].rearrange(
                "p (b c) -> p b c", c=16),
                W["lora_B"][e].rearrange("(b p) c -> p b c", p=128)
                [:, b0:b0 + 8, :],
                deps=regs_begin_write(sregs))
            regs_end_write(sregs, t_ld)
            tbt = C.tb[1]
            reg = C.tb_reg[1]
            wdeps = reg.begin_write() + [t_ld, C.t_ident]
            mms = []
            for b in range(8):
                mm = C.pe(lambda t, b=b: t.transpose(
                    tbt[0:16, b * 128:(b + 1) * 128],
                    stg[:, b * 16:(b + 1) * 16], C.ident[:]),
                    deps=(wdeps if b == 0 else ()))
                mms.append(mm)
            reg.end_write(mms)
            ev = C.dve(lambda v: v.tensor_copy(ost[0:16, 0:1024],
                                               tbt[0:16, 0:1024]),
                       deps=[mms[-1]] + regs_begin_write(oregs))
            reg.add_reader(ev)
            regs_end_write(oregs, ev)
            regs_add_reader(sregs, ev)
            t_o = C.dma_sync(lambda s, e=e, b0=b0: s.dma_start(
                W["BT"][e, :, b0 * 128:(b0 + 8) * 128], ost[0:16, 0:1024]),
                deps=[ev])
            regs_add_reader(oregs, t_o)
            out_toks.append(t_o)
    return out_toks


def emit_l0_wpairs(C, W):
    PAIRS = [(h, g) for h in range(4) for g in range(h, 4)]
    regC, regD = C.ar_reg["C"], C.ar_reg["D"]
    wd = regC.begin_write() + regD.begin_write()
    toks = []
    rowstage = C.scr[0:1, 0:1024]
    scr_wd = C.scr_reg.begin_write()
    regA = C.ar_reg["A"]
    wdA = regA.begin_write()
    dB = ar(C, AR_A, 0, 2048).bitcast(F32)
    rB = ar(C, AR_A, 2048, 2048).bitcast(F32)
    M = float(1.5 * 2 ** 23)
    prev_row = scr_wd
    prev_ab = wdA
    rowstage2 = ar(C, AR_A, 4096, 2048).bitcast(F32)[0:1, :]
    for pi, (h, g) in enumerate(PAIRS):
        t_r1 = C.dma_sync(lambda s, h=h: s.dma_start(
            rowstage, W["phase"][h:h + 1, :]), deps=prev_row)
        t_r1b = C.dma_sync(lambda s, g=g: s.dma_start(
            rowstage2, W["phase"][g:g + 1, :]), deps=prev_row)
        t_r2 = C.dve(lambda v: v.tensor_sub(rowstage, rowstage, rowstage2),
                     deps=[t_r1, t_r1b])
        bts = []
        for half in range(2):
            pbi = 4 + half
            bd = C.pb_reg[pbi].begin_write()
            mm = C.pe(lambda t, half=half, pbi=pbi: t.matmul(
                C.pb[pbi][:, :], C.onesrowf[:],
                rowstage[:, half * 512:(half + 1) * 512],
                start=True, stop=True),
                deps=[t_r2, C.t_onesrf] + bd)
            C.pb_reg[pbi].end_write(mm)
            ev = C.dve(lambda v, half=half, pbi=pbi: v.tensor_scalar_add(
                dB[:, half * 512:(half + 1) * 512], C.pb[pbi][:, :],
                float(np.pi / 2)), deps=[mm] + prev_ab)
            C.pb_reg[pbi].add_reader(ev)
            bts.append(ev)
        prev_row = [t_r2] + bts
        t_u = C.dve(lambda v: v.tensor_scalar(
            rB, dB, float(1.0 / (2 * np.pi)), M, ALU.mult, ALU.add),
            deps=bts)
        t_v = C.dve(lambda v: v.tensor_scalar_add(rB, rB, -M), deps=[t_u])
        t_rr = C.dve(lambda v: v.scalar_tensor_tensor(
            dB, rB, float(-2 * np.pi), dB, ALU.mult, ALU.add),
            deps=[t_v])
        if pi < 8:
            dst = ar(C, AR_C, pi * 1024, 1024)
        else:
            dst = ar(C, AR_D, (pi - 8) * 1024, 1024)
        t_w = C.act(lambda s, dst=dst: s.activation(dst, dB, AF.Sin),
                    deps=[t_rr] + (wd if pi == 0 else []))
        prev_ab = [t_w]
        toks.append(t_w)
    regC.end_write(toks)
    regD.end_write(toks)
    regA.end_write(toks)
    C.scr_reg.end_write(toks)

    dbg_dump(C, "l0wb", toks)

    def wB(h, g):
        hh, gg = (h, g) if h <= g else (g, h)
        pi = PAIRS.index((hh, gg))
        if pi < 8:
            return ar(C, AR_C, pi * 1024, 1024)
        return ar(C, AR_D, (pi - 8) * 1024, 1024)
    return wB, toks


def emit_l0_proj_spill(C, W, wname, bname, scrname):
    xn_deps = C.xnT_reg.read_deps()
    wt8 = wb(C, 0, 4096)
    wregs = wb_slots(C, 0, 4096)
    stg = [wb(C, 4096, 512), wb(C, 4608, 512)]
    stregs = [wb_slots(C, 4096, 512), wb_slots(C, 4608, 512)]
    brow = C.rowp[0:1, 0:512]
    brreg = C.rowp_reg
    toks = []
    dscr = W[scrname].rearrange("(tt p) f -> p tt f", p=128)
    for fg in range(8):
        t_ws = []
        wdeps = regs_begin_write(wregs)
        for dc in range(8):
            t_w = load_cast(C, wt8[:, dc * 512:(dc + 1) * 512],
                            W[wname][dc * 128:(dc + 1) * 128,
                                     fg * 512:(fg + 1) * 512],
                            deps=wdeps if dc == 0 else [])
            t_ws.append(t_w)
        regs_end_write(wregs, t_ws)
        t_br = load_cast(C, brow,
                         W[bname].unsqueeze(0)[:, fg * 512:(fg + 1) * 512],
                         deps=brreg.begin_write())
        brreg.end_write(t_br)
        for tt in range(NTT):
            bi = tt % 4
            bankd = C.pb_reg[bi].begin_write()
            mml = None
            for dc in range(8):
                mm = C.pe(lambda t, bi=bi, dc=dc, tt=tt: t.matmul(
                    C.pb[bi][:, :], xnT_tile(C, dc, tt * 128, 128),
                    wt8[:, dc * 512:(dc + 1) * 512],
                    start=(dc == 0), stop=False),
                    deps=([t_ws[-1]] + list(bankd) +
                          (xn_deps if tt == 0 and dc == 0 else []))
                    if dc == 0 else [t_ws[-1]])
                mml = mm
            mmb = C.pe(lambda t, bi=bi: t.matmul(
                C.pb[bi][:, :], C.onesrow[:], brow, start=False, stop=True),
                deps=[t_br, C.t_onesr])
            brreg.add_reader(mmb)
            regs_add_reader(wregs, mmb)
            C.pb_reg[bi].end_write(mmb)
            st = stg[tt % 2]
            ev = C.dve(lambda v, bi=bi, st=st: v.tensor_copy(
                st, C.pb[bi][:, :]),
                deps=[mmb] + regs_begin_write(stregs[tt % 2]))
            C.pb_reg[bi].add_reader(ev)
            t_o = C.dma_sync(lambda s, st=st, tt=tt, fg=fg: s.dma_start(
                dscr[:, tt, fg * 512:(fg + 1) * 512], st), deps=[ev])
            regs_end_write(stregs[tt % 2], [ev, t_o])
            toks.append(t_o)
    return toks


def emit_l0_attn_mix(C, W, wBfn, wB_toks, qkv_toks):
    regA, regB = C.ar_reg["A"], C.ar_reg["B"]
    prod = wb(C, 0, 1024)
    dum = wb(C, 1024, 1024)
    pregs = wb_slots(C, 0, 2048)
    ost = wb(C, 4096, 1024)
    oregs = wb_slots(C, 4096, 1024)
    prod2 = wb(C, 2048, 1024)
    qv = W["qscr"].rearrange("(tt p) f -> p tt f", p=128)
    kv_ = W["kscr"].rearrange("(tt p) f -> p tt f", p=128)
    vv = W["vscr"].rearrange("(tt p) f -> p tt f", p=128)
    prev_mix = []

    attn_cols = smcol(C, 32)
    sm_t = smcol(C, 8)
    ot_toks_all = []
    for sp in range(8):
        wrA = regA.begin_write()
        wrB = regB.begin_write()
        t_q = C.dma_sync(lambda s, sp=sp: s.dma_start(
            ar(C, AR_A, 0, 8192).rearrange("p (tt f) -> p tt f", f=4096),
            qv[:, sp * 2:(sp + 1) * 2, :]), deps=wrA + qkv_toks)
        t_k = C.dma_sync(lambda s, sp=sp: s.dma_start(
            ar(C, AR_B, 0, 8192).rearrange("p (tt f) -> p tt f", f=4096),
            kv_[:, sp * 2:(sp + 1) * 2, :]), deps=wrB + qkv_toks)
        regA.end_write(t_q)
        regB.end_write(t_k)
        sc_toks = []
        pd = regs_begin_write(pregs)
        for it in range(2):
            for h in range(4):
                for g in range(4):
                    ci = it * 16 + h * 4 + g
                    pbuf = prod if ci % 2 == 0 else prod2
                    war = sc_toks[-2:-1] if len(sc_toks) >= 2 else []
                    t_p = C.pool(lambda gg, it=it, h=h, g=g, pbuf=pbuf:
                                 gg.tensor_mul(
                                     pbuf,
                                     ar(C, AR_A, it * 4096 + h * 1024, 1024),
                                     ar(C, AR_B, it * 4096 + g * 1024, 1024)),
                                 deps=[t_q, t_k] + war +
                                 (pd if it == 0 and h == 0 and g == 0
                                  else []))
                    t_s = C.dve(lambda v, ci=ci, h=h, g=g, pbuf=pbuf:
                                v.scalar_tensor_tensor(
                                    dum, pbuf, 1.0, wBfn(h, g), ALU.mult,
                                    ALU.mult,
                                    accum_out=attn_cols[:, ci:ci + 1]),
                                deps=[t_p] + wB_toks + prev_mix)
                    sc_toks.append(t_s)
        regs_end_write(pregs, sc_toks)
        t_dmp = dbg_dump(C, "l0sc0", sc_toks) if sp == 0 else []
        at_toks = []
        for it in range(2):
            blk = attn_cols[:, it * 16:(it + 1) * 16]
            t_e = C.act(lambda s, blk=blk: s.activation(
                blk, blk, AF.Exp, scale=float(1.0 / 32.0)),
                deps=sc_toks + t_dmp)
            t_r = C.dve(lambda v, blk=blk, it=it: v.tensor_reduce(
                sm_t[:, it * 4:(it + 1) * 4],
                blk.rearrange("p (h g) -> p h g", g=4), AX.X, ALU.add),
                deps=[t_e])
            t_i = C.dve(lambda v, it=it: v.reciprocal(
                sm_t[:, it * 4:(it + 1) * 4], sm_t[:, it * 4:(it + 1) * 4]),
                deps=[t_r])
            t_n = C.dve(lambda v, blk=blk, it=it: v.tensor_mul(
                blk.rearrange("p (h g) -> p h g", g=4),
                blk.rearrange("p (h g) -> p h g", g=4),
                sm_t[:, it * 4:(it + 1) * 4].unsqueeze(2).to_broadcast(
                    (128, 4, 4))), deps=[t_i])
            at_toks.append(t_n)
        if sp == 0:
            dbg_dump(C, "l0sm0", at_toks + sc_toks)
        wrA2 = regA.begin_write()
        t_v = C.dma_sync(lambda s, sp=sp: s.dma_start(
            ar(C, AR_A, 8192, 8192).rearrange("p (tt f) -> p tt f", f=4096),
            vv[:, sp * 2:(sp + 1) * 2, :]), deps=wrA2 + qkv_toks)
        regA.end_write(t_v)
        wrB2 = regB.begin_write()
        t_z = C.pool(lambda g: g.memset(ar(C, AR_B, 8192, 8192), 0.0),
                     deps=wrB2)
        mix_toks = [t_z]
        for it in range(2):
            for h in range(4):
                for g in range(4):
                    ci = it * 16 + h * 4 + g
                    t_m = C.dve(lambda v, it=it, h=h, g=g, ci=ci:
                                v.scalar_tensor_tensor(
                                    ar(C, AR_B, 8192 + it * 4096 + h * 1024, 1024),
                                    ar(C, AR_A, 8192 + it * 4096 + g * 1024, 1024),
                                    attn_cols[:, ci:ci + 1],
                                    ar(C, AR_B, 8192 + it * 4096 + h * 1024, 1024),
                                    ALU.mult, ALU.add),
                                deps=[t_v, at_toks[it]] + mix_toks[-1:])
                    mix_toks.append(t_m)
        regB.end_write(mix_toks)
        prev_mix = mix_toks[-1:]
        if sp == 0:
            dbg_dump(C, "l0mix0", mix_toks)
        ot_evs = []
        first = True
        for it in range(2):
            for h in range(4):
                extra = (regB.r + regB.w) if first else []
                first = False
                tbt = C.tb[0]
                treg = C.tb_reg[0]
                wdeps = treg.begin_write() + mix_toks[-1:] + extra + [C.t_ident]
                mms = []
                for dc in range(8):
                    mm = C.pe(lambda t, it=it, h=h, dc=dc: t.transpose(
                        tbt[:, dc * 128:(dc + 1) * 128],
                        C.arena[:, AR_B + 8192 + it * 4096 + h * 1024
                                + dc * 128:
                                AR_B + 8192 + it * 4096 + h * 1024
                                + dc * 128 + 128],
                        C.ident[:]), deps=(wdeps if dc == 0 else []))
                    mms.append(mm)
                treg.end_write(mms)
                ev = C.dve(lambda v, it=it, h=h: v.tensor_copy(
                    _l0_otdst(C, it, h),
                    tbt[:, 0:1024].rearrange("p (dc k jj) -> p dc k jj",
                                             k=32, jj=4)),
                    deps=[mms[-1]])
                treg.add_reader(ev)
                ot_evs.append(ev)
        t_sp = C.dma_sync(lambda s, sp=sp: s.dma_start(
            W["otscr"][sp].transpose([1, 0, 2]),
            ar(C, AR_B, 0, 8192).rearrange(
                "p (cc i) -> p cc i", i=256)), deps=ot_evs)
        ot_toks_all.append(t_sp)
        regB.w = list(ot_evs)
        regB.r = [t_sp]
    return ot_toks_all


def _l0_otdst(C, it, h):
    full = C.arena[:, AR_B:AR_B + 8192].rearrange(
        "p (jj dc i) -> p jj dc i", jj=4, dc=8)
    v = full[:, :, :, h * 64 + it * 32: h * 64 + it * 32 + 32]
    return v.transpose([0, 2, 3, 1])


def emit_l0_attn_out(C, W, ot_toks):
    otile = [wb(C, 0, 768), wb(C, 768, 768)]
    otregs = [Region("l0ot0"), Region("l0ot1")]
    wtile = [wb(C, 2048, 512), wb(C, 2560, 512)]
    wtregs = [wb_slots(C, 2048, 512), wb_slots(C, 2560, 512)]
    brow = C.rowp[0:1, 512:1536]
    t_bo = load_cast(C, brow, W["bo"].unsqueeze(0),
                     deps=C.rowp_reg.begin_write())
    C.rowp_reg.end_write(t_bo)
    otv = W["otscr"].rearrange("sp cc p (hh i) -> p hh sp cc i", i=128)
    add_toks = []
    groups = [(0, 6), (6, 2), (8, 6), (14, 2)]
    for fh in range(2):
        for g0, gn in groups:
            hh = g0 // 8
            sp0 = g0 % 8
            mml = [None] * gn
            t_prev = None
            for cc in range(32):
                ot = otile[cc % 2]
                t_ot = C.dma_sync(lambda s, cc=cc, hh=hh, sp0=sp0, gn=gn,
                                  ot=ot: s.dma_start(
                                      ot[:, 0:gn * 128].rearrange(
                                          "p (tt i) -> p tt i", i=128),
                                      otv[:, hh, sp0:sp0 + gn, cc, :]),
                                  deps=otregs[cc % 2].begin_write() +
                                  (ot_toks if cc < 2 else []))
                otregs[cc % 2].end_write(t_ot)
                wt = wtile[cc % 2]
                t_w = load_cast(C, wt, W["Wo"][cc * 128:(cc + 1) * 128,
                                               fh * 512:(fh + 1) * 512],
                                deps=regs_begin_write(wtregs[cc % 2]))
                regs_end_write(wtregs[cc % 2], t_w)
                mms = []
                for ii in range(gn):
                    bankd = C.pb_reg[ii].begin_write() if cc == 0 else ()
                    mm = C.pe(lambda t, ot=ot, wt=wt, ii=ii, cc=cc: t.matmul(
                        C.pb[ii][:, :], ot[:, ii * 128:(ii + 1) * 128],
                        wt, start=(cc == 0), stop=False),
                        deps=[t_ot, t_w] + list(bankd))
                    regs_add_reader(wtregs[cc % 2], mm)
                    mms.append(mm)
                    mml[ii] = mm
                otregs[cc % 2].add_reader(mms[-1])
            for ii in range(gn):
                mmb = C.pe(lambda t, ii=ii, fh=fh: t.matmul(
                    C.pb[ii][:, :], C.onesrow[:],
                    brow[:, fh * 512:(fh + 1) * 512],
                    start=False, stop=True), deps=[t_bo, C.t_onesr])
                C.pb_reg[ii].end_write(mmb)
                tt2 = g0 + ii
                sp, hh2 = tt2 % 8, tt2 // 8
                evs = []
                for s2 in range(2):
                    h = hh2 * 2 + s2
                    row = h * 512 + sp * 64
                    tt = row // 128
                    off = row % 128
                    xd = C.xres_reg[tt].begin_write()
                    sl = slice(tt * D + fh * 512, tt * D + (fh + 1) * 512)
                    ev = C.dve(lambda v, sl=sl, off=off, s2=s2, ii=ii:
                               v.tensor_add(
                                   C.x_res[off:off + 64, sl],
                                   C.x_res[off:off + 64, sl],
                                   C.pb[ii][s2 * 64:(s2 + 1) * 64, :]),
                               deps=[mmb] + xd)
                    C.xres_reg[tt].end_write(ev)
                    evs.append(ev)
                C.pb_reg[ii].add_reader(evs)
                add_toks += evs
    return add_toks



def emit_l0_gating(C, W):
    xn_deps = C.xnT_reg.read_deps()
    regC, regD = C.ar_reg["C"], C.ar_reg["D"]
    wrC = regC.begin_write()
    wrD = regD.begin_write()
    stg = wb(C, 0, 1024)
    sregs = wb_slots(C, 0, 1024)
    t_a = load_cast(C, stg, W["lora_A"].rearrange("e r d -> (e r) d"),
                    deps=regs_begin_write(sregs))
    regs_end_write(sregs, t_a)
    slab_evs = []
    dslab = ar(C, AR_D, 0, 1088).rearrange("p (dc x) -> p dc x", x=136)
    for k0 in range(8):
        ev = tb_transpose_batch(
            C, [(lambda dc=dc: stg[:, dc * 128:(dc + 1) * 128])
                for dc in range(8)][k0:k0 + 1] if False else
            [(lambda k0=k0: stg[:, k0 * 128:(k0 + 1) * 128])],
            lambda nb, k0=k0: dslab[:, k0, 0:128],
            deps=[t_a] + (wrD if k0 == 0 else []), tbi=1)
        slab_evs.append(ev)
    regs_add_reader(sregs, slab_evs)
    t_g = load_cast(C, stg[0:8, 0:1024], W["gate_w"],
                    deps=regs_begin_write(sregs) + slab_evs)
    regs_end_write(sregs, t_g)
    for dc in range(8):
        ev = tb_transpose_batch(
            C, [(lambda dc=dc: stg[0:8, dc * 128:(dc + 1) * 128])],
            lambda nb, dc=dc: dslab[:, dc, 128:136],
            deps=[t_g], tbi=1, src_rows=8)
        slab_evs.append(ev)
    regs_add_reader(sregs, slab_evs[-8:])
    regD.end_write(slab_evs)
    gbrow = C.rowp[0:1, 3584:3592]
    t_gb = load_cast(C, gbrow, W["gate_b"].unsqueeze(0),
                     deps=C.rowp_reg.begin_write())
    C.rowp_reg.end_write(t_gb)
    C.l0_gates = smcol(C, 128)
    C.l0_val2 = smcol(C, 16)
    tmp = smcol(C, 4)
    m1c, m2c, s1c, s2c = (tmp[:, i:i + 1] for i in range(4))
    zwr = wrC
    z_toks = []
    g_toks = []
    for tt in range(NTT):
        bi = tt % 2
        bankd = C.pb_reg[bi].begin_write()
        mml = None
        for dc in range(8):
            mm = C.pe(lambda t, bi=bi, dc=dc, tt=tt: t.matmul(
                C.pb[bi][:, 0:136], xnT_tile(C, dc, tt * 128, 128),
                dslab[:, dc, :], start=(dc == 0), stop=False),
                deps=(slab_evs + list(bankd) + xn_deps) if dc == 0
                else [slab_evs[-1]])
            mml = mm
        mmb = C.pe(lambda t, bi=bi: t.matmul(
            C.pb[bi][:, 128:136], C.onesrow[:], gbrow,
            start=False, stop=True), deps=[t_gb, C.t_onesr])
        C.pb_reg[bi].end_write(mmb)
        regD.add_reader(mmb)
        t_z = C.dve(lambda v, bi=bi, tt=tt: v.tensor_copy(
            ar(C, AR_C, tt * 128, 128), C.pb[bi][:, 0:128]),
            deps=[mmb] + (zwr if tt == 0 else []))
        z_toks.append(t_z)
        lg = C.pb[bi][:, 128:136]
        gcols = C.l0_gates[:, tt * 8:(tt + 1) * 8]
        t_m = C.dve(lambda v, lg=lg: v.reduce_max(m1c, lg, AX.X),
                    deps=[mmb])
        t_sub = C.dve(lambda v, lg=lg, gcols=gcols: v.tensor_scalar(
            gcols, lg, m1c, None, ALU.subtract), deps=[t_m])
        t_e2 = C.act(lambda s, gcols=gcols: s.activation(
            gcols, gcols, AF.Exp), deps=[t_sub])
        t_s = C.dve(lambda v, gcols=gcols: v.reduce_sum(s1c, gcols, AX.X),
                    deps=[t_e2])
        t_i = C.dve(lambda v: v.reciprocal(s1c, s1c), deps=[t_s])
        t_p = C.dve(lambda v, gcols=gcols: v.tensor_scalar(
            gcols, gcols, s1c, None, ALU.mult), deps=[t_i])
        C.pb_reg[bi].add_reader([t_z, t_sub])
        t_m1 = C.dve(lambda v, gcols=gcols: v.reduce_max(m1c, gcols, AX.X),
                     deps=[t_p])
        t_k = C.dve(lambda v, gcols=gcols: v.tensor_scalar(
            C.scr[:, 0:8], gcols, m1c, -1e30, ALU.is_ge, ALU.mult),
            deps=[t_m1] + C.scr_reg.begin_write())
        t_k2 = C.dve(lambda v, gcols=gcols: v.tensor_add(
            C.scr[:, 0:8], C.scr[:, 0:8], gcols), deps=[t_k])
        t_m2 = C.dve(lambda v: v.reduce_max(m2c, C.scr[:, 0:8], AX.X),
                     deps=[t_k2])
        C.scr_reg.end_write([t_k, t_k2])
        C.scr_reg.add_reader(t_m2)
        t_v2 = C.dve(lambda v, tt=tt: v.tensor_tensor(
            C.l0_val2[:, tt:tt + 1], m1c, m2c, ALU.add), deps=[t_m2, t_m1])
        t_gt = C.dve(lambda v, gcols=gcols, tt=tt: v.tensor_scalar(
            gcols, gcols, m2c, C.l0_val2[:, tt:tt + 1], ALU.is_ge, ALU.mult),
            deps=[t_v2])
        g_toks.append(t_gt)
    regC.end_write(z_toks)
    C.l0_gate_toks = g_toks
    C.l0_z_toks = z_toks
    dbg_dump(C, "l0gate", z_toks + g_toks)
    v2b = wb(C, 512, 16)
    t_c = C.dve(lambda v: v.tensor_copy(v2b, C.l0_val2),
                deps=g_toks + regs_begin_write(sregs))
    ev = tb_transpose_batch(
        C, [lambda: v2b], lambda nb: wb(C, 0, 128)[0:16, :],
        deps=[t_c], tbi=1, src_rows=128, src_cols=16)
    regs_end_write(sregs, [t_c, ev])
    t_o = C.dma_sync(lambda s: s.dma_start(
        W["rowscr"][:, :], wb(C, 0, 128)[0:16, :]), deps=[ev])
    v2row = C.rowp[0:1, 0:2048]
    t_v2r = C.dma_sync(lambda s: s.dma_start(
        v2row, W["rowscr"].rearrange("a b -> (a b)").unsqueeze(0)),
        deps=[t_o] + C.rowp_reg.begin_write())
    C.rowp_reg.end_write(t_v2r)
    C.l0_v2row = v2row
    C.l0_v2row_tok = t_v2r
    regs_add_reader(sregs, t_v2r)
    return z_toks


def emit_l0_ffn(C, W):
    xn_deps = C.xnT_reg.read_deps()
    regA, regB = C.ar_reg["A"], C.ar_reg["B"]
    wt8 = [wb(C, 0, 512), wb(C, 512, 512)]
    wtregs = [wb_slots(C, 0, 512), wb_slots(C, 512, 512)]
    bet = wb(C, 1024, 4096)
    betregs = wb_slots(C, 1024, 4096)
    ztp = ar(C, AR_D, 0, 1024)
    regD = C.ar_reg["D"]
    birow = C.rowp[0:1, 2048:2560]
    bireg = Region("l0bi")
    wiv = W["wiT"]
    wov = W["woT"]
    add_toks = []
    for sp in range(4):
        t0 = sp * 512
        wrA = regA.begin_write()
        base_toks = []
        for fg in range(8):
            t_bi = load_cast(C, birow,
                             W["bi"].unsqueeze(0)[:, fg * 512:(fg + 1) * 512],
                             deps=bireg.begin_write())
            bireg.end_write(t_bi)
            mml = [None] * 4
            for dc in range(8):
                wt = wt8[dc % 2]
                t_w = C.dma_sync(lambda s, wt=wt, dc=dc, fg=fg: s.dma_start(
                    wt, wiv[dc * 128:(dc + 1) * 128,
                            fg * 512:(fg + 1) * 512]),
                    deps=regs_begin_write(wtregs[dc % 2]))
                regs_end_write(wtregs[dc % 2], t_w)
                for tis in range(4):
                    bankd = C.pb_reg[tis].begin_write() if dc == 0 else ()
                    mm = C.pe(lambda t, wt=wt, tis=tis, dc=dc, t0=t0:
                              t.matmul(C.pb[tis][:, :],
                                       xnT_tile(C, dc, t0 + tis * 128, 128),
                                       wt, start=(dc == 0), stop=False),
                              deps=[t_w] + list(bankd) +
                              (xn_deps if dc == 0 and tis == 0 else []))
                    regs_add_reader(wtregs[dc % 2], mm)
                    mml[tis] = mm
            for tis in range(4):
                mmb = C.pe(lambda t, tis=tis: t.matmul(
                    C.pb[tis][:, :], C.onesrow[:], birow,
                    start=False, stop=True), deps=[t_bi, C.t_onesr])
                bireg.add_reader(mmb)
                C.pb_reg[tis].end_write(mmb)
                ev = C.dve(lambda v, tis=tis, fg=fg: v.tensor_copy(
                    C.arena[:, AR_A + tis * 4096 + fg * 512:
                            AR_A + tis * 4096 + fg * 512 + 512],
                    C.pb[tis][:, :]),
                    deps=[mmb] + (wrA if fg == 0 and tis == 0 else []))
                C.pb_reg[tis].add_reader(ev)
                base_toks.append(ev)
        regA.end_write(base_toks)
        if sp == 0:
            dbg_dump(C, "l0base0", base_toks)
        wrB = regB.begin_write()
        t_z0 = C.pool(lambda g: g.memset(ar(C, AR_B, 0, 16384), 0.0),
                      deps=wrB)
        acc_toks = [t_z0]
        for tis in range(4):
            tt = sp * 4 + tis
            tbt = C.tb[1]
            treg = C.tb_reg[1]
            wdeps = treg.begin_write() + C.l0_z_toks
            mms = []
            for e in range(8):
                mm = C.pe(lambda t, e=e, tt=tt: t.transpose(
                    tbt[0:16, e * 128:(e + 1) * 128],
                    ar(C, AR_C, tt * 128 + e * 16, 16), C.ident[:]),
                    deps=(wdeps if e == 0 else ()))
                mms.append(mm)
            treg.end_write(mms)
            t_zp = C.dve(lambda v: v.tensor_copy(ztp[0:16, :],
                                                 tbt[0:16, 0:1024]),
                         deps=[mms[-1]] + regD.begin_write())
            treg.add_reader(t_zp)
            regD.end_write(t_zp)
            for e in range(8):
                t_be = C.dma_sync(lambda s, e=e: s.dma_start(
                    bet[0:16, :], W["BT"][e]),
                    deps=regs_begin_write(betregs))
                regs_end_write(betregs, t_be)
                mmd = []
                for fs in range(8):
                    bankd = C.pb_reg[fs % 4].begin_write()
                    mm = C.pe(lambda t, fs=fs, e=e: t.matmul(
                        C.pb[fs % 4][:, :], ztp[0:16, e * 128:(e + 1) * 128],
                        bet[0:16, fs * 512:(fs + 1) * 512],
                        start=True, stop=True),
                        deps=[t_zp, t_be] + list(bankd))
                    regs_add_reader(betregs, mm)
                    regD.add_reader(mm)
                    C.pb_reg[fs % 4].end_write(mm)
                    mmd.append(mm)
                    sscr = C.scr[:, 0:512]
                    t_s = C.dve(lambda v, fs=fs, tis=tis: v.tensor_tensor(
                        sscr, C.pb[fs % 4][:, :],
                        C.arena[:, AR_A + tis * 4096 + fs * 512:
                                AR_A + tis * 4096 + fs * 512 + 512],
                        ALU.add),
                        deps=[mm, base_toks[-1]] + C.scr_reg.begin_write())
                    C.scr_reg.end_write(t_s)
                    C.pb_reg[fs % 4].add_reader(t_s)
                    t_u = C.act(lambda s, fs=fs: s.activation(
                        sscr, sscr, AF.Relu), deps=[t_s])
                    C.scr_reg.end_write([t_s, t_u])
                    t_ac = C.dve(lambda g, fs=fs, tis=tis, e=e, tt=tt:
                                  g.scalar_tensor_tensor(
                                      C.arena[:, AR_B + tis * 8192 + fs * 1024:
                                              AR_B + tis * 8192 + fs * 1024 + 1024]
                                      .bitcast(F32) if False else
                                      ar(C, AR_B, tis * 4096 + fs * 512, 512),
                                      sscr,
                                      C.l0_gates[:, tt * 8 + e:tt * 8 + e + 1],
                                      ar(C, AR_B, tis * 4096 + fs * 512, 512),
                                      ALU.mult, ALU.add),
                                  deps=[t_u, acc_toks[-1]] + C.l0_gate_toks[
                                      tt:tt + 1])
                    C.scr_reg.add_reader(t_ac)
                    acc_toks.append(t_ac)
        regB.end_write(acc_toks)
        if sp == 0:
            dbg_dump(C, "l0acc0", acc_toks)
        wrA2 = regA.begin_write()
        accT_evs = []
        for tis in range(4):
            for b0 in range(0, 32, 8):
                ev = tb_transpose_batch(
                    C,
                    [(lambda tis=tis, fc=fc: C.arena[
                        :, AR_B + tis * 4096 + fc * 128:
                        AR_B + tis * 4096 + fc * 128 + 128])
                     for fc in range(b0, b0 + 8)],
                    lambda nb, tis=tis, b0=b0: C.arena[
                        :, AR_A + (b0 // 8) * 4096 + tis * 1024:
                        AR_A + (b0 // 8) * 4096 + tis * 1024 + 1024],
                    deps=acc_toks[-1:] + (wrA2 if tis == 0 and b0 == 0
                                          else []), tbi=0)
                accT_evs.append(ev)
        regA.end_write(accT_evs)
        regB.add_reader(accT_evs)
        wo_t = [wb(C, 1024, 512), wb(C, 1536, 512)]
        woregs = [wb_slots(C, 1024, 512), wb_slots(C, 1536, 512)]
        for fh in range(2):
            mml2 = [None] * 4
            for fc in range(32):
                wt = wo_t[fc % 2]
                t_w = C.dma_sync(lambda s, wt=wt, fc=fc, fh=fh: s.dma_start(
                    wt, wov[fc * 128:(fc + 1) * 128,
                            fh * 512:(fh + 1) * 512]),
                    deps=regs_begin_write(woregs[fc % 2]))
                regs_end_write(woregs[fc % 2], t_w)
                for tis in range(4):
                    bankd = C.pb_reg[tis].begin_write() if fc == 0 else ()
                    mm = C.pe(lambda t, wt=wt, tis=tis, fc=fc: t.matmul(
                        C.pb[tis][:, :],
                        C.arena[:, AR_A + (fc // 8) * 4096 + tis * 1024 +
                                (fc % 8) * 128:
                                AR_A + (fc // 8) * 4096 + tis * 1024 +
                                (fc % 8) * 128 + 128],
                        wt, start=(fc == 0), stop=False),
                        deps=[t_w] + list(bankd) +
                        (accT_evs if fc == 0 and tis == 0 else []))
                    regs_add_reader(woregs[fc % 2], mm)
                    mml2[tis] = mm
            for tis in range(4):
                tt = sp * 4 + tis
                mmb = C.pe(lambda t, tis=tis, tt=tt, fh=fh: t.matmul(
                    C.pb[tis][:, :], C.l0_v2row[:, tt * 128:(tt + 1) * 128],
                    C.l0_borow2[:, fh * 512:(fh + 1) * 512],
                    start=False, stop=True),
                    deps=[C.l0_v2row_tok, C.l0_borow2_tok])
                C.pb_reg[tis].end_write(mmb)
                regA.add_reader(mmb)
                xd = C.xres_reg[tt].begin_write()
                sl = slice(tt * D + fh * 512, tt * D + (fh + 1) * 512)
                ev = C.dve(lambda v, tis=tis, sl=sl: v.tensor_add(
                    C.x_res[:, sl], C.x_res[:, sl], C.pb[tis][:, :]),
                    deps=[mmb] + xd)
                C.pb_reg[tis].add_reader(ev)
                C.xres_reg[tt].end_write(ev)
                add_toks.append(ev)
    return add_toks


def emit_layer0(C, W):
    parts = _os.environ.get("MK_L0_PARTS", "af")
    prep = emit_l0_prep(C, W) if "f" in parts else []
    emit_ln(C)
    if "a" in parts:
        wBfn, wB_toks = emit_l0_wpairs(C, W)
        qkv_toks = []
        for wname, bname, scrname in (("Wq", "bq", "qscr"),
                                      ("Wk", "bk", "kscr"),
                                      ("Wv", "bv", "vscr")):
            qkv_toks += emit_l0_proj_spill(C, W, wname, bname, scrname)
        ot_toks = emit_l0_attn_mix(C, W, wBfn, wB_toks, qkv_toks)
        emit_l0_attn_out(C, W, ot_toks)
    if "f" not in parts:
        layer_reset_pools(C)
        return
    emit_l0_gating(C, W)
    C.l0_borow2 = C.rowp[0:1, 2560:3584]
    t_b = load_cast(C, C.l0_borow2, W["bo2"].unsqueeze(0))
    C.l0_borow2_tok = C.dve(lambda v: v.tensor_single_scalar(
        C.l0_borow2, C.l0_borow2, 2.0, ALU.mult), deps=[t_b])
    emit_l0_ffn(C, W)
    layer_reset_pools(C)



def l2_declare(nc):
    W = {}
    for n in ("Wq", "Wk", "Wv"):
        W[n] = nc.dram_tensor(f"L2_attn_{n}", [16, D, 64], F32,
                              kind="ExternalInput").ap()
    W["Wp"] = nc.dram_tensor("L2_attn_Wp", [D, D], F32, kind="ExternalInput").ap()
    W["bp"] = nc.dram_tensor("L2_attn_bp", [D], F32, kind="ExternalInput").ap()
    W["rel"] = nc.dram_tensor("L2_attn_rel_table", [33, 16], F32,
                              kind="ExternalInput").ap()
    W["w_gate"] = nc.dram_tensor("L2_ffn_w_gate", [2816, D], F32,
                                 kind="ExternalInput").ap()
    W["w_up"] = nc.dram_tensor("L2_ffn_w_up", [2816, D], F32,
                               kind="ExternalInput").ap()
    W["w_down"] = nc.dram_tensor("L2_ffn_w_down", [D, 2816], F32,
                                 kind="ExternalInput").ap()
    dbg = {}
    if _os.environ.get("MK_DBG"):
        dbg = dict(kind="ExternalOutput")
    W["stripscr"] = nc.dram_tensor("l2_strip", [16, 1168], BF16, **dbg).ap()
    W["oscr"] = nc.dram_tensor("l2_oscr", [16, 64, T], BF16, **dbg).ap()
    W["wgqT"] = nc.dram_tensor("l2_wgqT", [D, 2816], BF16, **dbg).ap()
    W["wuqT"] = nc.dram_tensor("l2_wuqT", [D, 2816], BF16, **dbg).ap()
    W["wdqT"] = nc.dram_tensor("l2_wdqT", [2816, D], BF16, **dbg).ap()
    W["xqscr"] = nc.dram_tensor("l2_xqscr", [T, D], BF16, **dbg).ap()
    return W


def emit_quant_fp4(C, q, dst, nblk, deps, tmp_off=2048, qf32=False):
    n = nblk * 16
    assert nblk <= 32
    bm = C.qcols[:, 0:nblk]
    sc = C.qcols[:, 32:32 + nblk]
    rs = C.qcols[:, 64:64 + nblk]
    scrd = C.qreg.begin_write()
    qv = q.rearrange("p (b k) -> p b k", k=16)
    af = C.wbuf[:, tmp_off:tmp_off + 2 * n].bitcast(F32)
    rf = C.wbuf[:, tmp_off + 2 * n:tmp_off + 4 * n].bitcast(F32)
    sw = C.wbuf[:, tmp_off + 4 * n:tmp_off + 5 * n]
    gf = C.wbuf[:, tmp_off + 5 * n:tmp_off + 6 * n]
    tregs = wb_slots(C, tmp_off, 6 * n)
    twd = regs_begin_write(tregs)
    t1 = C.dve(lambda v: v.tensor_reduce(bm, qv, AX.X, ALU.max,
                                         apply_absolute_value=True),
               deps=deps + scrd)
    t2 = C.dve(lambda v: v.tensor_scalar(sc, bm, 1.0 / 6.0, 1e-12,
                                         ALU.mult, ALU.max), deps=[t1])
    t3 = C.dve(lambda v: v.reciprocal(rs, sc), deps=[t2])
    if qf32:
        t_s0 = C.dve(lambda v: v.tensor_scalar(sw, q, 0.0, -2.0, ALU.is_lt,
                                               ALU.mult), deps=deps + twd)
        t_s = C.dve(lambda v: v.tensor_scalar_add(sw, sw, 1.0), deps=[t_s0])
    else:
        t_s = C.dve(lambda v: v.tensor_single_scalar(
            sw.bitcast(mybir.dt.uint16), q.bitcast(mybir.dt.uint16),
            32768, ALU.bitwise_and), deps=deps + twd)
    t4 = C.dve(lambda v: v.tensor_tensor(
        af.rearrange("p (b k) -> p b k", k=16), qv,
        rs.unsqueeze(2).to_broadcast((128, nblk, 16)), ALU.mult),
        deps=[t3] + deps + twd)
    t5 = C.act(lambda s: s.activation(af, af, AF.Abs), deps=[t4])
    M = float(1.5 * 2 ** 23)
    t6 = C.dve(lambda v: v.tensor_scalar(rf, af, 2.0, 2.0, ALU.min, ALU.mult),
               deps=[t5])
    t7 = C.dve(lambda v: v.tensor_scalar(rf, rf, M, M, ALU.add, ALU.subtract),
               deps=[t6])
    tg = C.dve(lambda v: v.tensor_scalar(gf, af, 5.0, 2.0, ALU.is_gt,
                                         ALU.mult), deps=[t5])
    t8 = C.dve(lambda v: v.tensor_scalar(af, af, 2.0, 4.5, ALU.max, ALU.min),
               deps=[t5, tg])
    t9 = C.dve(lambda v: v.tensor_scalar(af, af, M, M + 2.0, ALU.add,
                                         ALU.subtract), deps=[t8])
    t10 = C.dve(lambda v: v.scalar_tensor_tensor(dst, rf, 0.5, af,
                                                 ALU.mult, ALU.add),
                deps=[t7, t9])
    t12 = C.dve(lambda v: v.tensor_tensor(dst, dst, gf, ALU.add),
                deps=[t10, tg])
    if qf32:
        t13 = C.dve(lambda v: v.tensor_tensor(dst, dst, sw, ALU.mult),
                    deps=[t12, t_s])
    else:
        t13 = C.dve(lambda v: v.tensor_tensor(
            dst.bitcast(mybir.dt.uint16), dst.bitcast(mybir.dt.uint16),
            sw.bitcast(mybir.dt.uint16), ALU.bitwise_or), deps=[t12, t_s])
    t14 = C.dve(lambda v: v.tensor_tensor(
        dst.rearrange("p (b k) -> p b k", k=16),
        dst.rearrange("p (b k) -> p b k", k=16),
        sc.unsqueeze(2).to_broadcast((128, nblk, 16)), ALU.mult),
        deps=[t13, t2])
    regs_end_write(tregs, [t_s, t4, t5, t6, t7, tg, t8, t9])
    regs_add_reader(tregs, [t10, t12, t13, t14])
    C.qreg.end_write([t1, t2, t3])
    C.qreg.add_reader([t4, t14])
    return t14


def emit_l2_weight_prep(C, W):
    toks = []
    stgf = wb(C, 0, 1024).bitcast(F32)
    sregs = wb_slots(C, 0, 1024)
    qdst = wb(C, 1024, 512)
    qregs = wb_slots(C, 1024, 512)
    ost = wb(C, 4608, 512)
    oregs = wb_slots(C, 4608, 512)
    for (src, dst, R, K2) in ((W["w_gate"], W["wgqT"], 2816, D),
                              (W["w_up"], W["wuqT"], 2816, D),
                              (W["w_down"], W["wdqT"], D, 2816)):
        dv = dst.rearrange("(kc p) r -> p kc r", p=128)
        for rb in range(R // 128):
            for hw0 in range(0, K2, 512):
                hwn = min(512, K2 - hw0)
                t_ld = C.dma_sync(lambda s, src=src, rb=rb, hw0=hw0, hwn=hwn:
                                  s.dma_start(stgf[:, 0:hwn],
                                              src[rb * 128:(rb + 1) * 128,
                                                  hw0:hw0 + hwn]),
                                  deps=regs_begin_write(sregs))
                regs_end_write(sregs, t_ld)
                qwd = regs_begin_write(qregs)
                t_q = emit_quant_fp4(C, stgf[:, 0:hwn], qdst[:, 0:hwn],
                                     hwn // 16, [t_ld] + qwd,
                                     tmp_off=1536, qf32=True)
                regs_add_reader(sregs, t_q)
                regs_end_write(qregs, t_q)
                nb = hwn // 128
                kg = hw0 // 128
                ev = tb_transpose_batch(
                    C,
                    [(lambda kc=kc: qdst[:, kc * 128:(kc + 1) * 128])
                     for kc in range(nb)],
                    lambda nb2: ost[:, 0:nb2 * 128],
                    deps=[t_q] + regs_begin_write(oregs), tbi=1)
                regs_end_write(oregs, ev)
                regs_add_reader(qregs, ev)
                t_o = C.dma_sync(lambda s, dv=dv, rb=rb, kg=kg, nb=nb:
                                 s.dma_start(
                                     dv[:, kg:kg + nb,
                                        rb * 128:(rb + 1) * 128],
                                     ost[:, 0:nb * 128].rearrange(
                                         "p (kc r) -> p kc r", r=128)),
                                 deps=[ev])
                regs_add_reader(oregs, t_o)
                toks.append(t_o)
    return toks


def l2_make_xq_hook(C, W):
    toks = []
    xnf = wb(C, 0, 2048).bitcast(F32)
    xfregs = wb_slots(C, 0, 2048)
    xqs = wb(C, 3584, 1024)
    xsregs = wb_slots(C, 3584, 1024)
    xqv = W["xqscr"].rearrange("(tt p) d -> p tt d", p=128)

    def hook(tt, xt, m_c, r_c, deps):
        wd = regs_begin_write(xfregs)
        t_x = C.dve(lambda v: v.tensor_scalar(
            xnf, xt, m_c, r_c, ALU.subtract, ALU.mult), deps=deps + wd)
        regs_end_write(xfregs, t_x)
        prev = regs_begin_write(xsregs)
        qts = []
        for q0 in range(0, 1024, 256):
            t_q = emit_quant_fp4(
                C, xnf[:, q0:q0 + 256],
                xqs[:, q0:q0 + 256], 16,
                [t_x] + qts[-1:] + (prev if q0 == 0 else []),
                tmp_off=2048, qf32=True)
            qts.append(t_q)
        regs_add_reader(xfregs, qts)
        regs_end_write(xsregs, qts)
        t_o = C.dma_sync(lambda s, tt=tt: s.dma_start(xqv[:, tt, :], xqs),
                         deps=qts[-1:])
        regs_add_reader(xsregs, t_o)
        toks.append(t_o)
    return hook, toks


def emit_l2_attn(C, W):
    xn_deps = C.xnT_reg.read_deps()
    regA, regB, regC, regD = (C.ar_reg[k] for k in "ABCD")
    t33 = C.scr[0:33, 0:16]
    scrd = C.scr_reg.begin_write()
    t_tb = C.dma_sync(lambda s: s.dma_start(t33, W["rel"]), deps=scrd)
    C.scr_reg.end_write(t_tb)
    striprow = wb(C, 0, 1168)
    spregs = wb_slots(C, 0, 1168)
    strip_toks = []
    for h in range(16):
        wd = regs_begin_write(spregs)
        t_z = C.pool(lambda g: g.memset(striprow, 0.0), deps=wd)
        t_f1 = C.dve(lambda v, h=h: v.tensor_scalar(
            striprow[0:1, 0:640], striprow[0:1, 0:640], 0.0,
            t33[0:1, h:h + 1], ALU.mult, ALU.add), deps=[t_tb, t_z])
        t_f2 = C.dma_pool(lambda g, h=h: g.dma_start(
            striprow[0:1, 623:640],
            W["rel"][0:17, h:h + 1].transpose([1, 0]),
            allow_slow_non_contiguous=True), deps=[t_z])
        t_f3 = C.dve(lambda v: v.tensor_scalar(
            striprow[0:1, 640:1168], striprow[0:1, 640:1168], 0.0,
            -30000.0, ALU.mult, ALU.add), deps=[t_z])
        t_o = C.dma_sync(lambda s, h=h: s.dma_start(
            W["stripscr"][h].unsqueeze(0), striprow[0:1, :]),
            deps=[t_f1, t_f2, t_f3])
        regs_end_write(spregs, [t_f1, t_f2, t_f3])
        regs_add_reader(spregs, t_o)
        strip_toks.append(t_o)
    C.scr_reg.add_reader(strip_toks)
    t0row = rowb_alloc(C, 16)
    t_t0 = C.dve(lambda v: v.tensor_copy(t0row, t33[0:1, :]), deps=[t_tb])
    C.scr_reg.add_reader(t_t0)
    tbl0col = smcol(C, 16)
    t_t0b = bcast_col(C, lambda: t0row, 16, tbl0col, [t_t0])

    s_tile = ar(C, AR_A, 0, 4096).bitcast(F32)
    p_tile = ar(C, AR_A, 4096, 2048)
    pT_sup = lambda off, n: ar(C, AR_B, off, n)
    qT = ar(C, AR_B, 8192, 2048)
    kT = ar(C, AR_B, 10240, 2048)
    vsl = ar(C, AR_B, 12288, 2048)
    Mh = ar(C, AR_D, 0, 1024)
    mstage = ar(C, AR_D, 1024, 1024)

    stat_m = smcol(C, 16)
    stat_r = smcol(C, 16)
    l2qkv = Region("l2qkv")
    C.l2qkv = l2qkv
    prev_sp = []

    wpair = [wb(C, 0, 512), wb(C, 512, 512)]
    wpregs = [wb_slots(C, 0, 512), wb_slots(C, 512, 512)]
    ostg = [wb(C, 1024, 512), wb(C, 1536, 512)]
    ostregs = [wb_slots(C, 1024, 512), wb_slots(C, 1536, 512)]

    for pair in range(8):
        wrB = l2qkv.begin_write()
        pair_toks = {}
        for wname, dst, scaleq in (("Wq", qT, True), ("Wk", kT, False)):
            evs = []
            mml = [None] * 4
            t_ws = []
            for cc in range(8):
                wt = wpair[cc % 2]
                t_w = load_cast(
                    C, wt[:, 0:128].rearrange("p (h d) -> p h d", d=64),
                    W[wname][2 * pair:2 * pair + 2,
                             cc * 128:(cc + 1) * 128, :].transpose([1, 0, 2]),
                    deps=regs_begin_write(wpregs[cc % 2]))
                regs_end_write(wpregs[cc % 2], t_w)
                t_ws.append(t_w)
                for ts4 in range(4):
                    bankd = C.pb_reg[ts4].begin_write() if cc == 0 else ()
                    mm = C.pe(lambda t, wt=wt, ts4=ts4, cc=cc: t.matmul(
                        C.pb[ts4][:, :],
                        wt[:, 0:128],
                        xnT_tile(C, cc, ts4 * 512, 512),
                        start=(cc == 0), stop=(cc == 7)),
                        deps=[t_w] + list(bankd) +
                        (xn_deps if cc == 0 and ts4 == 0 else []))
                    regs_add_reader(wpregs[cc % 2], mm)
                    mml[ts4] = mm
            for ts4 in range(4):
                C.pb_reg[ts4].end_write(mml[ts4])
                if scaleq:
                    ev = C.dve(lambda v, ts4=ts4, dst=dst: v.tensor_single_scalar(
                        dst[:, ts4 * 512:(ts4 + 1) * 512], C.pb[ts4][:, :],
                        0.125, ALU.mult),
                        deps=[mml[ts4]] + (wrB if ts4 == 0 else []))
                else:
                    ev = C.dve(lambda v, ts4=ts4, dst=dst: v.tensor_copy(
                        dst[:, ts4 * 512:(ts4 + 1) * 512], C.pb[ts4][:, :]),
                        deps=[mml[ts4]])
                C.pb_reg[ts4].add_reader(ev)
                evs.append(ev)
            pair_toks[wname] = evs
        v_evs = []
        t_ws = []
        vw = wb(C, 2048, 1024)
        vwregs = wb_slots(C, 2048, 1024)
        vwd = regs_begin_write(vwregs)
        for cc in range(8):
            t_w = load_cast(
                C, vw[:, cc * 128:(cc + 1) * 128]
                .rearrange("p (h d) -> p h d", d=64),
                W["Wv"][2 * pair:2 * pair + 2,
                        cc * 128:(cc + 1) * 128, :].transpose([1, 0, 2]),
                deps=vwd if cc == 0 else [])
            t_ws.append(t_w)
        regs_end_write(vwregs, t_ws)
        for tt in range(NTT):
            bi = 4 + (tt % 2)
            bankd = C.pb_reg[bi].begin_write()
            mml2 = None
            for cc in range(8):
                mm = C.pe(lambda t, bi=bi, cc=cc, tt=tt: t.matmul(
                    C.pb[bi][:, 0:128],
                    xnT_tile(C, cc, tt * 128, 128),
                    vw[:, cc * 128:(cc + 1) * 128],
                    start=(cc == 0), stop=(cc == 7)),
                    deps=([t_ws[-1]] + list(bankd)) if cc == 0 else [t_ws[-1]])
                regs_add_reader(vwregs, mm)
                mml2 = mm
            C.pb_reg[bi].end_write(mml2)
            ev = C.dve(lambda v, bi=bi, tt=tt: v.tensor_copy(
                vsl[:, tt * 128:(tt + 1) * 128], C.pb[bi][:, 0:128]),
                deps=[mml2])
            C.pb_reg[bi].add_reader(ev)
            v_evs.append(ev)
        l2qkv.end_write(pair_toks["Wq"] + pair_toks["Wk"] + v_evs)
        if pair == 0:
            dbg_dump(C, "l2v", pair_toks["Wq"] + pair_toks["Wk"] + v_evs)

        for s2 in range(2):
            h = pair * 2 + s2
            wrD = regD.begin_write()
            t_mp = C.dma_sync(lambda s, h=h: s.dma_start(
                mstage, _ovl_ap(W["stripscr"], h)),
                deps=[strip_toks[h]] + wrD)
            mm_j = []
            for half in range(2):
                bi = half
                bankd = C.pb_reg[bi].begin_write()
                mm = C.pe(lambda t, half=half, bi=bi: t.matmul(
                    C.pb[bi][:, :],
                    C.identJ[:],
                    mstage[:, half * 512:(half + 1) * 512],
                    start=True, stop=True),
                    deps=[t_mp, C.t_identJ] + list(bankd))
                C.pb_reg[bi].end_write(mm)
                ev = C.dve(lambda v, half=half, bi=bi: v.tensor_copy(
                    Mh[:, half * 512:(half + 1) * 512],
                    C.pb[bi][:, :]), deps=[mm])
                C.pb_reg[bi].add_reader(ev)
                mm_j.append(ev)
            regD.end_write(mm_j)
            regD.add_reader(t_mp)

            for isup in range(4):
                jext = (isup + 1) * 512
                njt = jext // 128
                wrB2 = []
                p_done = []
                for it in range(4):
                    itile = isup * 4 + it
                    iext = (itile + 1) * 128
                    njb = (iext + 511) // 512
                    wrA = regA.begin_write() if it == 0 and isup == 0 else []
                    sc_evs = []
                    for jb in range(njb):
                        bi = jb % 4
                        bankd = C.pb_reg[bi].begin_write()
                        mm = C.pe(lambda t, bi=bi, s2=s2, itile=itile, jb=jb:
                                  t.matmul(
                                      C.pb[bi][:, :],
                                      qT[s2 * 64:(s2 + 1) * 64,
                                         itile * 128:(itile + 1) * 128],
                                      kT[s2 * 64:(s2 + 1) * 64,
                                         jb * 512:(jb + 1) * 512],
                                      start=True, stop=True),
                                  deps=list(bankd) + pair_toks["Wq"] +
                                  pair_toks["Wk"])
                        C.pb_reg[bi].end_write(mm)
                        xoff = jb * 512 - itile * 128 + 512
                        if xoff >= 0:
                            assert xoff <= 512, (jb, itile)
                            ev = C.dve(lambda v, bi=bi, jb=jb, xoff=xoff:
                                       v.scalar_tensor_tensor(
                                           s_tile[:, jb * 512:(jb + 1) * 512],
                                           C.pb[bi][:, :], 1.0,
                                           Mh[:, xoff:xoff + 512],
                                           ALU.mult, ALU.add),
                                       deps=[mm, mm_j[-1]] + wrA + prev_sp)
                        else:
                            ev = C.dve(lambda v, bi=bi, jb=jb, h=h:
                                       v.tensor_scalar(
                                           s_tile[:, jb * 512:(jb + 1) * 512],
                                           C.pb[bi][:, :],
                                           tbl0col[:, h:h + 1], None,
                                           ALU.add),
                                       deps=[mm, t_t0b] + wrA + prev_sp)
                        C.pb_reg[bi].add_reader(ev)
                        regD.add_reader(ev)
                        sc_evs.append(ev)
                    mcol = stat_m[:, itile % 16:itile % 16 + 1]
                    rcol = stat_r[:, itile % 16:itile % 16 + 1]
                    t_m = C.dve(lambda v, njb=njb, mcol=mcol: v.reduce_max(
                        mcol, s_tile[:, 0:njb * 512], AX.X),
                        deps=sc_evs + prev_sp)
                    t_nm = C.dve(lambda v, mcol=mcol: v.tensor_single_scalar(
                        mcol, mcol, -1.0, ALU.mult), deps=[t_m])
                    t_e = C.act(lambda s, njb=njb, mcol=mcol, rcol=rcol:
                                s.activation(
                                    p_tile[:, 0:njb * 512],
                                    s_tile[:, 0:njb * 512], AF.Exp,
                                    bias=mcol, accum_out=rcol),
                                deps=[t_nm] + sc_evs)
                    t_r = C.dve(lambda v, rcol=rcol: v.reciprocal(rcol, rcol),
                                deps=[t_e])
                    t_p = C.dve(lambda v, njb=njb, rcol=rcol:
                                v.tensor_single_scalar(
                                    p_tile[:, 0:njb * 512],
                                    p_tile[:, 0:njb * 512], rcol, ALU.mult),
                                deps=[t_r])
                    if pair == 0 and s2 == 0 and isup == 0 and it == 0:
                        dbg_dump(C, "l2fl", [t_p, t_e, t_m] + sc_evs)
                    njt_i = njb * 4
                    wrB2 = regB.begin_write() if it == 0 else wrB2
                    for j0 in range(0, njt_i, 8):
                        nb = min(8, njt_i - j0)
                        ev = tb_transpose_batch(
                            C,
                            [(lambda jt=jt: p_tile[:, jt * 128:(jt + 1) * 128])
                             for jt in range(j0, j0 + nb)],
                            lambda nb2, j0=j0, it=it: _l2_ptdst(C, j0, nb2, it),
                            deps=[t_p] + (wrB2 if it == 0 and j0 == 0 else []),
                            tbi=(it + j0) % 2)
                        p_done.append(ev)
                    prev_sp = [t_e, t_p] + p_done[-1:]
                regB.end_write(p_done)
                bi = 4
                bankd = C.pb_reg[bi].begin_write()
                mml3 = None
                for jt in range(njt):
                    mm = C.pe(lambda t, bi=bi, jt=jt, s2=s2: t.matmul(
                        C.pb[bi][0:64, :],
                        vsl[:, jt * 128 + s2 * 64: jt * 128 + (s2 + 1) * 64],
                        pT_sup(jt * 512, 512),
                        start=(jt == 0), stop=(jt == njt - 1)),
                        deps=(list(bankd) + p_done + v_evs) if jt == 0
                        else [p_done[-1]])
                    mml3 = mm
                C.pb_reg[bi].end_write(mml3)
                regB.add_reader(mml3)
                l2qkv.add_reader(mml3)
                st = ostg[isup % 2]
                ev = C.dve(lambda v, bi=bi, st=st: v.tensor_copy(
                    st[0:64, :], C.pb[bi][0:64, :]),
                    deps=[mml3] + regs_begin_write(ostregs[isup % 2]))
                C.pb_reg[bi].add_reader(ev)
                t_os = C.dma_sync(lambda s, h=h, isup=isup, st=st: s.dma_start(
                    W["oscr"][h, :, isup * 512:(isup + 1) * 512], st[0:64, :]),
                    deps=[ev])
                regs_end_write(ostregs[isup % 2], [ev, t_os])
                if not hasattr(C, "l2_o_toks"):
                    C.l2_o_toks = []
                C.l2_o_toks.append(t_os)

    bprow = rowb_alloc(C, 1024)
    t_bp = load_cast(C, bprow, W["bp"].unsqueeze(0))
    wtile = [wb(C, 2048, 1024), wb(C, 3072, 1024)]
    wtregs = [wb_slots(C, 2048, 1024), wb_slots(C, 3072, 1024)]
    otile = [wb(C, 0, 512), wb(C, 512, 512)]
    otregs = [Region("l2o0"), Region("l2o1")]
    for ttg in range(4):
        mml = [[None, None] for _ in range(4)]
        for hc in range(8):
            wt = wtile[hc % 2]
            t_w = load_cast(C, wt, W["Wp"][hc * 128:(hc + 1) * 128, :],
                            deps=regs_begin_write(wtregs[hc % 2]))
            regs_end_write(wtregs[hc % 2], t_w)
            for t4 in range(4):
                tt = ttg * 4 + t4
                ot = otile[(hc * 4 + t4) % 2]
                t_ot = C.dma_sync(lambda s, hc=hc, tt=tt, ot=ot: s.dma_start(
                    ot[:, 0:128],
                    W["oscr"][2 * hc:2 * hc + 2, :,
                              tt * 128:(tt + 1) * 128]
                    .rearrange("h d t -> (h d) t")),
                    deps=otregs[(hc * 4 + t4) % 2].begin_write() +
                    (C.l2_o_toks if hc == 0 and t4 == 0 else []))
                otregs[(hc * 4 + t4) % 2].end_write(t_ot)
                for fh in range(2):
                    bi = t4 * 2 + fh
                    dst = C.pb[bi] if bi < 6 else None
                    if dst is None:
                        dst = C.tb[bi - 6][:, :].bitcast(F32)
                    bankd = (C.pb_reg[bi].begin_write() if bi < 6 else
                             C.tb_reg[bi - 6].begin_write()) if hc == 0 else ()
                    mm = C.pe(lambda t, dst=dst, ot=ot, wt=wt, fh=fh, hc=hc:
                              t.matmul(
                                  dst[:, 0:512], ot[:, 0:128],
                                  wt[:, fh * 512:(fh + 1) * 512],
                                  start=(hc == 0), stop=False),
                              deps=[t_ot, t_w] + list(bankd))
                    regs_add_reader(wtregs[hc % 2], mm)
                    otregs[(hc * 4 + t4) % 2].add_reader(mm)
                    mml[t4][fh] = mm
        for t4 in range(4):
            tt = ttg * 4 + t4
            for fh in range(2):
                bi = t4 * 2 + fh
                dst = C.pb[bi] if bi < 6 else C.tb[bi - 6][:, :].bitcast(F32)
                mmb = C.pe(lambda t, dst=dst, fh=fh: t.matmul(
                    dst[:, 0:512], C.onesrow[:],
                    bprow[:, fh * 512:(fh + 1) * 512],
                    start=False, stop=True), deps=[t_bp, C.t_onesr])
                if bi < 6:
                    C.pb_reg[bi].end_write(mmb)
                else:
                    C.tb_reg[bi - 6].end_write(mmb)
                xd = C.xres_reg[tt].begin_write()
                sl = slice(tt * D + fh * 512, tt * D + (fh + 1) * 512)
                ev = C.dve(lambda v, dst=dst, sl=sl: v.tensor_add(
                    C.x_res[:, sl], C.x_res[:, sl], dst[:, 0:512]),
                    deps=[mmb] + xd)
                if bi < 6:
                    C.pb_reg[bi].add_reader(ev)
                else:
                    C.tb_reg[bi - 6].add_reader(ev)
                C.xres_reg[tt].end_write(ev)
    C.l2_o_toks = []


def _ovl_ap(stripscr, h):
    from concourse.bass_types import AP as _AP
    base = stripscr[h]
    return _AP(tensor=base.tensor, offset=h * 1168, ap=[[1, 128], [1, 1024]])


def _l2_ptdst(C, j0, nb, it):
    v = C.arena[:, AR_B:AR_B + 8192].rearrange("p (jt i) -> p jt i", i=512)
    return v[:, j0:j0 + nb, it * 128:(it + 1) * 128]



def emit_l2_ffn(C, W, prep_toks, xq_toks):
    regA, regB, regC, regD = (C.ar_reg[k] for k in "ABCD")
    wrB = regB.begin_write()
    if hasattr(C, "l2qkv"):
        wrB = wrB + C.l2qkv.begin_write()
    xqT_v = ar(C, AR_B, 0, 16384).rearrange("p (dc t) -> p dc t", t=T)
    xqv = W["xqscr"].rearrange("(tt p) d -> p tt d", p=128)
    xstg = [wb(C, 0, 1024), wb(C, 1024, 1024)]
    xstregs = [wb_slots(C, 0, 1024), wb_slots(C, 1024, 1024)]
    xqt_toks = []
    for tt in range(NTT):
        st = xstg[tt % 2]
        t_l = C.dma_sync(lambda s, tt=tt, st=st: s.dma_start(
            st, xqv[:, tt, :]),
            deps=regs_begin_write(xstregs[tt % 2]) +
            (xq_toks if tt < 2 else []))
        regs_end_write(xstregs[tt % 2], t_l)
        ev = tb_transpose_batch(
            C, [(lambda dc=dc, st=st: st[:, dc * 128:(dc + 1) * 128])
                for dc in range(NDC)],
            lambda nb, tt=tt: xqT_v[:, :, tt * 128:(tt + 1) * 128],
            deps=[t_l] + (wrB if tt == 0 else []), tbi=tt % 2)
        regs_add_reader(xstregs[tt % 2], ev)
        xqt_toks.append(ev)
    regB.end_write(xqt_toks)
    xq_toks = xqt_toks

    IBS = [512, 512, 512, 512, 512, 256]
    IB0 = [0, 512, 1024, 1536, 2048, 2560]
    wt_g = [wb(C, 0, 512), wb(C, 512, 512)]
    wt_u = [wb(C, 1024, 512), wb(C, 1536, 512)]
    wgregs = [wb_slots(C, 0, 512), wb_slots(C, 512, 512)]
    wuregs = [wb_slots(C, 1024, 512), wb_slots(C, 1536, 512)]
    sg = wb(C, 2048, 1024).bitcast(F32)
    sgregs = wb_slots(C, 2048, 1024)
    hscr = wb(C, 3072, 1024).bitcast(F32)
    hsregs = wb_slots(C, 3072, 1024)
    hq_v = ar(C, AR_A, 0, 11264)
    hqT_v = ar(C, AR_C, 0, 5632)
    wrA_all = regA.begin_write()
    for ttg in range(4):
        hq_toks = []
        wrA = regA.begin_write() if ttg > 0 else wrA_all
        for ib in range(6):
            ibw = IBS[ib]
            mmg = [None] * 4
            mmu = [None] * 4
            for dc in range(8):
                wg = wt_g[dc % 2]
                wu = wt_u[dc % 2]
                t_wg = C.dma_sync(lambda s, wg=wg, dc=dc, ib=ib, ibw=ibw:
                                  s.dma_start(wg[:, 0:ibw],
                                              W["wgqT"][dc * 128:(dc + 1) * 128,
                                                        IB0[ib]:IB0[ib] + ibw]),
                                  deps=regs_begin_write(wgregs[dc % 2]) +
                                  (prep_toks if ttg == 0 and ib == 0 and dc < 2
                                   else []))
                regs_end_write(wgregs[dc % 2], t_wg)
                t_wu = C.dma_sync(lambda s, wu=wu, dc=dc, ib=ib, ibw=ibw:
                                  s.dma_start(wu[:, 0:ibw],
                                              W["wuqT"][dc * 128:(dc + 1) * 128,
                                                        IB0[ib]:IB0[ib] + ibw]),
                                  deps=regs_begin_write(wuregs[dc % 2]))
                regs_end_write(wuregs[dc % 2], t_wu)
                for t4 in range(4):
                    tt = ttg * 4 + t4
                    bg = C.pb_reg[t4].begin_write() if dc == 0 else ()
                    mm1 = C.pe(lambda t, wg=wg, t4=t4, dc=dc, tt=tt, ibw=ibw:
                               t.matmul(C.pb[t4][:, 0:ibw],
                                        xqT_v[:, dc, tt * 128:(tt + 1) * 128],
                                        wg[:, 0:ibw],
                                        start=(dc == 0), stop=(dc == 7)),
                               deps=[t_wg] + list(bg) +
                               (xq_toks if dc == 0 and t4 == 0 else []))
                    regs_add_reader(wgregs[dc % 2], mm1)
                    mmg[t4] = mm1
                    bi2 = 4 + t4 % 2
                    bu = (C.pb_reg[bi2].begin_write() if t4 < 2 else
                          C.tb_reg[t4 - 2].begin_write()) if dc == 0 else ()
                    dstu = (C.pb[bi2][:, 0:ibw] if t4 < 2 else
                            C.tb[t4 - 2][:, :].bitcast(F32)[:, 0:ibw])
                    mm2 = C.pe(lambda t, wu=wu, dstu=dstu, dc=dc, tt=tt,
                               ibw=ibw: t.matmul(
                                   dstu,
                                   xqT_v[:, dc, tt * 128:(tt + 1) * 128],
                                   wu[:, 0:ibw],
                                   start=(dc == 0), stop=(dc == 7)),
                               deps=[t_wu] + list(bu))
                    regs_add_reader(wuregs[dc % 2], mm2)
                    mmu[t4] = mm2
            for t4 in range(4):
                dstu = (C.pb[4 + t4 % 2][:, 0:IBS[ib]] if t4 < 2 else
                        C.tb[t4 - 2][:, :].bitcast(F32)[:, 0:IBS[ib]])
                ibw = IBS[ib]
                t_sg = C.act(lambda s, t4=t4, ibw=ibw: s.activation(
                    sg[:, 0:ibw], C.pb[t4][:, 0:ibw], AF.Silu),
                    deps=[mmg[t4]] + regs_begin_write(sgregs))
                C.pb_reg[t4].end_write(mmg[t4])
                C.pb_reg[t4].add_reader(t_sg)
                regs_end_write(sgregs, t_sg)
                t_h = C.dve(lambda v, dstu=dstu, ibw=ibw: v.tensor_tensor(
                    hscr[:, 0:ibw], dstu, sg[:, 0:ibw], ALU.mult),
                    deps=[t_sg, mmu[t4]] + regs_begin_write(hsregs))
                if t4 < 2:
                    C.pb_reg[4 + t4 % 2].end_write(mmu[t4])
                    C.pb_reg[4 + t4 % 2].add_reader(t_h)
                else:
                    C.tb_reg[t4 - 2].end_write(mmu[t4])
                    C.tb_reg[t4 - 2].add_reader(t_h)
                regs_add_reader(sgregs, t_h)
                regs_end_write(hsregs, t_h)
                prevq = None
                for q0 in range(0, ibw, 128):
                    t_hq = emit_quant_fp4(
                        C, hscr[:, q0:q0 + 128],
                        hq_v[:, t4 * 2816 + IB0[ib] + q0:
                             t4 * 2816 + IB0[ib] + q0 + 128],
                        8,
                        [t_h] + ([prevq] if prevq else []) +
                        (wrA if ib == 0 and t4 == 0 and q0 == 0 else []),
                        tmp_off=4096, qf32=True)
                    prevq = t_hq
                regs_add_reader(hsregs, prevq)
                hq_toks.append(prevq)
        regA.end_write(hq_toks)
        if ttg == 0:
            dbg_dump(C, "l2hq", hq_toks)
        wdt = [wb(C, 0, 1024), wb(C, 1024, 1024)]
        wdregs = [wb_slots(C, 0, 1024), wb_slots(C, 1024, 1024)]
        for t4p in range(2):
            t4s = (t4p * 2, t4p * 2 + 1)
            wrC = regC.begin_write()
            hT_toks = []
            blocks = [(ic, t4) for ic in range(22) for t4 in t4s]
            for b0 in range(0, len(blocks), 8):
                grp = blocks[b0:b0 + 8]
                ev = tb_transpose_batch(
                    C,
                    [(lambda ic=ic, t4=t4: hq_v[
                        :, t4 * 2816 + ic * 128: t4 * 2816 + ic * 128 + 128])
                     for (ic, t4) in grp],
                    lambda nb, b0=b0: ar(C, AR_C, b0 * 128, nb * 128),
                    deps=hq_toks + (wrC if b0 == 0 else []), tbi=b0 // 8 % 2)
                hT_toks.append(ev)
            regC.end_write(hT_toks)
            mmld = [[None, None], [None, None]]
            for i0 in range(22):
                wd_ = wdt[i0 % 2]
                t_w = C.dma_sync(lambda s, wd_=wd_, i0=i0: s.dma_start(
                    wd_, W["wdqT"][i0 * 128:(i0 + 1) * 128, :]),
                    deps=regs_begin_write(wdregs[i0 % 2]))
                regs_end_write(wdregs[i0 % 2], t_w)
                for j in range(2):
                    for dmh in range(2):
                        bi = j * 2 + dmh
                        bankd = (C.pb_reg[bi].begin_write()
                                 if i0 == 0 else ())
                        mm = C.pe(lambda t, wd_=wd_, i0=i0, j=j, dmh=dmh:
                                  t.matmul(
                                      C.pb[bi][:, :] if False else
                                      C.pb[j * 2 + dmh][:, :],
                                      ar(C, AR_C, (i0 * 2 + j) * 128, 128),
                                      wd_[:, dmh * 512:(dmh + 1) * 512],
                                      start=(i0 == 0), stop=(i0 == 21)),
                                  deps=[t_w, hT_toks[-1]] + list(bankd))
                        regs_add_reader(wdregs[i0 % 2], mm)
                        regC.add_reader(mm)
                        mmld[j][dmh] = mm
            for j in range(2):
                t4 = t4s[j]
                tt = ttg * 4 + t4
                for dmh in range(2):
                    bi = j * 2 + dmh
                    C.pb_reg[bi].end_write(mmld[j][dmh])
                    xd = C.xres_reg[tt].begin_write()
                    sl = slice(tt * D + dmh * 512, tt * D + (dmh + 1) * 512)
                    ev = C.dve(lambda v, bi=bi, sl=sl: v.tensor_add(
                        C.x_res[:, sl], C.x_res[:, sl], C.pb[bi][:, :]),
                        deps=[mmld[j][dmh]] + xd)
                    C.pb_reg[bi].add_reader(ev)
                    C.xres_reg[tt].end_write(ev)


def _l2_hqtdst(C, grp, icbase):
    (ic0, t40) = grp[0]
    start = ((ic0 - icbase) * 4 + t40) * 128
    return ar(C, AR_C, start, len(grp) * 128)


def emit_layer2(C, W):
    parts = _os.environ.get("MK_L2_PARTS", "af")
    prep = emit_l2_weight_prep(C, W) if "f" in parts else []
    if "f" in parts:
        hook, xq_toks = l2_make_xq_hook(C, W)
        emit_ln(C, tile_hook=hook)
    else:
        emit_ln(C)
        xq_toks = []
    if "a" in parts:
        emit_l2_attn(C, W)
    if "f" in parts:
        emit_l2_ffn(C, W, prep, xq_toks)
    layer_reset_pools(C)




def _sap(base, off, dims):
    from concourse.bass_types import AP as _bAP
    return _bAP(tensor=base.tensor, offset=base.offset + off,
                ap=[base.ap[0]] + dims)


def l3_declare(nc):
    W = {}
    W["W_in"] = nc.dram_tensor("L3_attn_W_in", [4096, D], F32,
                               kind="ExternalInput").ap()
    W["conv_w"] = nc.dram_tensor("L3_attn_conv_w", [2048, 1, 4], F32,
                                 kind="ExternalInput").ap()
    W["conv_b"] = nc.dram_tensor("L3_attn_conv_b", [2048], F32,
                                 kind="ExternalInput").ap()
    W["W_x"] = nc.dram_tensor("L3_attn_W_x", [96, 2048], F32,
                              kind="ExternalInput").ap()
    W["W_dt"] = nc.dram_tensor("L3_attn_W_dt", [2048, 64], F32,
                               kind="ExternalInput").ap()
    W["b_dt"] = nc.dram_tensor("L3_attn_b_dt", [2048], F32,
                               kind="ExternalInput").ap()
    W["A_log"] = nc.dram_tensor("L3_attn_A_log", [2048, 16], F32,
                                kind="ExternalInput").ap()
    W["D_param"] = nc.dram_tensor("L3_attn_D_param", [2048], F32,
                                  kind="ExternalInput").ap()
    W["W_out"] = nc.dram_tensor("L3_attn_W_out", [D, 2048], F32,
                                kind="ExternalInput").ap()
    W["gate_w"] = nc.dram_tensor("L3_ffn_gate_w", [8, D], F32,
                                 kind="ExternalInput").ap()
    W["lb_bias"] = nc.dram_tensor("L3_ffn_lb_bias", [8], F32,
                                  kind="ExternalInput").ap()
    W["egw"] = nc.dram_tensor("L3_ffn_expert_gate_w", [8, 512, D], F32,
                              kind="ExternalInput").ap()
    W["euw"] = nc.dram_tensor("L3_ffn_expert_up_w", [8, 512, D], F32,
                              kind="ExternalInput").ap()
    W["edw"] = nc.dram_tensor("L3_ffn_expert_down_w", [8, D, 512], F32,
                              kind="ExternalInput").ap()
    W["sgw"] = nc.dram_tensor("L3_ffn_shared_gate_w", [1024, D], F32,
                              kind="ExternalInput").ap()
    W["suw"] = nc.dram_tensor("L3_ffn_shared_up_w", [1024, D], F32,
                              kind="ExternalInput").ap()
    W["sdw"] = nc.dram_tensor("L3_ffn_shared_down_w", [D, 1024], F32,
                              kind="ExternalInput").ap()
    _dbg_env = _os.environ.get("MK_DBG", "")

    def dbg_k(nm):
        if _dbg_env == "1" or nm in _dbg_env.split(","):
            return dict(kind="ExternalOutput")
        return {}
    W["WinT"] = nc.dram_tensor("l3_WinT", [D, 4096], BF16).ap()
    W["WxT"] = nc.dram_tensor("l3_WxT", [2048, 96], BF16).ap()
    W["WdtT"] = nc.dram_tensor("l3_WdtT", [64, 2048], BF16).ap()
    W["WoutT"] = nc.dram_tensor("l3_WoutT", [2048, D], BF16).ap()
    W["egwT"] = nc.dram_tensor("l3_egwT", [8, D, 512], BF16).ap()
    W["euwT"] = nc.dram_tensor("l3_euwT", [8, D, 512], BF16).ap()
    W["edwT"] = nc.dram_tensor("l3_edwT", [8, 512, D], BF16).ap()
    W["sgwT"] = nc.dram_tensor("l3_sgwT", [D, 1024], BF16).ap()
    W["suwT"] = nc.dram_tensor("l3_suwT", [D, 1024], BF16).ap()
    W["sdwT"] = nc.dram_tensor("l3_sdwT", [1024, D], BF16).ap()
    W["xcscr"] = nc.dram_tensor("l3_xcscr", [16, 128, T], BF16, **dbg_k("xcscr")).ap()
    W["zscr"] = nc.dram_tensor("l3_zscr", [16, 128, T], BF16, **dbg_k("zscr")).ap()
    W["xdbl"] = nc.dram_tensor("l3_xdbl", [T, 96], BF16, **dbg_k("xdbl")).ap()
    W["dscr"] = nc.dram_tensor("l3_dscr", [16, 128, T], F32, **dbg_k("dscr")).ap()
    W["y2scr"] = nc.dram_tensor("l3_y2scr", [16, 128, T], BF16, **dbg_k("y2scr")).ap()
    W["ccin"] = nc.dram_tensor("l3_ccin", [128, 128], F32).ap()
    W["ccout"] = nc.dram_tensor("l3_ccout", [128, 128], F32).ap()
    return W


def _l3t(C, src_ap, dst_ap, R, K2, toks):
    ping = K2 <= 1024
    if ping:
        stgs = [wb(C, 0, 1024), wb(C, 1024, 1024)]
        sregss = [wb_slots(C, 0, 1024), wb_slots(C, 1024, 1024)]
    else:
        stgs = [wb(C, 0, 2048)]
        sregss = [wb_slots(C, 0, 2048)]
    ost = wb(C, 2048, 1024)
    oregs = wb_slots(C, 2048, 1024)
    nkc = K2 // 128 if K2 >= 128 else 0
    dv = (dst_ap.rearrange("(kc p) r -> p kc r", p=128) if nkc else None)
    for rb in range((R + 127) // 128):
        rn = min(128, R - rb * 128)
        stg = stgs[rb % len(stgs)]
        sregs = sregss[rb % len(stgs)]
        t_ld = load_cast(C, stg[0:rn, 0:K2],
                         src_ap[rb * 128:rb * 128 + rn, :],
                         deps=regs_begin_write(sregs))
        regs_end_write(sregs, t_ld)
        if nkc:
            for k0 in range(0, nkc, 8):
                nb = min(8, nkc - k0)
                ev = tb_transpose_batch(
                    C, [(lambda kc=kc, rn=rn, stg=stg:
                         stg[0:rn, kc * 128:(kc + 1) * 128])
                        for kc in range(k0, k0 + nb)],
                    lambda nb2, rn=rn: ost[:, 0:nb2 * rn],
                    deps=[t_ld] + regs_begin_write(oregs), tbi=1,
                    src_rows=rn)
                regs_end_write(oregs, ev)
                regs_add_reader(sregs, ev)
                t_o = C.dma_sync(
                    lambda s, k0=k0, nb=nb, rb=rb, rn=rn: s.dma_start(
                        dv[:, k0:k0 + nb, rb * 128:rb * 128 + rn],
                        ost[:, 0:nb * rn].rearrange("p (kc r) -> p kc r",
                                                    r=rn)),
                    deps=[ev])
                regs_add_reader(oregs, t_o)
                toks.append(t_o)
        else:
            ev = tb_transpose_batch(
                C, [lambda rn=rn, stg=stg: stg[0:rn, 0:K2]],
                lambda nb2, rn=rn: ost[0:K2, 0:rn],
                deps=[t_ld] + regs_begin_write(oregs), tbi=1,
                src_rows=rn, src_cols=K2)
            regs_end_write(oregs, ev)
            regs_add_reader(sregs, ev)
            t_o = C.dma_sync(lambda s, rb=rb, rn=rn: s.dma_start(
                dst_ap[:, rb * 128:rb * 128 + rn], ost[0:K2, 0:rn]),
                deps=[ev])
            regs_add_reader(oregs, t_o)
            toks.append(t_o)


def emit_l3_prep_attn(C, W):
    P = {}
    for name, src, dst, R, K2 in (
            ("WinT", W["W_in"], W["WinT"], 4096, D),
            ("WxT", W["W_x"], W["WxT"], 96, 2048),
            ("WdtT", W["W_dt"], W["WdtT"], 2048, 64),
            ("WoutT", W["W_out"], W["WoutT"], D, 2048)):
        toks = []
        _l3t(C, src, dst, R, K2, toks)
        P[name] = toks
    return P


def emit_l3_prep_ffn(C, W):
    P = {}
    for e in range(8):
        for name, src, dst, R, K2 in (
                (f"egwT{e}", W["egw"][e], W["egwT"][e], 512, D),
                (f"euwT{e}", W["euw"][e], W["euwT"][e], 512, D),
                (f"edwT{e}", W["edw"][e], W["edwT"][e], D, 512)):
            toks = []
            _l3t(C, src, dst, R, K2, toks)
            P[name] = toks
    for name, src, dst, R, K2 in (
            ("sgwT", W["sgw"], W["sgwT"], 1024, D),
            ("suwT", W["suw"], W["suwT"], 1024, D),
            ("sdwT", W["sdw"], W["sdwT"], D, 1024)):
        toks = []
        _l3t(C, src, dst, R, K2, toks)
        P[name] = toks
    return P


def emit_l3_gating(C, W):
    xn_deps = C.xnT_reg.read_deps()
    stg = wb(C, 0, 1024)
    sregs = wb_slots(C, 0, 1024)
    t_g = load_cast(C, stg[0:8, 0:1024], W["gate_w"],
                    deps=regs_begin_write(sregs))
    regs_end_write(sregs, t_g)
    gsl = wb(C, 1024, 64)
    gregs = wb_slots(C, 1024, 64)
    ev_g = tb_transpose_batch(
        C, [(lambda dc=dc: stg[0:8, dc * 128:(dc + 1) * 128])
            for dc in range(8)],
        lambda nb: gsl[:, 0:64],
        deps=[t_g] + regs_begin_write(gregs), tbi=1, src_rows=8)
    regs_end_write(gregs, ev_g)
    regs_add_reader(sregs, ev_g)
    lbrow = rowb_alloc(C, 8)
    t_lb = load_cast(C, lbrow, W["lb_bias"].unsqueeze(0),
                     deps=C.rowp_reg.begin_write())
    C.rowp_reg.end_write(t_lb)
    C.l3_masks = smcol(C, 128)
    C.l3_w = smcol(C, 128)
    tmp4 = smcol(C, 4)
    m1c, m2c, s1c, _ = (tmp4[:, i:i + 1] for i in range(4))
    mb16 = wb(C, 1088, 16)
    mregs = wb_slots(C, 1088, 16)
    mask_toks = []
    cnt_mm = wsum_mm = None
    prev_sm = []
    for tt in range(NTT):
        bi = tt % 2
        bankd = C.pb_reg[bi].begin_write()
        for dc in range(8):
            mm = C.pe(lambda t, bi=bi, dc=dc, tt=tt: t.matmul(
                C.pb[bi][:, 0:8], xnT_tile(C, dc, tt * 128, 128),
                gsl[:, dc * 8:(dc + 1) * 8],
                start=(dc == 0), stop=False),
                deps=([ev_g] + list(bankd) + xn_deps) if dc == 0 else [])
            regs_add_reader(gregs, mm)
        mmb = C.pe(lambda t, bi=bi: t.matmul(
            C.pb[bi][:, 0:8], C.onesrow[:], lbrow,
            start=False, stop=True), deps=[t_lb, C.t_onesr])
        C.rowp_reg.add_reader(mmb)
        C.pb_reg[bi].end_write(mmb)
        pb8 = C.l3_w[:, tt * 8:(tt + 1) * 8]
        mb8 = C.l3_masks[:, tt * 8:(tt + 1) * 8]
        t_sg = C.act(lambda s, bi=bi, pb8=pb8: s.activation(
            pb8, C.pb[bi][:, 0:8], AF.Sigmoid), deps=[mmb] + prev_sm)
        C.pb_reg[bi].add_reader(t_sg)
        t_m1 = C.dve(lambda v, pb8=pb8: v.reduce_max(m1c, pb8, AX.X),
                     deps=[t_sg])
        t_k = C.dve(lambda v, pb8=pb8: v.tensor_scalar(
            C.qcols[:, 0:8], pb8, m1c, -1e30, ALU.is_ge, ALU.mult),
            deps=[t_m1] + C.qreg.begin_write())
        t_k2 = C.dve(lambda v, pb8=pb8: v.tensor_add(
            C.qcols[:, 0:8], C.qcols[:, 0:8], pb8), deps=[t_k])
        t_m2 = C.dve(lambda v: v.reduce_max(m2c, C.qcols[:, 0:8], AX.X),
                     deps=[t_k2])
        C.qreg.end_write([t_k, t_k2])
        C.qreg.add_reader(t_m2)
        t_mk = C.dve(lambda v, pb8=pb8, mb8=mb8: v.tensor_scalar(
            mb8, pb8, m2c, None, ALU.is_ge), deps=[t_m2])
        t_sm = C.dve(lambda v: v.tensor_tensor(s1c, m1c, m2c, ALU.add),
                     deps=[t_m2, t_m1])
        t_se = C.dve(lambda v: v.tensor_scalar(
            s1c, s1c, 1e-8, None, ALU.add), deps=[t_sm])
        t_ri = C.dve(lambda v: v.reciprocal(s1c, s1c), deps=[t_se])
        t_w = C.dve(lambda v, pb8=pb8: v.tensor_scalar(
            pb8, pb8, s1c, None, ALU.mult), deps=[t_ri])
        t_w2 = C.dve(lambda v, pb8=pb8, mb8=mb8: v.tensor_tensor(
            pb8, pb8, mb8, ALU.mult), deps=[t_w, t_mk])
        prev_sm = [t_w2, t_mk]
        mask_toks.append(t_mk)
        t_cm = C.dve(lambda v, mb8=mb8: v.tensor_copy(mb16[:, 0:8], mb8),
                     deps=[t_mk] + regs_begin_write(mregs))
        t_cw = C.dve(lambda v, pb8=pb8: v.tensor_copy(mb16[:, 8:16], pb8),
                     deps=[t_w2, t_cm])
        regs_end_write(mregs, [t_cm, t_cw])
        bd4 = C.pb_reg[4].begin_write() if tt == 0 else ()
        cnt_mm = C.pe(lambda t, tt=tt: t.matmul(
            C.pb[4][0:1, 0:8], C.onescol[:], mb16[:, 0:8],
            start=(tt == 0), stop=(tt == 15)),
            deps=[t_cm, C.t_ones] + list(bd4))
        bd5 = C.pb_reg[5].begin_write() if tt == 0 else ()
        wsum_mm = C.pe(lambda t, tt=tt: t.matmul(
            C.pb[5][0:1, 0:8], C.onescol[:], mb16[:, 8:16],
            start=(tt == 0), stop=(tt == 15)),
            deps=[t_cw, C.t_ones] + list(bd5))
        regs_add_reader(mregs, [cnt_mm, wsum_mm])
    C.pb_reg[4].end_write(cnt_mm)
    C.pb_reg[5].end_write(wsum_mm)
    scrd = C.scr_reg.begin_write()
    ccst = C.scr[:, 0:128]
    t_z = C.pool(lambda g: g.memset(ccst, 0.0), deps=scrd)
    t_c1 = C.dve(lambda v: v.tensor_copy(ccst[0:1, 0:8], C.pb[4][0:1, 0:8]),
                 deps=[t_z, cnt_mm])
    t_c2 = C.dve(lambda v: v.tensor_copy(ccst[0:1, 8:16], C.pb[5][0:1, 0:8]),
                 deps=[t_z, wsum_mm])
    C.pb_reg[4].add_reader(t_c1)
    C.pb_reg[5].add_reader(t_c2)
    C.scr_reg.end_write([t_z, t_c1, t_c2])
    t_st = C.dma_sync(lambda s: s.dma_start(W["ccin"], ccst),
                      deps=[t_c1, t_c2])
    C.scr_reg.add_reader(t_st)
    t_cc = C.pool(lambda g: g.collective_compute(
        "AllReduce", ALU.add, replica_groups=[list(range(NCORES))],
        ins=[W["ccin"].opt()], outs=[W["ccout"].opt()]), deps=[t_st])
    C.l3_cc_tok = t_cc
    C.l3_mask_toks = mask_toks


def emit_l3_gates_final(C, W):
    rows = rowf_alloc(C, 16)
    t_ld = C.dma_sync(lambda s: s.dma_start(rows, W["ccout"][0:1, 0:16]),
                      deps=[C.l3_cc_tok])
    t_mx = C.dve(lambda v: v.tensor_scalar(
        rows[:, 0:8], rows[:, 0:8], 1.0, None, ALU.max), deps=[t_ld])
    t_ri = C.dve(lambda v: v.reciprocal(rows[:, 0:8], rows[:, 0:8]),
                 deps=[t_mx])
    t_wm = C.dve(lambda v: v.tensor_tensor(
        rows[:, 8:16], rows[:, 8:16], rows[:, 0:8], ALU.mult), deps=[t_ri])
    wmb = rowb_alloc(C, 8)
    t_cb = C.dve(lambda v: v.tensor_copy(wmb, rows[:, 8:16]), deps=[t_wm])
    wmB = smcol(C, 8)
    t_b = bcast_col(C, lambda: wmb, 8, wmB, [t_cb])
    g_toks = []
    for tt in range(NTT):
        gb = C.l3_w[:, tt * 8:(tt + 1) * 8]
        t_gt = C.dve(lambda v, gb=gb, tt=tt: v.tensor_tensor(
            gb, C.l3_masks[:, tt * 8:(tt + 1) * 8], wmB, ALU.mult),
            deps=[t_b, C.l3_mask_toks[tt]])
        g_toks.append(t_gt)
    C.l3_gates = C.l3_w
    C.l3_gate_toks = g_toks


def emit_l3_attn(C, W, P):
    xn_deps = C.xnT_reg.read_deps()
    regA = C.ar_reg["A"]
    regB = C.ar_reg["B"]
    regC = C.ar_reg["C"]
    regD = C.ar_reg["D"]
    baseb = C.arena[:, :]
    basef = C.arena[:, :].bitcast(F32)
    scrv = C.scr[:, :]
    scrb = C.scr[:, :].bitcast(BF16)
    wbv = C.wbuf[:, :]
    wbf = C.wbuf[:, :].bitcast(F32)
    sdeps = C.scr_reg.begin_write()
    scr_toks = []

    t_Al = C.dma_sync(lambda s: s.dma_start(
        C.scr[:, 0:256].rearrange("p (dc n) -> p dc n", n=16),
        W["A_log"].rearrange("(dc p) n -> p dc n", p=128)), deps=sdeps)
    t_Ae = C.act(lambda s: s.activation(
        C.scr[:, 0:256], C.scr[:, 0:256], AF.Exp), deps=[t_Al])
    t_An = C.dve(lambda v: v.tensor_single_scalar(
        C.scr[:, 0:256], C.scr[:, 0:256], -1.0, ALU.mult), deps=[t_Ae])
    t_Dl = C.dma_sync(lambda s: s.dma_start(
        C.scr[:, 256:272],
        W["D_param"].rearrange("(dc p) -> p dc", p=128),
        allow_slow_non_contiguous=True), deps=sdeps)
    t_bdt = C.dma_sync(lambda s: s.dma_start(
        C.scr[:, 336:352],
        W["b_dt"].rearrange("(dc p) -> p dc", p=128),
        allow_slow_non_contiguous=True), deps=sdeps)
    scr_toks += [t_An, t_Dl, t_bdt]

    Adeps = regA.begin_write()
    Cdeps = regC.begin_write()
    stagef = [basef[:, 0:2051], basef[:, 2052:4103]]
    accf = basef[:, 4104:6152]
    xcst = [ar(C, AR_C, 0, 2048), ar(C, AR_C, 2048, 2048)]
    winb = [wb(C, 0, 1024), wb(C, 1024, 1024)]
    winregs = [wb_slots(C, 0, 1024), wb_slots(C, 1024, 1024)]
    rgSt = [Region("l3st0"), Region("l3st1")]
    rgAcc = Region("l3acc")
    rgXc = [Region("l3xc0"), Region("l3xc1")]
    rgWc = [Region("l3wc0"), Region("l3wc1")]
    winv = W["WinT"].rearrange("(dc p) e -> p dc e", p=128)
    cwv = W["conv_w"].rearrange("(ec p) a k -> p ec (a k)", p=128)
    cbv = W["conv_b"].rearrange("(ec p) -> p ec", p=128)
    t_p0 = C.pool(lambda g: g.memset(stagef[0][:, 0:3], 0.0), deps=Adeps)
    t_p1 = C.pool(lambda g: g.memset(stagef[1][:, 0:3], 0.0), deps=Adeps)
    for rg in rgSt:
        rg.end_write([t_p0, t_p1])
    rgAcc.end_write(Adeps)
    for rg in rgXc:
        rg.end_write(Cdeps)
    for rg in rgWc:
        rg.end_write(sdeps)
    xc_store = [None] * 16
    z_store = [None] * 16
    aA_toks = [t_p0, t_p1]
    cA_toks = []
    for ec in range(32):
        pp = ec % 2
        t_w = C.dma_sync(lambda s, ec=ec, pp=pp: s.dma_start(
            winb[pp][:, :].rearrange("p (dc e) -> p dc e", e=128),
            winv[:, :, ec * 128:(ec + 1) * 128]),
            deps=[P["WinT"][ec]] + regs_begin_write(winregs[pp]))
        regs_end_write(winregs[pp], t_w)
        xpath = ec < 16
        if xpath:
            t_wc = C.dma_sync(lambda s, ec=ec, pp=pp: s.dma_start(
                C.scr[:, 272 + pp * 10:276 + pp * 10], cwv[:, ec, :]),
                deps=rgWc[pp].begin_write())
            t_cb = C.dma_sync(lambda s, ec=ec, pp=pp: s.dma_start(
                C.scr[:, 276 + pp * 10:277 + pp * 10], cbv[:, ec:ec + 1],
                allow_slow_non_contiguous=True), deps=[t_wc])
            rgWc[pp].end_write([t_wc, t_cb])
        ev4 = []
        for ts4 in range(4):
            bi = (ec * 4 + ts4) % 6
            bankd = C.pb_reg[bi].begin_write()
            mml = None
            for dc in range(8):
                mml = C.pe(lambda t, bi=bi, dc=dc, ts4=ts4, pp=pp: t.matmul(
                    C.pb[bi][:, :], winb[pp][:, dc * 128:(dc + 1) * 128],
                    xnT_tile(C, dc, ts4 * 512, 512),
                    start=(dc == 0), stop=(dc == 7)),
                    deps=([t_w] + list(bankd) + (xn_deps if ec == 0 else []))
                    if dc == 0 else [])
                regs_add_reader(winregs[pp], mml)
            C.pb_reg[bi].end_write(mml)
            if xpath:
                ev = C.dve(lambda v, bi=bi, ts4=ts4, pp=pp: v.tensor_copy(
                    stagef[pp][:, 3 + ts4 * 512:3 + ts4 * 512 + 512],
                    C.pb[bi][:, :]),
                    deps=[mml] + (rgSt[pp].begin_write() if ts4 == 0 else []))
            else:
                ev = C.dve(lambda v, bi=bi, ts4=ts4, pp=pp: v.tensor_copy(
                    xcst[pp][:, ts4 * 512:(ts4 + 1) * 512], C.pb[bi][:, :]),
                    deps=[mml] + (rgXc[pp].begin_write() if ts4 == 0 else []))
            C.pb_reg[bi].add_reader(ev)
            ev4.append(ev)
        if xpath:
            rgSt[pp].end_write(ev4)
            aA_toks += ev4
            wc = C.scr[:, 272 + pp * 10:277 + pp * 10]
            ad = rgAcc.begin_write()
            t_c = C.dve(lambda v, pp=pp, wc=wc: v.tensor_scalar(
                accf, stagef[pp][:, 0:2048], wc[:, 0:1], None, ALU.mult),
                deps=ev4 + [t_wc] + ad)
            for k in (1, 2, 3):
                t_c = C.dve(lambda v, pp=pp, wc=wc, k=k:
                            v.scalar_tensor_tensor(
                                accf, stagef[pp][:, k:2048 + k],
                                wc[:, k:k + 1], accf, ALU.mult, ALU.add),
                            deps=[t_c])
            rgAcc.end_write(t_c)
            rgSt[pp].add_reader(t_c)
            rgWc[pp].add_reader(t_c)
            aA_toks.append(t_c)
            xcw = rgXc[pp].begin_write()
            t_si = C.act(lambda s, pp=pp, wc=wc: s.activation(
                xcst[pp][:, :], accf, AF.Silu, bias=wc[:, 4:5]),
                deps=[t_c, t_cb] + xcw)
            rgAcc.add_reader(t_si)
            rgWc[pp].add_reader(t_si)
            rgXc[pp].end_write(t_si)
            aA_toks.append(t_si)
            cA_toks.append(t_si)
            scr_toks.append(t_si)
            t_o = C.dma_sync(lambda s, ec=ec, pp=pp: s.dma_start(
                W["xcscr"][ec], xcst[pp][:, :]), deps=[t_si])
            xc_store[ec] = t_o
        else:
            rgXc[pp].end_write(ev4)
            cA_toks += ev4
            t_o = C.dma_sync(lambda s, ec=ec, pp=pp: s.dma_start(
                W["zscr"][ec - 16], xcst[pp][:, :]), deps=ev4)
            z_store[ec - 16] = t_o
        rgXc[pp].add_reader(t_o)
        cA_toks.append(t_o)
    regA.end_write(aA_toks)
    regC.end_write(cA_toks)
    if _os.environ.get("MK_L3_PH", "ABCD") == "A":
        C.sync(lambda s: s.nop(), deps=cA_toks + scr_toks + sum(P.values(), []))
        C.scr_reg.end_write(scr_toks)
        return

    Adeps = regA.begin_write()
    Cdeps = regC.begin_write()
    bcut = _os.environ.get("MK_L3_B_CUT", "")
    wdts = ar(C, AR_A, 0, 2048)
    t_wdt = None
    if bcut != "x" and bcut != "t":
        t_wdt = C.dma_sync(lambda s: s.dma_start(
            wdts[0:64, :], W["WdtT"][:, :]), deps=P["WdtT"] + Adeps)
    wxs = wb(C, 0, 1536)
    wxregs = wb_slots(C, 0, 1536)
    t_wx = C.dma_sync(lambda s: s.dma_start(
        wxs[:, :].rearrange("p (ec n) -> p ec n", n=96),
        W["WxT"].rearrange("(ec p) n -> p ec n", p=128)),
        deps=P["WxT"] + regs_begin_write(wxregs))
    regs_end_write(wxregs, t_wx)
    xdst = [ar(C, AR_C, 0, 96), ar(C, AR_C, 96, 96)]
    dtlst = [ar(C, AR_C, 192, 128), ar(C, AR_C, 320, 128)]
    tmp1 = basef[:, 16640:17152]
    tmp2 = basef[:, 17152:17664]
    dstg = basef[:, 17664:18176]
    xcb = wb(C, 2048, 2048)
    wsl2 = regs_begin_write([C.wslot[2], C.wslot[3]])
    rgXd = [Region("l3xd0"), Region("l3xd1")]
    rgDtl = [Region("l3dt0"), Region("l3dt1")]
    rgT1 = Region("l3t1")
    rgT2 = Region("l3t2")
    rgDs = Region("l3ds")
    rgXcb = [Region("l3xb0"), Region("l3xb1")]
    for rg in (rgXd[0], rgXd[1], rgDtl[0], rgDtl[1], rgT1, rgT2, rgDs):
        rg.end_write(Cdeps)
    rgXcb[0].end_write(wsl2)
    dscr_store = [[] for _ in range(NTT)]
    xdbl_store = [None] * NTT
    cB_toks = []
    xcb_all = []
    delta_mms = []
    dvv = W["dscr"].rearrange("dc p t -> p dc t")
    xcvB = W["xcscr"].rearrange("dc p t -> p dc t")
    for tt in range(NTT):
        pq = tt % 2
        bi = pq
        bankd = C.pb_reg[bi].begin_write()
        t_xt = C.dma_sync(lambda s, tt=tt: s.dma_start(
            xcb[:, :].rearrange("p (dc t) -> p dc t", t=128),
            xcvB[:, :, tt * 128:(tt + 1) * 128]),
            deps=xc_store + rgXcb[0].begin_write())
        rgXcb[0].end_write(t_xt)
        xcb_all.append(t_xt)
        mml = None
        for ec in range(16):
            mml = C.pe(lambda t, bi=bi, ec=ec: t.matmul(
                C.pb[bi][:, 0:96], xcb[:, ec * 128:(ec + 1) * 128],
                wxs[:, ec * 96:(ec + 1) * 96],
                start=(ec == 0), stop=False),
                deps=[t_xt, t_wx] + (list(bankd) if ec == 0 else []))
            rgXcb[0].add_reader(mml)
            regs_add_reader(wxregs, mml)
        xcb_all.append(mml)
        mml = mm_close(C, C.pb[bi][:, 0:96], 96, [mml])
        C.pb_reg[bi].end_write(mml)
        t_xe = C.dve(lambda v, bi=bi, pq=pq: v.tensor_copy(
            xdst[pq][:, :], C.pb[bi][:, 0:96]),
            deps=[mml] + rgXd[pq].begin_write())
        C.pb_reg[bi].add_reader(t_xe)
        rgXd[pq].end_write(t_xe)
        t_xo = C.dma_sync(lambda s, tt=tt, pq=pq: s.dma_start(
            W["xdbl"][tt * 128:(tt + 1) * 128, :], xdst[pq][:, :]),
            deps=[t_xe])
        xdbl_store[tt] = t_xo
        rgXd[pq].add_reader(t_xo)
        if bcut == "x":
            cB_toks += [t_xe, t_xo]
            continue
        ev_t = tb_transpose_batch(
            C, [lambda pq=pq: xdst[pq][:, 0:64]],
            lambda nb, pq=pq: dtlst[pq][0:64, 0:128],
            deps=[t_xe] + rgDtl[pq].begin_write(),
            tbi=0, src_rows=128, src_cols=64)
        rgXd[pq].add_reader(ev_t)
        rgDtl[pq].end_write(ev_t)
        cB_toks += [t_xe, t_xo, ev_t]
        if bcut == "t":
            continue
        if tt >= int(_os.environ.get("MK_L3_B_NTT", "16")):
            continue
        for dsg in range(4):
            bj = 2 + dsg % 2
            bankd2 = C.pb_reg[bj].begin_write()
            mms = []
            for ds4 in range(4):
                ds = dsg * 4 + ds4
                mm = C.pe(lambda t, bj=bj, ds=ds, ds4=ds4, pq=pq: t.matmul(
                    C.pb[bj][:, ds4 * 128:(ds4 + 1) * 128],
                    wdts[0:64, ds * 128:(ds + 1) * 128],
                    dtlst[pq][0:64, :], start=True, stop=False),
                    deps=[ev_t, t_wdt] + (list(bankd2) if ds4 == 0 else []))
                rgDtl[pq].add_reader(mm)
                delta_mms.append(mm)
                mm = mm_close(C, C.pb[bj][:, ds4 * 128:(ds4 + 1) * 128],
                              128, [mm])
                mms.append(mm)
                delta_mms.append(mm)
            C.pb_reg[bj].end_write(mms)
            t1d = rgT1.begin_write()
            if bcut == "d1":
                tcp = C.dve(lambda v, bj=bj: v.tensor_copy(
                    tmp1, C.pb[bj][:, :]), deps=mms + t1d)
                C.pb_reg[bj].add_reader(tcp)
                rgT1.end_write(tcp)
                cB_toks.append(tcp)
                continue
            if bcut == "d6":
                tcp = C.dve(lambda v, bj=bj: v.tensor_copy(
                    tmp1, C.pb[bj][:, :]), deps=mms + t1d)
                C.pb_reg[bj].add_reader(tcp)
                t_cl = C.dve(lambda v: v.tensor_scalar(
                    dstg, tmp1, 1e-4, 10.0, ALU.add, ALU.min),
                    deps=[tcp] + rgDs.begin_write())
                rgDs.end_write(t_cl)
                rgT1.end_write([tcp, t_cl])
                cB_toks.append(t_cl)
                continue
            tls = []
            for ds4 in range(4):
                bcol = C.scr[:, 336 + dsg * 4 + ds4:337 + dsg * 4 + ds4]
                if _os.environ.get("MK_L3_TSIMM"):
                    tl = C.dve(lambda v, bj=bj, ds4=ds4:
                               v.tensor_scalar(
                                   tmp1[:, ds4 * 128:(ds4 + 1) * 128],
                                   C.pb[bj][:, ds4 * 128:(ds4 + 1) * 128],
                                   0.5, None, ALU.add),
                               deps=[mms[-1], t_bdt] +
                               (t1d if ds4 == 0 else []))
                else:
                    tl = C.dve(lambda v, bj=bj, ds4=ds4, bcol=bcol:
                               v.tensor_scalar(
                                   tmp1[:, ds4 * 128:(ds4 + 1) * 128],
                                   C.pb[bj][:, ds4 * 128:(ds4 + 1) * 128],
                                   bcol, None, ALU.add),
                               deps=[mms[-1], t_bdt] +
                               (t1d if ds4 == 0 else []))
                C.pb_reg[bj].add_reader(tl)
                tls.append(tl)
            if bcut == "d5":
                t_cl = C.dve(lambda v: v.tensor_copy(dstg, tmp1),
                             deps=tls + rgDs.begin_write())
                rgT1.add_reader(t_cl)
                rgDs.end_write(t_cl)
                cB_toks.append(t_cl)
                rgT1.end_write(tls)
                continue
            if bcut == "d3":
                t_cl = C.dve(lambda v: v.tensor_scalar(
                    dstg, tmp1, 1e-4, 10.0, ALU.add, ALU.min),
                    deps=tls + rgDs.begin_write())
                _unused = 0
                rgT1.add_reader(t_cl)
                rgDs.end_write(t_cl)
                cB_toks.append(t_cl)
                rgT1.end_write(tls)
                continue
            t_ab = C.act(lambda s: s.activation(tmp2, tmp1, AF.Abs),
                         deps=tls + rgT2.begin_write())
            t_ex = C.act(lambda s: s.activation(tmp2, tmp2, AF.Exp,
                                                scale=-1.0), deps=[t_ab])
            t_ln = C.act(lambda s: s.activation(tmp2, tmp2, AF.Ln,
                                                bias=1.0), deps=[t_ex])
            rgT2.end_write(t_ln)
            t_rl = C.dve(lambda v: v.tensor_scalar(
                tmp1, tmp1, 0.0, None, ALU.max), deps=[t_ab])
            t_ad = C.dve(lambda v: v.tensor_add(tmp1, tmp1, tmp2),
                         deps=[t_rl, t_ln])
            rgT1.end_write(tls + [t_rl, t_ad])
            rgT2.add_reader(t_ad)
            t_cl = C.dve(lambda v: v.tensor_scalar(
                dstg, tmp1, 1e-4, 10.0, ALU.add, ALU.min),
                deps=[t_ad] + rgDs.begin_write())
            rgT1.add_reader(t_cl)
            rgDs.end_write(t_cl)
            if bcut == "d2":
                cB_toks.append(t_cl)
                continue
            t_sp = C.dma_sync(lambda s, tt=tt, dsg=dsg: s.dma_start(
                dvv[:, dsg * 4:(dsg + 1) * 4, tt * 128:(tt + 1) * 128],
                dstg.rearrange("p (ds t) -> p ds t", t=128)), deps=[t_cl])
            rgDs.add_reader(t_sp)
            dscr_store[tt].append(t_sp)
            cB_toks += [t_ln, t_ad, t_cl, t_sp]
    regA.end_write([t_wdt] if t_wdt is not None else [])
    regA.add_reader(delta_mms)
    regC.end_write(cB_toks)
    regs_end_write([C.wslot[2]], xcb_all)
    if bcut in ("", "d2", "d3", "d5", "d6"):
        scr_toks += [t_cl]
    if _os.environ.get("MK_L3_PH", "ABCD") == "AB":
        C.sync(lambda s: s.nop(), deps=cB_toks + scr_toks + sum(P.values(), []))
        C.scr_reg.end_write(scr_toks)
        return

    C0f, TMPf, DELf = 16384, 19456, 20480
    XC0, Z0 = AR_C + 4096, AR_C + 5120
    Bdeps = regB.begin_write()
    Cdeps = regC.begin_write()
    Ddeps = regD.begin_write()
    rgDel = Region("l3del")
    rgXcc = Region("l3xcc")
    rgZc = Region("l3zc")
    rgDx = Region("l3dx")
    rgHs = [Region("l3hs0"), Region("l3hs1")]
    rgTmp = [Region("l3tp0"), Region("l3tp1")]
    rgDel.end_write(Ddeps)
    for rg in (rgXcc, rgZc, rgHs[0], rgHs[1], rgTmp[0], rgTmp[1]):
        rg.end_write(Cdeps)
    rowsc = rowb_alloc(C, 2048)
    brow = rowsc[0:1, 0:1024]
    crow = rowsc[0:1, 1024:2048]
    rgBr = Region("l3br")
    rgCr = Region("l3cr")
    rowd = C.rowp_reg.begin_write()
    rgBr.end_write(rowd)
    rgCr.end_write(rowd)
    xcv = W["xcscr"].rearrange("dc p t -> p dc t")
    zv = W["zscr"].rearrange("dc p t -> p dc t")
    y2v = W["y2scr"].rearrange("dc p t -> p dc t")
    prev_tok = [None, None]
    y2_store = [None] * 32
    slot_regs = [[C.wslot[0]], [C.wslot[1]], [C.wslot[2], C.wslot[3]],
                 [C.wslot[4]]]
    cC_toks = []
    t_dl_last = None
    fence = None
    use_fence = not _os.environ.get("MK_L3_NOFENCE")
    chunk_last = []
    for c in range(32):
        t0 = c * 64
        tt2 = c // 2
        if use_fence and c > 0:
            fence = C.sync(lambda s: s.nop(), deps=chunk_last)
            chunk_last = []
        fdep = [fence] if (use_fence and fence is not None) else []
        t_dl = C.dma_sync(lambda s, t0=t0: s.dma_start(
            basef[:, DELf:DELf + 1024].rearrange("p (dc t) -> p dc t", t=64),
            dvv[:, :, t0:t0 + 64]),
            deps=dscr_store[tt2] + rgDel.begin_write() + fdep)
        rgDel.end_write(t_dl)
        t_dl_last = t_dl
        t_xc = C.dma_sync(lambda s, t0=t0: s.dma_start(
            baseb[:, XC0:XC0 + 1024].rearrange("p (dc t) -> p dc t", t=64),
            xcv[:, :, t0:t0 + 64]),
            deps=xc_store + rgXcc.begin_write() + fdep)
        rgXcc.end_write(t_xc)
        t_zl = C.dma_sync(lambda s, t0=t0: s.dma_start(
            baseb[:, Z0:Z0 + 1024].rearrange("p (dc t) -> p dc t", t=64),
            zv[:, :, t0:t0 + 64]),
            deps=z_store + rgZc.begin_write())
        t_br = C.dma_sync(lambda s, t0=t0: s.dma_start(
            brow.rearrange("a (t n) -> a t n", n=16),
            W["xdbl"][t0:t0 + 64, 64:80].unsqueeze(0)),
            deps=[xdbl_store[tt2]] + rgBr.begin_write())
        rgBr.end_write(t_br)
        t_cr = C.dma_sync(lambda s, t0=t0: s.dma_start(
            crow.rearrange("a (t n) -> a t n", n=16),
            W["xdbl"][t0:t0 + 64, 80:96].unsqueeze(0)),
            deps=[xdbl_store[tt2]] + rgCr.begin_write())
        rgCr.end_write(t_cr)
        bb_toks, cb_toks = [], []
        for src_row, rg, dst0, out, t_r in ((brow, rgBr, 0, bb_toks, t_br),
                                            (crow, rgCr, 1024, cb_toks,
                                             t_cr)):
            wsl = regs_begin_write(slot_regs[dst0 // 1024])
            for q in (0, 1):
                wd5 = C.pb_reg[5].begin_write()
                mm = C.pe(lambda t, src_row=src_row, q=q: t.matmul(
                    C.pb[5][:, :], C.onesrow[:],
                    src_row[0:1, q * 512:(q + 1) * 512],
                    start=True, stop=True),
                    deps=wd5 + [t_r, C.t_onesr])
                C.pb_reg[5].end_write(mm)
                rg.add_reader(mm)
                ev = C.dve(lambda v, q=q, dst0=dst0: v.tensor_copy(
                    wbv[:, dst0 + q * 512:dst0 + (q + 1) * 512],
                    C.pb[5][:, :]), deps=[mm] + (wsl if q == 0 else []))
                C.pb_reg[5].add_reader(ev)
                out.append(ev)
            regs_end_write(slot_regs[dst0 // 1024], out)
        exp_toks = []
        aw = regA.begin_write()
        bw = regB.begin_write() if c > 0 else list(Bdeps)
        t_prea = None
        for hf in range(2):
            in0 = _sap(basef, DELf + 32 * hf, [[1, 32], [64, 16], [0, 16]])
            in1 = _sap(scrv, 0, [[0, 32], [16, 16], [1, 16]])
            pre = _sap(basef, 8192, [[256, 32], [16, 16], [1, 16]])
            t_prea = C.dve(lambda g, in0=in0, in1=in1, pre=pre:
                           g.tensor_tensor(pre, in0, in1, ALU.mult),
                           deps=[t_dl, t_An] +
                           (bw if hf == 0 else [exp_toks[0]]))
            rgDel.add_reader(t_prea)
            tex = C.act(lambda s, hf=hf: s.activation(
                baseb[:, AR_A + hf * 8192:AR_A + (hf + 1) * 8192],
                basef[:, 8192:16384],
                AF.Exp), deps=[t_prea] + (aw if hf == 0 else [])
                + [C.t_zmm])
            exp_toks.append(tex)
        regA.end_write(exp_toks)
        t_dx = C.dve(lambda v: v.tensor_tensor(
            scrb[:, 704:1728], basef[:, DELf:DELf + 1024],
            baseb[:, XC0:XC0 + 1024], ALU.mult),
            deps=[t_dl, t_xc] + rgDx.begin_write() +
            (sdeps if c == 0 else []))
        rgDx.end_write(t_dx)
        rgDel.add_reader(t_dx)
        rgXcc.add_reader(t_dx)
        in0 = _sap(scrb, 704, [[1, 64], [64, 16], [0, 16]])
        in1 = _sap(wbv, 0, [[16, 64], [0, 16], [1, 16]])
        buo = _sap(baseb, AR_B, [[256, 64], [16, 16], [1, 16]])
        t_bu = C.dve(lambda g, in0=in0, in1=in1, buo=buo:
                     g.tensor_tensor(buo, in0, in1, ALU.mult),
                     deps=[t_dx] + bb_toks + exp_toks)
        regs_add_reader(slot_regs[0], t_bu)
        rgDx.add_reader(t_bu)
        t_bc = C.dve(lambda g: g.tensor_scalar(
            baseb[:, AR_B:AR_B + 16384], baseb[:, AR_B:AR_B + 16384],
            -10.0, 10.0, ALU.max, ALU.min), deps=[t_bu])
        regB.end_write([t_prea, t_bu, t_bc])
        ybw = regs_begin_write(slot_regs[2])
        yb_toks = []
        for s in range(16):
            S = c * 16 + s
            b = S % 2
            subw = rgHs[b].begin_write()
            sub_o3 = []
            for tl in range(4):
                t = 4 * s + tl
                cur = basef[:, C0f + b * 1024 + tl * 256:
                            C0f + b * 1024 + (tl + 1) * 256]
                a_t = baseb[:, AR_A + t * 256:AR_A + (t + 1) * 256]
                bu_t = baseb[:, AR_B + t * 256:AR_B + (t + 1) * 256]
                dd = (subw if tl == 0 else [])
                if S == 0 and tl == 0:
                    o1 = C.dve(lambda e, cur=cur, bu_t=bu_t:
                               e.tensor_copy(cur, bu_t),
                               deps=[t_bc] + dd)
                else:
                    pb_ = (S - 1) % 2 if tl == 0 else b
                    ptl = 3 if tl == 0 else tl - 1
                    prev = basef[:, C0f + pb_ * 1024 + ptl * 256:
                                 C0f + pb_ * 1024 + (ptl + 1) * 256]
                    o1 = C.dver(lambda e, cur=cur, prev=prev, a_t=a_t:
                                e.tensor_tensor(cur, prev, a_t, ALU.mult),
                                deps=[prev_tok[0],
                                      exp_toks[t // 32], t_bc] + dd)
                    if tl == 0:
                        rgHs[pb_].add_reader(o1)
                    o1 = C.dver(lambda e, cur=cur, bu_t=bu_t:
                                e.tensor_tensor(cur, cur, bu_t, ALU.add),
                                deps=[o1])
                o3 = C.dver(lambda e, cur=cur: e.tensor_scalar(
                    cur, cur, -20.0, 20.0, ALU.max, ALU.min),
                    deps=[o1])
                prev_tok[0] = o3
                sub_o3.append(o3)
            rgHs[b].end_write(sub_o3)
            regA.add_reader(sub_o3)
            regB.add_reader(sub_o3)
            hsin = _sap(basef, C0f + b * 1024, [[16, 16], [256, 4], [1, 16]])
            cbin = _sap(wbv, 1024 + 64 * s, [[0, 16], [16, 4], [1, 16]])
            ytmp = _sap(basef, TMPf, [[64, 16], [16, 4], [1, 16]])
            t_yt = C.dve(lambda v, hsin=hsin, cbin=cbin, ytmp=ytmp:
                         v.tensor_tensor(ytmp, hsin, cbin, ALU.mult),
                         deps=[sub_o3[-1]] + cb_toks +
                         rgTmp[0].begin_write() + rgTmp[1].begin_write())
            rgHs[b].add_reader(t_yt)
            regs_add_reader(slot_regs[1], t_yt)
            t_yr = C.dve(lambda v, ytmp=ytmp, s=s: v.reduce_sum(
                _sap(wbf, 1024 + 4 * s, [[64, 16], [1, 4]]),
                ytmp, AX.X), deps=[t_yt] + (ybw if s == 0 else []))
            rgTmp[0].end_write(t_yt)
            rgTmp[1].end_write(t_yt)
            rgTmp[0].add_reader(t_yr)
            rgTmp[1].add_reader(t_yr)
            yb_toks.append(t_yr)
            cC_toks += sub_o3
        regs_end_write(slot_regs[2], yb_toks)
        t_sz = C.act(lambda s: s.activation(
            baseb[:, Z0:Z0 + 1024], baseb[:, Z0:Z0 + 1024], AF.Silu),
            deps=[t_zl])
        rgZc.end_write([t_zl, t_sz])
        tq = rgTmp[0].begin_write() + rgTmp[1].begin_write()
        in0 = _sap(baseb, XC0, [[64, 16], [1, 64]])
        in1 = _sap(scrv, 256, [[1, 16], [0, 64]])
        y2f = _sap(basef, TMPf, [[64, 16], [1, 64]])
        t_q1 = C.dve(lambda g, in0=in0, in1=in1, y2f=y2f: g.tensor_tensor(
            y2f, in0, in1, ALU.mult), deps=[t_xc, t_Dl] + tq)
        rgXcc.add_reader(t_q1)
        t_q2 = C.dve(lambda g, y2f=y2f: g.tensor_tensor(
            y2f, y2f, _sap(wbf, 1024, [[64, 16], [1, 64]]), ALU.add),
            deps=[t_q1] + yb_toks)
        regs_add_reader(slot_regs[2], t_q2)
        t_q3 = C.dve(lambda g: g.tensor_scalar(
            basef[:, TMPf:TMPf + 1024], basef[:, TMPf:TMPf + 1024],
            -50.0, 50.0, ALU.max, ALU.min), deps=[t_q2])
        y2w = regs_begin_write(slot_regs[3])
        t_q4 = C.dve(lambda g: g.tensor_tensor(
            wbv[:, 4096:5120], basef[:, TMPf:TMPf + 1024],
            baseb[:, Z0:Z0 + 1024], ALU.mult), deps=[t_q3, t_sz] + y2w)
        rgZc.add_reader(t_q4)
        rgTmp[0].end_write(t_q4)
        rgTmp[1].end_write(t_q4)
        regs_end_write(slot_regs[3], t_q4)
        t_ys = C.dma_sync(lambda s, t0=t0: s.dma_start(
            y2v[:, :, t0:t0 + 64],
            wbv[:, 4096:5120].rearrange("p (dc t) -> p dc t", t=64)),
            deps=[t_q4])
        regs_add_reader(slot_regs[3], t_ys)
        y2_store[c] = t_ys
        chunk_last = [t_ys, t_q4, prev_tok[0], yb_toks[-1],
                      exp_toks[-1], t_bc, t_sz]
        cC_toks += [t_xc, t_zl, t_sz, t_q4, t_ys]
    regC.end_write(cC_toks)
    regD.end_write([t_dl_last])
    regD.add_reader([t_dx])
    scr_toks += [t_dx]
    if _os.environ.get("MK_L3_PH", "ABCD") == "ABC":
        C.sync(lambda s: s.nop(), deps=cC_toks + y2_store + scr_toks + sum(P.values(), []))
        C.scr_reg.end_write(scr_toks)
        return

    Adeps = regA.begin_write()
    wov = ar(C, AR_A, 0, 16384).rearrange("p (dc e) -> p dc e", e=1024)
    t_wo = C.dma_sync(lambda s: s.dma_start(
        wov, W["WoutT"].rearrange("(dc p) e -> p dc e", p=128)),
        deps=P["WoutT"] + Adeps)
    ylb = [wb(C, 0, 2048), wb(C, 2048, 2048)]
    ylregs = [wb_slots(C, 0, 2048), wb_slots(C, 2048, 2048)]
    wo_reads = []
    for tt in range(NTT):
        pp = tt % 2
        t_yl = C.dma_sync(lambda s, tt=tt, pp=pp: s.dma_start(
            ylb[pp][:, :].rearrange("p (dc t) -> p dc t", t=128),
            y2v[:, :, tt * 128:(tt + 1) * 128]),
            deps=[y2_store[2 * tt], y2_store[2 * tt + 1]] +
            regs_begin_write(ylregs[pp]))
        regs_end_write(ylregs[pp], t_yl)
        xd = C.xres_reg[tt].begin_write()
        evs = []
        for dmh in range(2):
            bi = (tt * 2 + dmh) % 4
            bankd = C.pb_reg[bi].begin_write()
            mml = None
            for dc in range(16):
                mml = C.pe(lambda t, bi=bi, dc=dc, dmh=dmh, pp=pp: t.matmul(
                    C.pb[bi][:, :], ylb[pp][:, dc * 128:(dc + 1) * 128],
                    wov[:, dc, dmh * 512:(dmh + 1) * 512],
                    start=(dc == 0), stop=(dc == 15)),
                    deps=[t_yl, t_wo] + (list(bankd) if dc == 0 else []))
                regs_add_reader(ylregs[pp], mml)
            C.pb_reg[bi].end_write(mml)
            wo_reads.append(mml)
            sl = slice(tt * D + dmh * 512, tt * D + (dmh + 1) * 512)
            ev = C.dve(lambda v, bi=bi, sl=sl: v.tensor_add(
                C.x_res[:, sl], C.x_res[:, sl], C.pb[bi][:, :]),
                deps=[mml] + xd)
            C.pb_reg[bi].add_reader(ev)
            evs.append(ev)
        C.xres_reg[tt].end_write(evs)
    regA.end_write([t_wo])
    regA.add_reader(wo_reads)
    C.scr_reg.end_write(scr_toks)


def emit_l3_ffn(C, W, P):
    xn_deps = C.xnT_reg.read_deps()
    regA = C.ar_reg["A"]
    regB = C.ar_reg["B"]
    regC = C.ar_reg["C"]
    Adeps = regA.begin_write()
    Bdeps = regB.begin_write()
    Cdeps = regC.begin_write()
    rgGU = [Region("l3gu0"), Region("l3gu1")]
    rgDN = [Region("l3dn0"), Region("l3dn1")]
    rgH = Region("l3h")
    for rg in rgGU:
        rg.end_write(Bdeps)
    for rg in rgDN:
        rg.end_write(Adeps)
    rgH.end_write(Cdeps)
    sgf = [wb(C, 0, 1024).bitcast(F32), wb(C, 1024, 1024).bitcast(F32)]
    sgregs = [wb_slots(C, 0, 1024), wb_slots(C, 1024, 1024)]
    all_mms = []
    for u in range(10):
        e = u if u < 8 else None
        h2 = u - 8
        pp = u % 2
        guv = ar(C, AR_B, pp * 8192, 8192).rearrange(
            "p (g dc i) -> p g dc i", g=2, i=512)
        dnv = ar(C, AR_A, pp * 4096, 4096).rearrange(
            "p (ic h) -> p ic h", h=1024)
        if e is not None:
            src_g = W["egwT"][e].rearrange("(dc p) i -> p dc i", p=128)
            src_u = W["euwT"][e].rearrange("(dc p) i -> p dc i", p=128)
            src_d = W["edwT"][e].rearrange("(ic p) h -> p ic h", p=128)
            dep_g, dep_u, dep_d = (P[f"egwT{e}"], P[f"euwT{e}"],
                                   P[f"edwT{e}"])
        else:
            src_g = W["sgwT"].rearrange("(dc p) i -> p dc i", p=128)[
                :, :, h2 * 512:(h2 + 1) * 512]
            src_u = W["suwT"].rearrange("(dc p) i -> p dc i", p=128)[
                :, :, h2 * 512:(h2 + 1) * 512]
            src_d = W["sdwT"].rearrange("(ic p) h -> p ic h", p=128)[
                :, h2 * 4:(h2 + 1) * 4, :]
            dep_g, dep_u, dep_d = P["sgwT"], P["suwT"], P["sdwT"]
        t_lg = C.dma_sync(lambda s, guv=guv, src_g=src_g: s.dma_start(
            guv[:, 0], src_g), deps=dep_g + rgGU[pp].begin_write())
        t_lu = C.dma_sync(lambda s, guv=guv, src_u=src_u: s.dma_start(
            guv[:, 1], src_u), deps=dep_u + [t_lg])
        rgGU[pp].end_write([t_lg, t_lu])
        t_ld = C.dma_sync(lambda s, dnv=dnv, src_d=src_d: s.dma_start(
            dnv, src_d), deps=dep_d + rgDN[pp].begin_write())
        rgDN[pp].end_write(t_ld)
        hd = rgH.begin_write()
        h_toks = []
        for ic in range(4):
            for ts4 in range(4):
                bg = ts4 % 2
                bu_ = 2 + ts4 % 2
                bdg = C.pb_reg[bg].begin_write()
                bdu = C.pb_reg[bu_].begin_write()
                mmg = mmu = None
                for dc in range(8):
                    mmg = C.pe(lambda t, bg=bg, dc=dc, ic=ic, ts4=ts4,
                               guv=guv: t.matmul(
                                   C.pb[bg][:, :],
                                   guv[:, 0, dc, ic * 128:(ic + 1) * 128],
                                   xnT_tile(C, dc, ts4 * 512, 512),
                                   start=(dc == 0), stop=(dc == 7)),
                               deps=([t_lg] + list(bdg) +
                                     (xn_deps if u == 0 else []))
                               if dc == 0 else [])
                    mmu = C.pe(lambda t, bu_=bu_, dc=dc, ic=ic, ts4=ts4,
                               guv=guv: t.matmul(
                                   C.pb[bu_][:, :],
                                   guv[:, 1, dc, ic * 128:(ic + 1) * 128],
                                   xnT_tile(C, dc, ts4 * 512, 512),
                                   start=(dc == 0), stop=(dc == 7)),
                               deps=[t_lu] + (list(bdu) if dc == 0 else []))
                    all_mms += [mmg, mmu]
                rgGU[pp].add_reader([mmg, mmu])
                C.pb_reg[bg].end_write(mmg)
                C.pb_reg[bu_].end_write(mmu)
                sp = ts4 % 2
                t_sl = C.act(lambda s, bg=bg, sp=sp: s.activation(
                    sgf[sp][:, :], C.pb[bg][:, :], AF.Silu),
                    deps=[mmg] + regs_begin_write(sgregs[sp]))
                regs_end_write(sgregs[sp], t_sl)
                C.pb_reg[bg].add_reader(t_sl)
                t_h = C.dve(lambda v, bu_=bu_, sp=sp, ic=ic, ts4=ts4:
                            v.tensor_tensor(
                                ar(C, AR_C, ic * 2048 + ts4 * 512, 512),
                                C.pb[bu_][:, :], sgf[sp][:, :], ALU.mult),
                            deps=[t_sl, mmu] +
                            (hd if ic == 0 and ts4 == 0 else []))
                C.pb_reg[bu_].add_reader(t_h)
                regs_add_reader(sgregs[sp], t_h)
                h_toks.append(t_h)
        rgH.end_write(h_toks)
        for tt in range(NTT):
            for dmh in range(2):
                bi = 4 + (tt * 2 + dmh) % 2
                bankd = C.pb_reg[bi].begin_write()
                mml = None
                for ic in range(4):
                    mml = C.pe(lambda t, bi=bi, ic=ic, tt=tt, dmh=dmh,
                               dnv=dnv: t.matmul(
                                   C.pb[bi][:, :],
                                   ar(C, AR_C, ic * 2048 + tt * 128, 128),
                                   dnv[:, ic, dmh * 512:(dmh + 1) * 512],
                                   start=(ic == 0), stop=(ic == 3)),
                               deps=[t_ld, h_toks[4 * ic + tt // 4]] +
                               (list(bankd) if ic == 0 else []))
                    rgDN[pp].add_reader(mml)
                    rgH.add_reader(mml)
                    all_mms.append(mml)
                C.pb_reg[bi].end_write(mml)
                xd = C.xres_reg[tt].begin_write()
                sl = slice(tt * D + dmh * 512, tt * D + (dmh + 1) * 512)
                if e is not None:
                    gcol = C.l3_gates[:, tt * 8 + e:tt * 8 + e + 1]
                    ev = C.dve(lambda v, bi=bi, sl=sl, gcol=gcol:
                               v.scalar_tensor_tensor(
                                   C.x_res[:, sl], C.pb[bi][:, :], gcol,
                                   C.x_res[:, sl], ALU.mult, ALU.add),
                               deps=[mml, C.l3_gate_toks[tt]] + xd)
                else:
                    ev = C.dve(lambda v, bi=bi, sl=sl: v.tensor_add(
                        C.x_res[:, sl], C.x_res[:, sl], C.pb[bi][:, :]),
                        deps=[mml] + xd)
                C.pb_reg[bi].add_reader(ev)
                C.xres_reg[tt].end_write(ev)
    regA.end_write(all_mms[-1:])
    regB.end_write(all_mms[-1:])
    regC.end_write(all_mms[-1:])


def emit_layer3(C, W):
    parts = _os.environ.get("MK_L3_PARTS", "af")
    P = {}
    if "a" in parts:
        P.update(emit_l3_prep_attn(C, W))
    emit_ln(C)
    if "f" in parts:
        emit_l3_gating(C, W)
    if "a" in parts:
        emit_l3_attn(C, W, P)
    if "f" in parts:
        P.update(emit_l3_prep_ffn(C, W))
        emit_l3_gates_final(C, W)
        emit_l3_ffn(C, W, P)
    layer_reset_pools(C)



LAYER_DECL = {0: l0_declare, 1: l1_declare, 2: l2_declare,
              3: l3_declare}
LAYER_EMIT = {0: emit_layer0, 1: emit_layer1, 2: emit_layer2,
              3: emit_layer3}

LAYER_INPUT_NAMES = {
    0: ["L0_attn_phase", "L0_attn_Wq", "L0_attn_bq", "L0_attn_Wk",
        "L0_attn_bk", "L0_attn_Wv", "L0_attn_bv", "L0_attn_Wo", "L0_attn_bo",
        "L0_ffn_wi", "L0_ffn_bi", "L0_ffn_wo", "L0_ffn_bo", "L0_ffn_lora_A",
        "L0_ffn_lora_B", "L0_ffn_gate_w", "L0_ffn_gate_b"],
    1: ["L1_attn_W_qkv", "L1_attn_W_out", "L1_attn_b_out", "L1_ffn_k0",
        "L1_ffn_k1", "L1_ffn_k2", "L1_ffn_s0", "L1_ffn_s1", "L1_ffn_s2"],
    2: ["L2_attn_Wq", "L2_attn_Wk", "L2_attn_Wv", "L2_attn_Wp", "L2_attn_bp",
        "L2_attn_rel_table", "L2_ffn_w_gate", "L2_ffn_w_up", "L2_ffn_w_down"],
    3: ["L3_attn_W_in", "L3_attn_conv_w", "L3_attn_conv_b", "L3_attn_W_x",
        "L3_attn_W_dt", "L3_attn_b_dt", "L3_attn_A_log", "L3_attn_D_param",
        "L3_attn_W_out", "L3_ffn_gate_w", "L3_ffn_lb_bias",
        "L3_ffn_expert_gate_w", "L3_ffn_expert_up_w", "L3_ffn_expert_down_w",
        "L3_ffn_shared_gate_w", "L3_ffn_shared_up_w", "L3_ffn_shared_down_w"],
}

_PROG_CACHE = {}


def build_program(layers):
    key = tuple(layers)
    if key in _PROG_CACHE:
        return _PROG_CACHE[key]
    nc = bass.Bass("TRN2", target_bir_lowering=False)
    x_in = nc.dram_tensor("x_in", [T, D], F32, kind="ExternalInput").ap()
    x_out = nc.dram_tensor("x_out", [T, D], F32, kind="ExternalOutput").ap()
    with ExitStack() as stack:
        C = make_ctx(nc, stack)
        Ws = {l: LAYER_DECL[l](nc) for l in layers}
        emit_load_x(C, x_in)
        for l in layers:
            LAYER_EMIT[l](C, Ws[l])
        emit_store_x(C, x_out)
        C.prog.finalize(nc, stack)
    names = []
    for l in layers:
        names += LAYER_INPUT_NAMES[l]
    _PROG_CACHE[key] = (nc, names)
    return nc, names


LAUNCH_PLAN = [[0, 1, 2, 3]]


def _input_for(name, arr):
    a = np.asarray(arr)
    if a.dtype == np.bool_:
        a = a.astype(np.uint8)
    elif a.dtype != np.float32:
        a = a.astype(np.float32)
    if a.ndim == 0:
        a = a.reshape(1)
    return np.ascontiguousarray(a)


def kernel(**inputs):
    x = np.ascontiguousarray(np.asarray(inputs["L0_attn_x"], np.float32))
    cur = [x[c] for c in range(NCORES)]
    for layers in LAUNCH_PLAN:
        nc, names = build_program(layers)
        in_maps = []
        for c in range(NCORES):
            m = {"x_in": cur[c]}
            for n in names:
                m[n] = _input_for(n, inputs[n])
            in_maps.append(m)
        res = run_bass_kernel_spmd(nc, in_maps, core_ids=list(range(NCORES)))
        cur = [np.asarray(res.results[c]["x_out"]) for c in range(NCORES)]
    return np.stack(cur, axis=0)


# version 08
# speedup vs baseline: 1.1773x; 1.1773x over previous
import numpy as np
from contextlib import ExitStack
import concourse.bass as bass
import concourse.mybir as mybir
from concourse.bass_utils import run_bass_kernel_spmd

F32 = mybir.dt.float32
BF16 = mybir.dt.bfloat16
U8 = mybir.dt.uint8
I32 = mybir.dt.int32
AF = mybir.ActivationFunctionType
ALU = mybir.AluOpType
AX = mybir.AxisListType

T = 2048
D = 1024
NCORES = 8
NTT = T // 128
NDC = D // 128


QUEUES = ("pe", "dve", "act", "pool", "sync")
ENG_ATTR = {"pe": "tensor", "dve": "vector", "act": "scalar",
            "pool": "gpsimd", "sync": "sync"}


class _Op:
    __slots__ = ("fn", "deps", "dma", "used", "val", "relaxed", "idx")


class Tok:
    __slots__ = ("q", "op")

    def __init__(self, q, op):
        self.q = q
        self.op = op


def _flat(deps):
    out = []
    stack = list(deps) if isinstance(deps, (list, tuple)) else [deps]
    while stack:
        d = stack.pop()
        if d is None:
            continue
        if isinstance(d, (list, tuple)):
            stack.extend(d)
        else:
            out.append(d)
    return out


class Q:
    def __init__(self, name):
        self.name = name
        self.ops = []

    def emit(self, fn, deps=(), dma=False, relaxed=False):
        ds = _flat(deps)
        op = _Op()
        op.fn = fn
        op.deps = ds
        op.dma = dma
        op.relaxed = relaxed
        op.idx = len(self.ops)
        op.used = False
        op.val = None
        for d in ds:
            d.op.used = True
        self.ops.append(op)
        return Tok(self, op)


class Prog:
    def __init__(self):
        self.q = {n: Q(n) for n in QUEUES}

    def finalize(self, nc, stack):
        for name, q in self.q.items():
            ev, dv = 0, 0
            for op in q.ops:
                if not op.used:
                    continue
                if op.dma:
                    dv += 16
                    op.val = ("d", dv)
                else:
                    ev += 1
                    op.val = ("e", ev)
        sems = {}
        for name in QUEUES:
            sems[(name, "e")] = stack.enter_context(nc.semaphore(f"se_{name}"))
            sems[(name, "d")] = stack.enter_context(nc.semaphore(f"sd_{name}"))
        blk = stack.enter_context(nc.Block())

        def make_body(qname):
            q = self.q[qname]

            def run(eng):
                last = {}
                selfk = int(_os.environ.get("MK_SELFK", "0"))
                for op in q.ops:
                    for dep in op.deps:
                        if dep.op.val is None:
                            raise RuntimeError(
                                f"{qname}: dep on never-valued token "
                                f"(queue {dep.q.name})")
                        kind, val = dep.op.val
                        if dep.q.name == qname \
                                and not op.dma and not dep.op.dma \
                                and (op.relaxed or
                                     (selfk and op.idx - dep.op.idx > selfk)):
                            continue
                        key = (dep.q.name, kind)
                        if last.get(key, 0) >= val:
                            continue
                        eng.wait_ge(sems[key], val)
                        last[key] = val
                    inst = op.fn(eng)
                    if op.used:
                        if inst is None:
                            raise RuntimeError(f"{qname}: used op returned None")
                        kind, _ = op.val
                        inst.then_inc(sems[(qname, kind)], 16 if op.dma else 1)
            return run

        for name in QUEUES:
            getattr(blk, ENG_ATTR[name])(make_body(name))


class Region:

    def __init__(self, name=""):
        self.name = name
        self.w = []
        self.r = []

    def begin_write(self):
        deps = self.w + self.r
        self.w = []
        self.r = []
        return deps

    def end_write(self, toks):
        self.w.extend(_flat([toks]))

    def read_deps(self):
        return list(self.w)

    def add_reader(self, toks):
        self.r.extend(_flat([toks]))


def regs_begin_write(regs):
    deps = []
    for r in regs:
        deps += r.begin_write()
    return deps


def regs_end_write(regs, toks):
    for r in regs:
        r.end_write(toks)


def regs_read_deps(regs):
    deps = []
    for r in regs:
        deps += r.read_deps()
    return deps


def regs_add_reader(regs, toks):
    for r in regs:
        r.add_reader(toks)



class Ctx:
    pass


def make_ctx(nc, stack):
    C = Ctx()
    C.nc = nc
    C.stack = stack
    C.prog = Prog()
    C.pe = lambda fn, deps=(): C.prog.q["pe"].emit(fn, deps)
    C.dver = lambda fn, deps=(): C.prog.q["dve"].emit(fn, deps, relaxed=True)
    C.dve = lambda fn, deps=(): C.prog.q["dve"].emit(fn, deps)
    C.act = lambda fn, deps=(): C.prog.q["act"].emit(fn, deps)
    C.pool = lambda fn, deps=(): C.prog.q["pool"].emit(fn, deps)
    C.sync = lambda fn, deps=(): C.prog.q["sync"].emit(fn, deps)
    C.dma_sync = lambda fn, deps=(): C.prog.q["sync"].emit(fn, deps, dma=True)
    C.dma_act = lambda fn, deps=(): C.prog.q["act"].emit(fn, deps, dma=True)
    C.dma_pool = lambda fn, deps=(): C.prog.q["pool"].emit(fn, deps, dma=True)

    sb = lambda name, shape, dt: stack.enter_context(
        nc.sbuf_tensor(name, shape, dt))
    ps = lambda name, shape, dt: stack.enter_context(
        nc.psum_tensor(name, shape, dt))

    C.x_res = sb("x_res", [128, NTT * D], F32)
    C.xnT = sb("xnT", [128, NDC * T], BF16)
    C.arena = sb("arena", [128, 43008], BF16)
    C.wbuf = sb("wbuf", [128, 5120], BF16)
    C.scr = sb("scr", [128, 1024], F32)
    C.small = sb("small", [128, 512], F32)
    C.ident = sb("ident", [128, 128], BF16)
    C.identi = sb("identi", [128, 128], I32)
    C.onescol = sb("onescol", [128, 1], BF16)
    C.onesrow = sb("onesrow", [1, 128], BF16)
    C.onesrowf = sb("onesrowf", [1, 128], F32)
    C.rowp = sb("rowp", [1, 3712], BF16)
    C.rowpf = sb("rowpf", [1, 256], F32)

    C.pb = [ps(f"pb{i}", [128, 512], F32) for i in range(6)]
    C.tb = [ps(f"tb{i}", [128, 1024], BF16) for i in range(2)]
    C.pb_reg = [Region(f"pb{i}") for i in range(6)]
    C.tb_reg = [Region(f"tb{i}") for i in range(2)]

    C.xres_reg = [Region(f"xres{tt}") for tt in range(NTT)]
    C.xnT_reg = Region("xnT")
    C.ar_reg = {k: Region(k) for k in ("A", "B", "C", "D")}
    C.wslot = [Region(f"w{i}") for i in range(5)]
    C.scr_reg = Region("scr")
    C.rowp_reg = Region("rowp")

    C.identJ = sb("identJ", [128, 128], BF16)
    t_iota = C.pool(lambda g: g.iota(C.identi[:], [[1, 128]], base=0,
                                     channel_multiplier=-1))
    C.t_ident = C.dve(lambda v: v.tensor_single_scalar(
        C.ident[:], C.identi[:], 0, ALU.is_equal), deps=[t_iota])
    t_iota2 = C.pool(lambda g: g.iota(C.identi[:], [[1, 128]], base=-127,
                                      channel_multiplier=1), deps=[C.t_ident])
    C.t_identJ = C.dve(lambda v: v.tensor_single_scalar(
        C.identJ[:], C.identi[:], 0, ALU.is_equal), deps=[t_iota2])
    C.zmm = sb("zmm", [128, 128], BF16)
    C.t_zmm = C.pool(lambda g: g.memset(C.zmm[:, :], 0.0))
    C.t_ones = C.pool(lambda g: g.memset(C.onescol[:], 1.0))
    C.t_onesr = C.pool(lambda g: g.memset(C.onesrow[:], 1.0))
    C.t_onesrf = C.pool(lambda g: g.memset(C.onesrowf[:], 1.0))
    C.small_next = 0
    C.rowp_next = 0
    C.rowpf_next = 0
    C.qcols = C.small[:, 416:512]
    C.qreg = Region("qcols")
    return C


def mm_close(C, out_ap, ncols, deps):
    return C.pe(lambda t: t.matmul(
        out_ap, C.zmm[:, :], C.xnT[:, 0:ncols],
        start=False, stop=True), deps=_flat(deps) + [C.t_zmm])


AR_A = 0
AR_B = 16384
AR_C = 32768
AR_D = 40960


def ar(C, base, off, n):
    return C.arena[:, base + off: base + off + n]


def wb(C, off, n):
    return C.wbuf[:, off:off + n]


def wb_slots(C, off, n):
    return [C.wslot[i] for i in range(off // 1024, (off + n + 1023) // 1024)]


def smcol(C, n=1):
    i = C.small_next
    C.small_next += n
    assert C.small_next <= 416, "small pool overflow"
    return C.small[:, i:i + n]


def rowb_alloc(C, n):
    i = C.rowp_next
    C.rowp_next += n
    assert C.rowp_next <= 3712, "row pool overflow"
    return C.rowp[0:1, i:i + n]


def rowf_alloc(C, n):
    i = C.rowpf_next
    C.rowpf_next += n
    assert C.rowpf_next <= 256, "rowf pool overflow"
    return C.rowpf[0:1, i:i + n]


def layer_reset_pools(C):
    C.small_next = 0
    C.rowp_next = 0
    C.rowpf_next = 0



def tb_transpose_batch(C, src_fns, dst_fn, deps, tbi=0, src_rows=128,
                       src_cols=128):
    assert len(src_fns) <= 8
    tbt = C.tb[tbi]
    reg = C.tb_reg[tbi]
    wdeps = reg.begin_write() + _flat(deps) + [C.t_ident]
    mms = []
    for b, src_fn in enumerate(src_fns):
        mm = C.pe(lambda t, b=b, src_fn=src_fn: t.transpose(
            tbt[0:src_cols, b * src_rows:(b + 1) * src_rows], src_fn(),
            C.ident[0:src_rows, 0:src_rows]),
            deps=(wdeps if b == 0 else []))
        mms.append(mm)
    reg.end_write(mms)
    nb = len(src_fns)
    ev = C.dve(lambda v: v.tensor_copy(dst_fn(nb),
                                       tbt[0:src_cols, 0:nb * src_rows]),
               deps=[mms[-1]])
    reg.add_reader(ev)
    return ev



def dbg_dump(C, tag, deps):
    if _os.environ.get("MK_DBG_AT") != tag:
        return []
    nc = C.nc
    da = nc.dram_tensor("dbg_arena", [128, 43008], BF16, kind="ExternalOutput").ap()
    ds = nc.dram_tensor("dbg_small", [128, 512], F32, kind="ExternalOutput").ap()
    dc = nc.dram_tensor("dbg_scr", [128, 1024], F32, kind="ExternalOutput").ap()
    t1 = C.dma_sync(lambda s: s.dma_start(da, C.arena[:, :]), deps=deps)
    t2 = C.dma_sync(lambda s: s.dma_start(ds, C.small[:, :]), deps=deps)
    t3 = C.dma_sync(lambda s: s.dma_start(dc, C.scr[:, :]), deps=deps)
    for k in C.ar_reg:
        C.ar_reg[k].add_reader([t1])
    C.scr_reg.add_reader([t3])
    C.sync(lambda s: s.nop(), deps=[t1, t2, t3])
    return [t1, t2, t3]

def bcast_col(C, src_row_fn, n, dst_f32_ap, deps, pbi=5):
    assert n <= 512
    reg = C.pb_reg[pbi]
    wd = reg.begin_write()
    mm = C.pe(lambda t: t.matmul(C.pb[pbi][:, 0:n], C.onesrow[:],
                                 src_row_fn(), start=True, stop=True),
              deps=wd + _flat(deps) + [C.t_onesr])
    reg.end_write(mm)
    ev = C.dve(lambda v: v.tensor_copy(dst_f32_ap, C.pb[pbi][:, 0:n]),
               deps=[mm])
    reg.add_reader(ev)
    return ev


def load_cast(C, dst_ap, src_ap, deps=()):
    return C.dma_pool(lambda g: g.dma_start(dst_ap, src_ap), deps=deps)


def load_wT_rowblock(C, Wap, rb, K, dst_off, stage_off, deps=(), tbi=1):
    nkc = K // 128
    stg = wb(C, stage_off, K)
    sregs = wb_slots(C, stage_off, K)
    dregs = wb_slots(C, dst_off, nkc * 128)
    t_ld = load_cast(C, stg, Wap[rb * 128:(rb + 1) * 128, 0:K],
                     deps=_flat(deps) + regs_begin_write(sregs))
    regs_end_write(sregs, t_ld)
    wdeps = regs_begin_write(dregs)
    evs = []
    for k0 in range(0, nkc, 8):
        nb = min(8, nkc - k0)
        ev = tb_transpose_batch(
            C,
            [(lambda kc=kc: stg[:, kc * 128:(kc + 1) * 128])
             for kc in range(k0, k0 + nb)],
            lambda nb, k0=k0: wb(C, dst_off + k0 * 128, nb * 128),
            deps=[t_ld] + (wdeps if k0 == 0 else []), tbi=tbi)
        evs.append(ev)
    regs_add_reader(sregs, evs)
    regs_end_write(dregs, evs)
    return evs



def emit_load_x(C, x_in):
    xv = C.x_res[:, :].rearrange("p (tt d) -> p tt d", d=D)
    src = x_in.rearrange("(tt p) d -> p tt d", p=128)
    tok = C.dma_sync(lambda s: s.dma_start(xv, src))
    for tt in range(NTT):
        C.xres_reg[tt].begin_write()
        C.xres_reg[tt].end_write(tok)
    return tok


def emit_store_x(C, x_out):
    xv = C.x_res[:, :].rearrange("p (tt d) -> p tt d", d=D)
    dst = x_out.rearrange("(tt p) d -> p tt d", p=128)
    deps = []
    for tt in range(NTT):
        deps += C.xres_reg[tt].read_deps()
        deps += C.xres_reg[tt].r
    tok = C.dma_sync(lambda s: s.dma_start(dst, xv), deps=deps)
    C.sync(lambda s: s.nop(), deps=[tok])
    return tok


def emit_ln(C, tile_hook=None):
    xnT_wdeps = C.xnT_reg.begin_write()
    ev_toks = []
    regD = C.ar_reg["D"]
    dD = regD.begin_write()
    xbuf = [ar(C, AR_D, 0, 1024), ar(C, AR_D, 1024, 1024)]
    xbuf_tok = [dD, dD]
    xnT_v = C.xnT[:, :].rearrange("p (dc t) -> p dc t", t=T)
    scr_wd = C.scr_reg.begin_write()
    sq_toks = []
    for tt in range(NTT):
        xt = C.x_res[:, tt * D:(tt + 1) * D]
        xdeps = C.xres_reg[tt].read_deps()
        cols = smcol(C, 6)
        s_c, q_c, m_c, e_c, n_c, r_c = (cols[:, i:i + 1] for i in range(6))
        sqscr = C.scr[:, 0:1024]
        t_s = C.dve(lambda v, xt=xt, s_c=s_c: v.reduce_sum(s_c, xt, AX.X),
                    deps=xdeps)
        t_sq = C.act(lambda s, xt=xt, q_c=q_c, sqscr=sqscr: s.activation(
            sqscr, xt, AF.Square, accum_out=q_c),
            deps=xdeps + (scr_wd if tt == 0 else []))
        sq_toks.append(t_sq)
        t_m = C.dve(lambda v, s_c=s_c, m_c=m_c: v.tensor_single_scalar(
            m_c, s_c, 1.0 / D, ALU.mult), deps=[t_s])
        t_e = C.dve(lambda v, q_c=q_c, e_c=e_c: v.tensor_scalar(
            e_c, q_c, 1.0 / D, 1e-5, ALU.mult, ALU.add), deps=[t_sq])
        t_n = C.dve(lambda v, m_c=m_c, n_c=n_c: v.scalar_tensor_tensor(
            n_c, m_c, -1.0, m_c, ALU.mult, ALU.mult), deps=[t_m])
        t_v = C.dve(lambda v, n_c=n_c, e_c=e_c: v.tensor_add(
            n_c, n_c, e_c), deps=[t_n, t_e])
        t_r = C.dve(lambda v, n_c=n_c, r_c=r_c: v.reciprocal(r_c, n_c),
                    deps=[t_v])
        t_rs = C.act(lambda s, r_c=r_c: s.activation(r_c, r_c, AF.Sqrt),
                     deps=[t_r])
        xb = xbuf[tt % 2]
        t_xn = C.dve(lambda v, xb=xb, xt=xt, m_c=m_c, r_c=r_c: v.tensor_scalar(
            xb, xt, m_c, r_c, ALU.subtract, ALU.mult),
            deps=[t_m, t_rs] + xdeps + xbuf_tok[tt % 2])
        C.xres_reg[tt].add_reader(t_xn)
        ev = tb_transpose_batch(
            C,
            [(lambda xb=xb, dc=dc: xb[:, dc * 128:(dc + 1) * 128])
             for dc in range(NDC)],
            lambda nb, tt=tt: xnT_v[:, :, tt * 128:(tt + 1) * 128],
            deps=[t_xn] + (xnT_wdeps if tt == 0 else []), tbi=tt % 2)
        xbuf_tok[tt % 2] = [ev]
        ev_toks.append(ev)
        if tile_hook is not None:
            tile_hook(tt, xt, m_c, r_c, [t_m, t_rs] + xdeps)
    C.scr_reg.end_write(sq_toks)
    C.xnT_reg.end_write(ev_toks)
    regD.end_write(ev_toks)
    return ev_toks


def xnT_tile(C, dc, t0, n):
    return C.xnT[:, dc * T + t0: dc * T + t0 + n]



def l1_declare(nc):
    W = {}
    W["W_qkv"] = nc.dram_tensor("L1_attn_W_qkv", [3 * D, D], F32,
                                kind="ExternalInput").ap()
    W["W_out"] = nc.dram_tensor("L1_attn_W_out", [D, D], F32,
                                kind="ExternalInput").ap()
    W["b_out"] = nc.dram_tensor("L1_attn_b_out", [D], F32,
                                kind="ExternalInput").ap()
    W["k0"] = nc.dram_tensor("L1_ffn_k0", [2 * D, 4096], U8,
                             kind="ExternalInput").ap()
    W["k1"] = nc.dram_tensor("L1_ffn_k1", [4096, 4096], U8,
                             kind="ExternalInput").ap()
    W["k2"] = nc.dram_tensor("L1_ffn_k2", [4096, D], U8,
                             kind="ExternalInput").ap()
    W["s0"] = nc.dram_tensor("L1_ffn_s0", [1], F32, kind="ExternalInput").ap()
    W["s1"] = nc.dram_tensor("L1_ffn_s1", [1], F32, kind="ExternalInput").ap()
    W["s2"] = nc.dram_tensor("L1_ffn_s2", [1], F32, kind="ExternalInput").ap()
    W["k0w"] = nc.dram_tensor("l1_k0w", [D, 4096], BF16).ap()
    W["k1w"] = nc.dram_tensor("l1_k1w", [4096, 4096], BF16).ap()
    W["k2w"] = nc.dram_tensor("l1_k2w", [4096, D], BF16).ap()
    return W


def emit_l1_prep(C, W):
    u8a = wb(C, 0, 1024).bitcast(U8)
    u8b = wb(C, 1024, 1024).bitcast(U8)
    bfo = wb(C, 2048, 2048)
    rA = wb_slots(C, 0, 1024)
    rB = wb_slots(C, 1024, 1024)
    rO = wb_slots(C, 2048, 2048)
    out_toks = []
    for dc in range(8):
        for h in range(2):
            t_a = C.dma_sync(lambda s, dc=dc, h=h: s.dma_start(
                u8a, W["k0"][dc * 128:(dc + 1) * 128,
                             h * 2048:(h + 1) * 2048]),
                deps=regs_begin_write(rA))
            regs_end_write(rA, t_a)
            t_b = C.dma_sync(lambda s, dc=dc, h=h: s.dma_start(
                u8b, W["k0"][D + dc * 128: D + (dc + 1) * 128,
                             h * 2048:(h + 1) * 2048]),
                deps=regs_begin_write(rB))
            regs_end_write(rB, t_b)
            t_c = C.dve(lambda g: g.tensor_tensor(bfo, u8a, u8b,
                                                  ALU.subtract),
                        deps=[t_a, t_b] + regs_begin_write(rO))
            regs_end_write(rO, t_c)
            regs_add_reader(rA, t_c)
            regs_add_reader(rB, t_c)
            t_o = C.dma_sync(lambda s, dc=dc, h=h: s.dma_start(
                W["k0w"][dc * 128:(dc + 1) * 128, h * 2048:(h + 1) * 2048],
                bfo), deps=[t_c])
            regs_add_reader(rO, t_o)
            out_toks.append(t_o)
    for (src, dst, rows, cols) in ((W["k1"], W["k1w"], 4096, 4096),
                                   (W["k2"], W["k2w"], 4096, D)):
        nh = max(1, cols // 2048)
        w = cols // nh
        for dc in range(rows // 128):
            for h in range(nh):
                t_a = C.dma_sync(lambda s, src=src, dc=dc, h=h, w=w:
                                 s.dma_start(u8a[:, 0:w],
                                             src[dc * 128:(dc + 1) * 128,
                                                 h * w:(h + 1) * w]),
                                 deps=regs_begin_write(rA))
                regs_end_write(rA, t_a)
                t_c = C.dve(lambda g, w=w: g.tensor_scalar(
                    bfo[:, 0:w], u8a[:, 0:w], 2.0, -1.0, ALU.mult, ALU.add),
                    deps=[t_a] + regs_begin_write(rO))
                regs_end_write(rO, t_c)
                regs_add_reader(rA, t_c)
                t_o = C.dma_sync(lambda s, dst=dst, dc=dc, h=h, w=w:
                                 s.dma_start(dst[dc * 128:(dc + 1) * 128,
                                                 h * w:(h + 1) * w],
                                             bfo[:, 0:w]), deps=[t_c])
                regs_add_reader(rO, t_o)
                out_toks.append(t_o)
    return out_toks


def scalar_cols(C, rows_f32_dram, deps=()):
    n = len(rows_f32_dram)
    rf = rowf_alloc(C, n)
    ts = []
    for i, ap in enumerate(rows_f32_dram):
        ts.append(C.dma_sync(lambda s, i=i, ap=ap: s.dma_start(
            rf[:, i:i + 1], ap.unsqueeze(0)), deps=deps))
    rb = rowb_alloc(C, n)
    t_rb = C.dve(lambda v: v.tensor_copy(rb, rf), deps=ts)
    cols = smcol(C, n)
    t_bc = bcast_col(C, lambda: rb, n, cols, [t_rb])
    return cols, t_bc


def emit_l1_ffn(C, W, prep_toks):
    scol, t_sc0 = scalar_cols(C, [W["s0"], W["s1"], W["s2"]])
    scol2 = smcol(C, 3)
    t_sc = C.dve(lambda v: v.tensor_single_scalar(
        scol2, scol, 2.0, ALU.mult), deps=[t_sc0])
    s0c, s1c, s2c = (scol2[:, i:i + 1] for i in range(3))

    xn_deps = C.xnT_reg.read_deps()
    regA = C.ar_reg["A"]
    regB = C.ar_reg["B"]
    wbs = [(wb(C, 0, 1024), wb_slots(C, 0, 1024)),
           (wb(C, 1024, 1024), wb_slots(C, 1024, 1024))]
    add_toks = []
    for ts in range(4):
        t0 = ts * 512
        h1_toks = []
        wrA = regA.begin_write()
        for g in range(8):
            mm_last = [None] * 4
            for dc in range(8):
                wt, wregs = wbs[dc % 2]
                t_w = load_cast(
                    C, wt[:, 0:512],
                    W["k0w"][dc * 128:(dc + 1) * 128, g * 512:(g + 1) * 512],
                    deps=regs_begin_write(wregs) +
                    (prep_toks if ts == 0 and g == 0 and dc < 2 else []))
                regs_end_write(wregs, t_w)
                for f in range(4):
                    bankd = C.pb_reg[f].begin_write() if dc == 0 else ()
                    mm = C.pe(lambda t, wt=wt, f=f, dc=dc, t0=t0: t.matmul(
                        C.pb[f][:, :],
                        wt[:, f * 128:(f + 1) * 128],
                        xnT_tile(C, dc, t0, 512),
                        start=(dc == 0), stop=(dc == 7)),
                        deps=[t_w] + list(bankd) +
                        (xn_deps if dc == 0 and f == 0 else []))
                    regs_add_reader(wregs, mm)
                    mm_last[f] = mm
            for f in range(4):
                fc = g * 4 + f
                C.pb_reg[f].end_write(mm_last[f])
                ev = C.act(lambda s, f=f, fc=fc: s.activation(
                    ar(C, AR_A, fc * 512, 512), C.pb[f][:, :], AF.Relu,
                    scale=s0c),
                    deps=[mm_last[f], t_sc] + (wrA if g == 0 and f == 0 else []))
                C.pb_reg[f].add_reader(ev)
                h1_toks.append(ev)
        regA.end_write(h1_toks)
        h2_toks = []
        wrB = regB.begin_write()
        for g in range(8):
            mm_last = [None] * 4
            for kc in range(32):
                wt, wregs = wbs[kc % 2]
                t_w = load_cast(
                    C, wt[:, 0:512],
                    W["k1w"][kc * 128:(kc + 1) * 128, g * 512:(g + 1) * 512],
                    deps=regs_begin_write(wregs))
                regs_end_write(wregs, t_w)
                for f in range(4):
                    bankd = C.pb_reg[f].begin_write() if kc == 0 else ()
                    mm = C.pe(lambda t, wt=wt, f=f, kc=kc: t.matmul(
                        C.pb[f][:, :],
                        wt[:, f * 128:(f + 1) * 128],
                        ar(C, AR_A, kc * 512, 512),
                        start=(kc == 0), stop=(kc == 31)),
                        deps=[t_w] + list(bankd) +
                        (h1_toks if kc == 0 and f == 0 else []))
                    regs_add_reader(wregs, mm)
                    mm_last[f] = mm
            for f in range(4):
                fc = g * 4 + f
                C.pb_reg[f].end_write(mm_last[f])
                regA.add_reader(mm_last[f])
                ev = C.act(lambda s, f=f, fc=fc: s.activation(
                    ar(C, AR_B, fc * 512, 512), C.pb[f][:, :], AF.Relu,
                    scale=s1c),
                    deps=[mm_last[f], t_sc] + (wrB if g == 0 and f == 0 else []))
                C.pb_reg[f].add_reader(ev)
                h2_toks.append(ev)
        regB.end_write(h2_toks)
        for it in range(4):
            tt = ts * 4 + it
            mm_last = [None, None]
            for kc in range(32):
                wt, wregs = wbs[kc % 2]
                t_w = load_cast(
                    C, wt[:, 0:1024],
                    W["k2w"][kc * 128:(kc + 1) * 128, :],
                    deps=regs_begin_write(wregs))
                regs_end_write(wregs, t_w)
                for f in range(2):
                    bankd = C.pb_reg[f].begin_write() if kc == 0 else ()
                    mm = C.pe(lambda t, wt=wt, f=f, kc=kc, it=it: t.matmul(
                        C.pb[f][:, :],
                        ar(C, AR_B, kc * 512 + it * 128, 128),
                        wt[:, f * 512:(f + 1) * 512],
                        start=(kc == 0), stop=(kc == 31)),
                        deps=[t_w] + list(bankd) +
                        (h2_toks if kc == 0 and f == 0 else []))
                    regs_add_reader(wregs, mm)
                    mm_last[f] = mm
            for f in range(2):
                C.pb_reg[f].end_write(mm_last[f])
                regB.add_reader(mm_last[f])
                xd = C.xres_reg[tt].begin_write()
                sl = slice(tt * D + f * 512, tt * D + (f + 1) * 512)
                ev = C.dve(lambda v, f=f, sl=sl: v.scalar_tensor_tensor(
                    C.x_res[:, sl], C.pb[f][:, :], s2c, C.x_res[:, sl],
                    ALU.mult, ALU.add),
                    deps=[mm_last[f], t_sc] + xd)
                C.pb_reg[f].add_reader(ev)
                C.xres_reg[tt].end_write(ev)
                add_toks.append(ev)
    return add_toks


def emit_l1_attn(C, W):
    xn_deps = C.xnT_reg.read_deps()
    regA, regB, regC, regD = (C.ar_reg[k] for k in "ABCD")

    kv_evt = {"k": [], "v": []}
    for name, base_eb in (("k", 8), ("v", 16)):
        dst_base = AR_A if name == "k" else AR_B
        reg = regA if name == "k" else regB
        wr_dst = reg.begin_write()
        for g in range(2):
            tev = []
            for j in range(4):
                evs = load_wT_rowblock(C, W["W_qkv"], base_eb + g * 4 + j, D,
                                       j * 1024, 4096)
                tev += evs
            slab = wb(C, 0, 4096).rearrange("p (eb x) -> p eb x", x=1024)
            sregs = wb_slots(C, 0, 4096)
            for tt in range(NTT):
                bi = tt % 2
                bankd = C.pb_reg[bi].begin_write()
                mml = None
                for dc in range(NDC):
                    mm = C.pe(lambda t, bi=bi, dc=dc, tt=tt, slab=slab:
                              t.matmul(
                                  C.pb[bi][:, :],
                                  xnT_tile(C, dc, tt * 128, 128),
                                  slab[:, :, dc * 128:(dc + 1) * 128],
                                  start=(dc == 0), stop=(dc == 7)),
                              deps=(tev + list(bankd) + xn_deps)
                              if dc == 0 else [tev[-1]])
                    mml = mm
                C.pb_reg[bi].end_write(mml)
                regs_add_reader(sregs, mml)
                if name == "k":
                    ev = C.dve(lambda v, bi=bi, tt=tt, g=g, dst_base=dst_base:
                               v.tensor_scalar(
                                   ar(C, dst_base, tt * 1024 + g * 512, 512),
                                   C.pb[bi][:, :], 0.0, 1e-6, ALU.max, ALU.add),
                               deps=[mml] + (wr_dst if tt == 0 and g == 0 else []))
                else:
                    ev = C.dve(lambda v, bi=bi, tt=tt, g=g, dst_base=dst_base:
                               v.tensor_copy(
                                   ar(C, dst_base, tt * 1024 + g * 512, 512),
                                   C.pb[bi][:, :]),
                               deps=[mml] + (wr_dst if tt == 0 and g == 0 else []))
                C.pb_reg[bi].add_reader(ev)
                kv_evt[name].append(ev)
        reg.end_write(kv_evt[name])

    ksum_toks = []
    for g in range(2):
        bi = 2 + g
        bankd = C.pb_reg[bi].begin_write()
        mml = None
        for tt in range(NTT):
            mm = C.pe(lambda t, bi=bi, tt=tt, g=g: t.matmul(
                C.pb[bi][0:1, :], C.onescol[:, :],
                ar(C, AR_A, tt * 1024 + g * 512, 512),
                start=(tt == 0), stop=(tt == 15)),
                deps=(list(bankd) + kv_evt["k"] + [C.t_ones])
                if tt == 0 else ())
            mml = mm
        C.pb_reg[bi].end_write(mml)
        regA.add_reader(mml)
        ksum_toks.append(mml)
    ksrow = rowb_alloc(C, 1024)
    t_kr = []
    for g in range(2):
        tok = C.dve(lambda v, g=g: v.tensor_copy(
            ksrow[:, g * 512:(g + 1) * 512], C.pb[2 + g][0:1, :]),
            deps=[ksum_toks[g]])
        C.pb_reg[2 + g].add_reader(tok)
        t_kr.append(tok)
    scr_wd = C.scr_reg.begin_write()
    ksumB = C.scr[:, 0:1024]
    t_kb = []
    for g in range(2):
        t_kb.append(bcast_col(C, lambda g=g: ksrow[:, g * 512:(g + 1) * 512],
                              512, ksumB[:, g * 512:(g + 1) * 512],
                              [t_kr[g]] + (scr_wd if g == 0 else []), pbi=4))
    C.scr_reg.end_write(t_kb)

    regC_w = regC.begin_write()
    kv_toks = []
    for dkc in range(8):
        for dvh in range(2):
            bi = (dkc * 2 + dvh) % 2
            bankd = C.pb_reg[bi].begin_write()
            mml = None
            for tt in range(NTT):
                mm = C.pe(lambda t, bi=bi, dkc=dkc, dvh=dvh, tt=tt: t.matmul(
                    C.pb[bi][:, :],
                    ar(C, AR_A, tt * 1024 + dkc * 128, 128),
                    ar(C, AR_B, tt * 1024 + dvh * 512, 512),
                    start=(tt == 0), stop=(tt == 15)),
                    deps=(list(bankd) + kv_evt["k"] + kv_evt["v"])
                    if tt == 0 else ())
                mml = mm
            C.pb_reg[bi].end_write(mml)
            regA.add_reader(mml)
            regB.add_reader(mml)
            ev = C.dve(lambda v, bi=bi, dkc=dkc, dvh=dvh: v.tensor_copy(
                ar(C, AR_C, dkc * 1024 + dvh * 512, 512), C.pb[bi][:, :]),
                deps=[mml] + (regC_w if dkc == 0 and dvh == 0 else []))
            C.pb_reg[bi].add_reader(ev)
            kv_toks.append(ev)
    regC.end_write(kv_toks)

    wrA2 = regA.begin_write()
    q_toks = []
    for g in range(2):
        tev = []
        for j in range(4):
            evs = load_wT_rowblock(C, W["W_qkv"], g * 4 + j, D,
                                   j * 1024, 4096)
            tev += evs
        slab = wb(C, 0, 4096).rearrange("p (eb x) -> p eb x", x=1024)
        sregs = wb_slots(C, 0, 4096)
        for tt in range(NTT):
            bi = tt % 2
            bankd = C.pb_reg[bi].begin_write()
            mml = None
            for dc in range(NDC):
                mm = C.pe(lambda t, bi=bi, dc=dc, tt=tt, slab=slab: t.matmul(
                    C.pb[bi][:, :],
                    xnT_tile(C, dc, tt * 128, 128),
                    slab[:, :, dc * 128:(dc + 1) * 128],
                    start=(dc == 0), stop=(dc == 7)),
                    deps=(tev + list(bankd)) if dc == 0 else [tev[-1]])
                mml = mm
            C.pb_reg[bi].end_write(mml)
            regs_add_reader(sregs, mml)
            ev = C.dve(lambda v, bi=bi, tt=tt, g=g: v.tensor_scalar(
                ar(C, AR_A, tt * 1024 + g * 512, 512),
                C.pb[bi][:, :], 0.0, 1e-6, ALU.max, ALU.add),
                deps=[mml] + (wrA2 if tt == 0 and g == 0 else []))
            C.pb_reg[bi].add_reader(ev)
            q_toks.append(ev)
    regA.end_write(q_toks)

    kb_deps = C.scr_reg.read_deps()
    dummy = wb(C, 4096, 1024)
    dregs = wb_slots(C, 4096, 1024)
    dum_wd = regs_begin_write(dregs)
    qz_toks = []
    zc_all = smcol(C, 16)
    for tt in range(NTT):
        zc = zc_all[:, tt:tt + 1]
        t_z = C.dve(lambda v, tt=tt, zc=zc: v.scalar_tensor_tensor(
            dummy, ar(C, AR_A, tt * 1024, 1024), 1.0, ksumB,
            ALU.mult, ALU.mult, accum_out=zc),
            deps=q_toks + kb_deps + (dum_wd if tt == 0 else []))
        t_zr = C.dve(lambda v, zc=zc: v.tensor_scalar_add(zc, zc, 1e-6),
                     deps=[t_z])
        t_zi = C.dve(lambda v, zc=zc: v.reciprocal(zc, zc), deps=[t_zr])
        t_qz = C.dve(lambda v, tt=tt, zc=zc: v.tensor_single_scalar(
            ar(C, AR_A, tt * 1024, 1024), ar(C, AR_A, tt * 1024, 1024),
            zc, ALU.mult), deps=[t_zi])
        qz_toks.append(t_qz)
    regs_end_write(dregs, qz_toks)
    regA.end_write(qz_toks)
    C.scr_reg.add_reader(qz_toks)

    wrB2 = regB.begin_write()
    attn_toks = []
    for tsl in range(8):
        regD_w = regD.begin_write()
        tev = []
        for dh in range(2):
            ev = tb_transpose_batch(
                C,
                [(lambda it=it, dc=dc, tsl=tsl: ar(
                    C, AR_A, (tsl * 2 + it) * 1024 + dc * 128, 128))
                 for dc in range(dh * 4, dh * 4 + 4) for it in range(2)],
                lambda nb, dh=dh: ar(C, AR_D, dh * 1024, 1024),
                deps=qz_toks + (regD_w if dh == 0 else []), tbi=dh)
            tev.append(ev)
        regD.end_write(tev)
        regA.add_reader(tev)
        for dvc in range(8):
            bi = dvc % 2
            bankd = C.pb_reg[bi].begin_write()
            mml = None
            for dkc in range(8):
                mm = C.pe(lambda t, bi=bi, dvc=dvc, dkc=dkc: t.matmul(
                    C.pb[bi][:, 0:256],
                    ar(C, AR_C, dkc * 1024 + dvc * 128, 128),
                    ar(C, AR_D, dkc * 256, 256),
                    start=(dkc == 0), stop=(dkc == 7)),
                    deps=(list(bankd) + tev + kv_toks)
                    if dkc == 0 else [tev[-1]])
                mml = mm
            C.pb_reg[bi].end_write(mml)
            regD.add_reader(mml)
            regC.add_reader(mml)
            ev = C.dve(lambda v, bi=bi, dvc=dvc, tsl=tsl: v.tensor_copy(
                ar(C, AR_B, dvc * T + tsl * 256, 256), C.pb[bi][:, 0:256]),
                deps=[mml] + (wrB2 if tsl == 0 and dvc == 0 else []))
            C.pb_reg[bi].add_reader(ev)
            attn_toks.append(ev)
    regB.end_write(attn_toks)

    bo_row = rowb_alloc(C, 1024)
    t_bo = load_cast(C, bo_row, W["b_out"].unsqueeze(0))
    add_toks = []
    for fh in range(2):
        tev = []
        for j in range(4):
            evs = load_wT_rowblock(C, W["W_out"], fh * 4 + j, D,
                                   j * 1024, 4096)
            tev += evs
        slab = wb(C, 0, 4096).rearrange("p (fb x) -> p fb x", x=1024)
        sregs = wb_slots(C, 0, 4096)
        for tt in range(NTT):
            bi = tt % 2
            bankd = C.pb_reg[bi].begin_write()
            mml = None
            for ec in range(8):
                mm = C.pe(lambda t, bi=bi, ec=ec, tt=tt, slab=slab: t.matmul(
                    C.pb[bi][:, :],
                    ar(C, AR_B, ec * T + tt * 128, 128),
                    slab[:, :, ec * 128:(ec + 1) * 128],
                    start=(ec == 0), stop=False),
                    deps=(tev + list(bankd) + attn_toks)
                    if ec == 0 else [tev[-1]])
                mml = mm
            mmb = C.pe(lambda t, bi=bi, fh=fh: t.matmul(
                C.pb[bi][:, :], C.onesrow[:],
                bo_row[:, fh * 512:(fh + 1) * 512],
                start=False, stop=True), deps=[t_bo, C.t_onesr])
            C.pb_reg[bi].end_write(mmb)
            regs_add_reader(sregs, mmb)
            regB.add_reader(mmb)
            xd = C.xres_reg[tt].begin_write()
            sl = slice(tt * D + fh * 512, tt * D + (fh + 1) * 512)
            ev = C.dve(lambda v, bi=bi, sl=sl: v.tensor_add(
                C.x_res[:, sl], C.x_res[:, sl], C.pb[bi][:, :]),
                deps=[mmb] + xd)
            C.pb_reg[bi].add_reader(ev)
            C.xres_reg[tt].end_write(ev)
            add_toks.append(ev)
    return add_toks


import os as _os


def emit_layer1(C, W):
    parts = _os.environ.get("MK_L1_PARTS", "fa")
    prep = emit_l1_prep(C, W) if "f" in parts else []
    emit_ln(C)
    if "f" in parts:
        emit_l1_ffn(C, W, prep)
    if "a" in parts:
        emit_l1_attn(C, W)
    layer_reset_pools(C)




def l0_declare(nc):
    W = {}
    W["phase"] = nc.dram_tensor("L0_attn_phase", [4, D], F32, kind="ExternalInput").ap()
    for n in ("Wq", "Wk", "Wv"):
        W[n] = nc.dram_tensor(f"L0_attn_{n}", [D, 4096], F32, kind="ExternalInput").ap()
    for n in ("bq", "bk", "bv"):
        W[n] = nc.dram_tensor(f"L0_attn_{n}", [4096], F32, kind="ExternalInput").ap()
    W["Wo"] = nc.dram_tensor("L0_attn_Wo", [4096, D], F32, kind="ExternalInput").ap()
    W["bo"] = nc.dram_tensor("L0_attn_bo", [D], F32, kind="ExternalInput").ap()
    W["wi"] = nc.dram_tensor("L0_ffn_wi", [4096, D], F32, kind="ExternalInput").ap()
    W["bi"] = nc.dram_tensor("L0_ffn_bi", [4096], F32, kind="ExternalInput").ap()
    W["wo2"] = nc.dram_tensor("L0_ffn_wo", [D, 4096], F32, kind="ExternalInput").ap()
    W["bo2"] = nc.dram_tensor("L0_ffn_bo", [D], F32, kind="ExternalInput").ap()
    W["lora_A"] = nc.dram_tensor("L0_ffn_lora_A", [8, 16, D], F32, kind="ExternalInput").ap()
    W["lora_B"] = nc.dram_tensor("L0_ffn_lora_B", [8, 4096, 16], F32, kind="ExternalInput").ap()
    W["gate_w"] = nc.dram_tensor("L0_ffn_gate_w", [8, D], F32, kind="ExternalInput").ap()
    W["gate_b"] = nc.dram_tensor("L0_ffn_gate_b", [8], F32, kind="ExternalInput").ap()
    dbg = {}
    if _os.environ.get("MK_DBG"):
        dbg = dict(kind="ExternalOutput")
    W["wiT"] = nc.dram_tensor("l0_wiT", [D, 4096], BF16).ap()
    W["woT"] = nc.dram_tensor("l0_woT", [4096, D], BF16).ap()
    W["BT"] = nc.dram_tensor("l0_BT", [8, 16, 4096], BF16).ap()
    W["qscr"] = nc.dram_tensor("l0_qscr", [T, 4096], BF16, **dbg).ap()
    W["kscr"] = nc.dram_tensor("l0_kscr", [T, 4096], BF16, **dbg).ap()
    W["vscr"] = nc.dram_tensor("l0_vscr", [T, 4096], BF16, **dbg).ap()
    W["otscr"] = nc.dram_tensor("l0_otscr", [8, 32, 128, 256], BF16, **dbg).ap()
    W["rowscr"] = nc.dram_tensor("l0_rowscr", [16, 128], BF16).ap()
    return W


def emit_l0_prep(C, W):
    out_toks = []
    stg = wb(C, 0, 4096)
    sregs = wb_slots(C, 0, 4096)
    ost = wb(C, 4096, 1024)
    oregs = wb_slots(C, 4096, 1024)
    for (src, dst, R, K) in ((W["wi"], W["wiT"], 4096, D),
                             (W["wo2"], W["woT"], D, 4096)):
        dv = dst.rearrange("(kc p) r -> p kc r", p=128)
        for rb in range(R // 128):
            t_ld = load_cast(C, stg[:, 0:K], src[rb * 128:(rb + 1) * 128, 0:K],
                             deps=regs_begin_write(sregs))
            regs_end_write(sregs, t_ld)
            for k0 in range(0, K // 128, 8):
                nb = min(8, K // 128 - k0)
                ev = tb_transpose_batch(
                    C,
                    [(lambda kc=kc: stg[:, kc * 128:(kc + 1) * 128])
                     for kc in range(k0, k0 + nb)],
                    lambda nb2, k0=k0: ost[:, 0:nb2 * 128],
                    deps=[t_ld] + regs_begin_write(oregs), tbi=1)
                regs_end_write(oregs, ev)
                regs_add_reader(sregs, ev)
                t_o = C.dma_sync(lambda s, dv=dv, rb=rb, k0=k0, nb=nb:
                                 s.dma_start(
                                     dv[:, k0:k0 + nb,
                                        rb * 128:(rb + 1) * 128],
                                     ost[:, 0:nb * 128].rearrange(
                                         "p (kc r) -> p kc r", r=128)),
                                 deps=[ev])
                regs_add_reader(oregs, t_o)
                out_toks.append(t_o)
    for e in range(8):
        for b0 in range(0, 32, 8):
            t_ld = load_cast(C, stg[:, 0:128].rearrange(
                "p (b c) -> p b c", c=16),
                W["lora_B"][e].rearrange("(b p) c -> p b c", p=128)
                [:, b0:b0 + 8, :],
                deps=regs_begin_write(sregs))
            regs_end_write(sregs, t_ld)
            tbt = C.tb[1]
            reg = C.tb_reg[1]
            wdeps = reg.begin_write() + [t_ld, C.t_ident]
            mms = []
            for b in range(8):
                mm = C.pe(lambda t, b=b: t.transpose(
                    tbt[0:16, b * 128:(b + 1) * 128],
                    stg[:, b * 16:(b + 1) * 16], C.ident[:]),
                    deps=(wdeps if b == 0 else ()))
                mms.append(mm)
            reg.end_write(mms)
            ev = C.dve(lambda v: v.tensor_copy(ost[0:16, 0:1024],
                                               tbt[0:16, 0:1024]),
                       deps=[mms[-1]] + regs_begin_write(oregs))
            reg.add_reader(ev)
            regs_end_write(oregs, ev)
            regs_add_reader(sregs, ev)
            t_o = C.dma_sync(lambda s, e=e, b0=b0: s.dma_start(
                W["BT"][e, :, b0 * 128:(b0 + 8) * 128], ost[0:16, 0:1024]),
                deps=[ev])
            regs_add_reader(oregs, t_o)
            out_toks.append(t_o)
    return out_toks


def emit_l0_wpairs(C, W):
    PAIRS = [(h, g) for h in range(4) for g in range(h, 4)]
    regC, regD = C.ar_reg["C"], C.ar_reg["D"]
    wd = regC.begin_write() + regD.begin_write()
    toks = []
    rowstage = C.scr[0:1, 0:1024]
    scr_wd = C.scr_reg.begin_write()
    regA = C.ar_reg["A"]
    wdA = regA.begin_write()
    dB = ar(C, AR_A, 0, 2048).bitcast(F32)
    rB = ar(C, AR_A, 2048, 2048).bitcast(F32)
    M = float(1.5 * 2 ** 23)
    prev_row = scr_wd
    prev_ab = wdA
    rowstage2 = ar(C, AR_A, 4096, 2048).bitcast(F32)[0:1, :]
    for pi, (h, g) in enumerate(PAIRS):
        t_r1 = C.dma_sync(lambda s, h=h: s.dma_start(
            rowstage, W["phase"][h:h + 1, :]), deps=prev_row)
        t_r1b = C.dma_sync(lambda s, g=g: s.dma_start(
            rowstage2, W["phase"][g:g + 1, :]), deps=prev_row)
        t_r2 = C.dve(lambda v: v.tensor_sub(rowstage, rowstage, rowstage2),
                     deps=[t_r1, t_r1b])
        bts = []
        for half in range(2):
            pbi = 4 + half
            bd = C.pb_reg[pbi].begin_write()
            mm = C.pe(lambda t, half=half, pbi=pbi: t.matmul(
                C.pb[pbi][:, :], C.onesrowf[:],
                rowstage[:, half * 512:(half + 1) * 512],
                start=True, stop=True),
                deps=[t_r2, C.t_onesrf] + bd)
            C.pb_reg[pbi].end_write(mm)
            ev = C.dve(lambda v, half=half, pbi=pbi: v.tensor_scalar_add(
                dB[:, half * 512:(half + 1) * 512], C.pb[pbi][:, :],
                float(np.pi / 2)), deps=[mm] + prev_ab)
            C.pb_reg[pbi].add_reader(ev)
            bts.append(ev)
        prev_row = [t_r2] + bts
        t_u = C.dve(lambda v: v.tensor_scalar(
            rB, dB, float(1.0 / (2 * np.pi)), M, ALU.mult, ALU.add),
            deps=bts)
        t_v = C.dve(lambda v: v.tensor_scalar_add(rB, rB, -M), deps=[t_u])
        t_rr = C.dve(lambda v: v.scalar_tensor_tensor(
            dB, rB, float(-2 * np.pi), dB, ALU.mult, ALU.add),
            deps=[t_v])
        if pi < 8:
            dst = ar(C, AR_C, pi * 1024, 1024)
        else:
            dst = ar(C, AR_D, (pi - 8) * 1024, 1024)
        t_w = C.act(lambda s, dst=dst: s.activation(dst, dB, AF.Sin),
                    deps=[t_rr] + (wd if pi == 0 else []))
        prev_ab = [t_w]
        toks.append(t_w)
    regC.end_write(toks)
    regD.end_write(toks)
    regA.end_write(toks)
    C.scr_reg.end_write(toks)

    dbg_dump(C, "l0wb", toks)

    def wB(h, g):
        hh, gg = (h, g) if h <= g else (g, h)
        pi = PAIRS.index((hh, gg))
        if pi < 8:
            return ar(C, AR_C, pi * 1024, 1024)
        return ar(C, AR_D, (pi - 8) * 1024, 1024)
    return wB, toks


def emit_l0_proj_spill(C, W, wname, bname, scrname):
    xn_deps = C.xnT_reg.read_deps()
    wt8 = wb(C, 0, 4096)
    wregs = wb_slots(C, 0, 4096)
    stg = [wb(C, 4096, 512), wb(C, 4608, 512)]
    stregs = [wb_slots(C, 4096, 512), wb_slots(C, 4608, 512)]
    brow = C.rowp[0:1, 0:512]
    brreg = C.rowp_reg
    toks = []
    dscr = W[scrname].rearrange("(tt p) f -> p tt f", p=128)
    for fg in range(8):
        t_ws = []
        wdeps = regs_begin_write(wregs)
        for dc in range(8):
            t_w = load_cast(C, wt8[:, dc * 512:(dc + 1) * 512],
                            W[wname][dc * 128:(dc + 1) * 128,
                                     fg * 512:(fg + 1) * 512],
                            deps=wdeps if dc == 0 else [])
            t_ws.append(t_w)
        regs_end_write(wregs, t_ws)
        t_br = load_cast(C, brow,
                         W[bname].unsqueeze(0)[:, fg * 512:(fg + 1) * 512],
                         deps=brreg.begin_write())
        brreg.end_write(t_br)
        for tt in range(NTT):
            bi = tt % 4
            bankd = C.pb_reg[bi].begin_write()
            mml = None
            for dc in range(8):
                mm = C.pe(lambda t, bi=bi, dc=dc, tt=tt: t.matmul(
                    C.pb[bi][:, :], xnT_tile(C, dc, tt * 128, 128),
                    wt8[:, dc * 512:(dc + 1) * 512],
                    start=(dc == 0), stop=False),
                    deps=([t_ws[-1]] + list(bankd) +
                          (xn_deps if tt == 0 and dc == 0 else []))
                    if dc == 0 else [t_ws[-1]])
                mml = mm
            mmb = C.pe(lambda t, bi=bi: t.matmul(
                C.pb[bi][:, :], C.onesrow[:], brow, start=False, stop=True),
                deps=[t_br, C.t_onesr])
            brreg.add_reader(mmb)
            regs_add_reader(wregs, mmb)
            C.pb_reg[bi].end_write(mmb)
            st = stg[tt % 2]
            ev = C.dve(lambda v, bi=bi, st=st: v.tensor_copy(
                st, C.pb[bi][:, :]),
                deps=[mmb] + regs_begin_write(stregs[tt % 2]))
            C.pb_reg[bi].add_reader(ev)
            t_o = C.dma_sync(lambda s, st=st, tt=tt, fg=fg: s.dma_start(
                dscr[:, tt, fg * 512:(fg + 1) * 512], st), deps=[ev])
            regs_end_write(stregs[tt % 2], [ev, t_o])
            toks.append(t_o)
    return toks


def emit_l0_attn_mix(C, W, wBfn, wB_toks, qkv_toks):
    regA, regB = C.ar_reg["A"], C.ar_reg["B"]
    prod = wb(C, 0, 1024)
    dum = wb(C, 1024, 1024)
    pregs = wb_slots(C, 0, 2048)
    ost = wb(C, 4096, 1024)
    oregs = wb_slots(C, 4096, 1024)
    prod2 = wb(C, 2048, 1024)
    qv = W["qscr"].rearrange("(tt p) f -> p tt f", p=128)
    kv_ = W["kscr"].rearrange("(tt p) f -> p tt f", p=128)
    vv = W["vscr"].rearrange("(tt p) f -> p tt f", p=128)
    prev_mix = []

    attn_cols = smcol(C, 32)
    sm_t = smcol(C, 8)
    ot_toks_all = []
    for sp in range(8):
        wrA = regA.begin_write()
        wrB = regB.begin_write()
        t_q = C.dma_sync(lambda s, sp=sp: s.dma_start(
            ar(C, AR_A, 0, 8192).rearrange("p (tt f) -> p tt f", f=4096),
            qv[:, sp * 2:(sp + 1) * 2, :]), deps=wrA + qkv_toks)
        t_k = C.dma_sync(lambda s, sp=sp: s.dma_start(
            ar(C, AR_B, 0, 8192).rearrange("p (tt f) -> p tt f", f=4096),
            kv_[:, sp * 2:(sp + 1) * 2, :]), deps=wrB + qkv_toks)
        regA.end_write(t_q)
        regB.end_write(t_k)
        sc_toks = []
        pd = regs_begin_write(pregs)
        for it in range(2):
            for h in range(4):
                for g in range(4):
                    ci = it * 16 + h * 4 + g
                    pbuf = prod if ci % 2 == 0 else prod2
                    war = sc_toks[-2:-1] if len(sc_toks) >= 2 else []
                    t_p = C.pool(lambda gg, it=it, h=h, g=g, pbuf=pbuf:
                                 gg.tensor_mul(
                                     pbuf,
                                     ar(C, AR_A, it * 4096 + h * 1024, 1024),
                                     ar(C, AR_B, it * 4096 + g * 1024, 1024)),
                                 deps=[t_q, t_k] + war +
                                 (pd if it == 0 and h == 0 and g == 0
                                  else []))
                    t_s = C.dve(lambda v, ci=ci, h=h, g=g, pbuf=pbuf:
                                v.scalar_tensor_tensor(
                                    dum, pbuf, 1.0, wBfn(h, g), ALU.mult,
                                    ALU.mult,
                                    accum_out=attn_cols[:, ci:ci + 1]),
                                deps=[t_p] + wB_toks + prev_mix)
                    sc_toks.append(t_s)
        regs_end_write(pregs, sc_toks)
        t_dmp = dbg_dump(C, "l0sc0", sc_toks) if sp == 0 else []
        at_toks = []
        for it in range(2):
            blk = attn_cols[:, it * 16:(it + 1) * 16]
            t_e = C.act(lambda s, blk=blk: s.activation(
                blk, blk, AF.Exp, scale=float(1.0 / 32.0)),
                deps=sc_toks + t_dmp)
            t_r = C.dve(lambda v, blk=blk, it=it: v.tensor_reduce(
                sm_t[:, it * 4:(it + 1) * 4],
                blk.rearrange("p (h g) -> p h g", g=4), AX.X, ALU.add),
                deps=[t_e])
            t_i = C.dve(lambda v, it=it: v.reciprocal(
                sm_t[:, it * 4:(it + 1) * 4], sm_t[:, it * 4:(it + 1) * 4]),
                deps=[t_r])
            t_n = C.dve(lambda v, blk=blk, it=it: v.tensor_mul(
                blk.rearrange("p (h g) -> p h g", g=4),
                blk.rearrange("p (h g) -> p h g", g=4),
                sm_t[:, it * 4:(it + 1) * 4].unsqueeze(2).to_broadcast(
                    (128, 4, 4))), deps=[t_i])
            at_toks.append(t_n)
        if sp == 0:
            dbg_dump(C, "l0sm0", at_toks + sc_toks)
        wrA2 = regA.begin_write()
        t_v = C.dma_sync(lambda s, sp=sp: s.dma_start(
            ar(C, AR_A, 8192, 8192).rearrange("p (tt f) -> p tt f", f=4096),
            vv[:, sp * 2:(sp + 1) * 2, :]), deps=wrA2 + qkv_toks)
        regA.end_write(t_v)
        wrB2 = regB.begin_write()
        t_z = C.pool(lambda g: g.memset(ar(C, AR_B, 8192, 8192), 0.0),
                     deps=wrB2)
        mix_toks = [t_z]
        for it in range(2):
            for h in range(4):
                for g in range(4):
                    ci = it * 16 + h * 4 + g
                    t_m = C.dve(lambda v, it=it, h=h, g=g, ci=ci:
                                v.scalar_tensor_tensor(
                                    ar(C, AR_B, 8192 + it * 4096 + h * 1024, 1024),
                                    ar(C, AR_A, 8192 + it * 4096 + g * 1024, 1024),
                                    attn_cols[:, ci:ci + 1],
                                    ar(C, AR_B, 8192 + it * 4096 + h * 1024, 1024),
                                    ALU.mult, ALU.add),
                                deps=[t_v, at_toks[it]] + mix_toks[-1:])
                    mix_toks.append(t_m)
        regB.end_write(mix_toks)
        prev_mix = mix_toks[-1:]
        if sp == 0:
            dbg_dump(C, "l0mix0", mix_toks)
        ot_evs = []
        first = True
        for it in range(2):
            for h in range(4):
                extra = (regB.r + regB.w) if first else []
                first = False
                tbt = C.tb[0]
                treg = C.tb_reg[0]
                wdeps = treg.begin_write() + mix_toks[-1:] + extra + [C.t_ident]
                mms = []
                for dc in range(8):
                    mm = C.pe(lambda t, it=it, h=h, dc=dc: t.transpose(
                        tbt[:, dc * 128:(dc + 1) * 128],
                        C.arena[:, AR_B + 8192 + it * 4096 + h * 1024
                                + dc * 128:
                                AR_B + 8192 + it * 4096 + h * 1024
                                + dc * 128 + 128],
                        C.ident[:]), deps=(wdeps if dc == 0 else []))
                    mms.append(mm)
                treg.end_write(mms)
                ev = C.dve(lambda v, it=it, h=h: v.tensor_copy(
                    _l0_otdst(C, it, h),
                    tbt[:, 0:1024].rearrange("p (dc k jj) -> p dc k jj",
                                             k=32, jj=4)),
                    deps=[mms[-1]])
                treg.add_reader(ev)
                ot_evs.append(ev)
        t_sp = C.dma_sync(lambda s, sp=sp: s.dma_start(
            W["otscr"][sp].transpose([1, 0, 2]),
            ar(C, AR_B, 0, 8192).rearrange(
                "p (cc i) -> p cc i", i=256)), deps=ot_evs)
        ot_toks_all.append(t_sp)
        regB.w = list(ot_evs)
        regB.r = [t_sp]
    return ot_toks_all


def _l0_otdst(C, it, h):
    full = C.arena[:, AR_B:AR_B + 8192].rearrange(
        "p (jj dc i) -> p jj dc i", jj=4, dc=8)
    v = full[:, :, :, h * 64 + it * 32: h * 64 + it * 32 + 32]
    return v.transpose([0, 2, 3, 1])


def emit_l0_attn_out(C, W, ot_toks):
    otile = [wb(C, 0, 768), wb(C, 768, 768)]
    otregs = [Region("l0ot0"), Region("l0ot1")]
    wtile = [wb(C, 2048 + i * 512, 512) for i in range(4)]
    wtregs = [wb_slots(C, 2048 + i * 512, 512) for i in range(4)]
    brow = C.rowp[0:1, 512:1536]
    t_bo = load_cast(C, brow, W["bo"].unsqueeze(0),
                     deps=C.rowp_reg.begin_write())
    C.rowp_reg.end_write(t_bo)
    otv = W["otscr"].rearrange("sp cc p (hh i) -> p hh sp cc i", i=128)
    add_toks = []
    groups = [(0, 6), (6, 2), (8, 6), (14, 2)]
    for fh in range(2):
        for g0, gn in groups:
            hh = g0 // 8
            sp0 = g0 % 8
            mml = [None] * gn
            t_prev = None
            for cc in range(32):
                ot = otile[cc % 2]
                t_ot = C.dma_sync(lambda s, cc=cc, hh=hh, sp0=sp0, gn=gn,
                                  ot=ot: s.dma_start(
                                      ot[:, 0:gn * 128].rearrange(
                                          "p (tt i) -> p tt i", i=128),
                                      otv[:, hh, sp0:sp0 + gn, cc, :]),
                                  deps=otregs[cc % 2].begin_write() +
                                  (ot_toks if cc < 2 else []))
                otregs[cc % 2].end_write(t_ot)
                wt = wtile[cc % 4]
                t_w = load_cast(C, wt, W["Wo"][cc * 128:(cc + 1) * 128,
                                               fh * 512:(fh + 1) * 512],
                                deps=regs_begin_write(wtregs[cc % 4]))
                regs_end_write(wtregs[cc % 4], t_w)
                mms = []
                for ii in range(gn):
                    bankd = C.pb_reg[ii].begin_write() if cc == 0 else ()
                    mm = C.pe(lambda t, ot=ot, wt=wt, ii=ii, cc=cc: t.matmul(
                        C.pb[ii][:, :], ot[:, ii * 128:(ii + 1) * 128],
                        wt, start=(cc == 0), stop=False),
                        deps=[t_ot, t_w] + list(bankd))
                    regs_add_reader(wtregs[cc % 4], mm)
                    mms.append(mm)
                    mml[ii] = mm
                otregs[cc % 2].add_reader(mms[-1])
            for ii in range(gn):
                mmb = C.pe(lambda t, ii=ii, fh=fh: t.matmul(
                    C.pb[ii][:, :], C.onesrow[:],
                    brow[:, fh * 512:(fh + 1) * 512],
                    start=False, stop=True), deps=[t_bo, C.t_onesr])
                C.pb_reg[ii].end_write(mmb)
                tt2 = g0 + ii
                sp, hh2 = tt2 % 8, tt2 // 8
                evs = []
                for s2 in range(2):
                    h = hh2 * 2 + s2
                    row = h * 512 + sp * 64
                    tt = row // 128
                    off = row % 128
                    xd = C.xres_reg[tt].begin_write()
                    sl = slice(tt * D + fh * 512, tt * D + (fh + 1) * 512)
                    ev = C.dve(lambda v, sl=sl, off=off, s2=s2, ii=ii:
                               v.tensor_add(
                                   C.x_res[off:off + 64, sl],
                                   C.x_res[off:off + 64, sl],
                                   C.pb[ii][s2 * 64:(s2 + 1) * 64, :]),
                               deps=[mmb] + xd)
                    C.xres_reg[tt].end_write(ev)
                    evs.append(ev)
                C.pb_reg[ii].add_reader(evs)
                add_toks += evs
    return add_toks



def emit_l0_gating(C, W):
    xn_deps = C.xnT_reg.read_deps()
    regC, regD = C.ar_reg["C"], C.ar_reg["D"]
    wrC = regC.begin_write()
    wrD = regD.begin_write()
    stg = wb(C, 0, 1024)
    sregs = wb_slots(C, 0, 1024)
    t_a = load_cast(C, stg, W["lora_A"].rearrange("e r d -> (e r) d"),
                    deps=regs_begin_write(sregs))
    regs_end_write(sregs, t_a)
    slab_evs = []
    dslab = ar(C, AR_D, 0, 1088).rearrange("p (dc x) -> p dc x", x=136)
    for k0 in range(8):
        ev = tb_transpose_batch(
            C, [(lambda dc=dc: stg[:, dc * 128:(dc + 1) * 128])
                for dc in range(8)][k0:k0 + 1] if False else
            [(lambda k0=k0: stg[:, k0 * 128:(k0 + 1) * 128])],
            lambda nb, k0=k0: dslab[:, k0, 0:128],
            deps=[t_a] + (wrD if k0 == 0 else []), tbi=1)
        slab_evs.append(ev)
    regs_add_reader(sregs, slab_evs)
    t_g = load_cast(C, stg[0:8, 0:1024], W["gate_w"],
                    deps=regs_begin_write(sregs) + slab_evs)
    regs_end_write(sregs, t_g)
    for dc in range(8):
        ev = tb_transpose_batch(
            C, [(lambda dc=dc: stg[0:8, dc * 128:(dc + 1) * 128])],
            lambda nb, dc=dc: dslab[:, dc, 128:136],
            deps=[t_g], tbi=1, src_rows=8)
        slab_evs.append(ev)
    regs_add_reader(sregs, slab_evs[-8:])
    regD.end_write(slab_evs)
    gbrow = C.rowp[0:1, 3584:3592]
    t_gb = load_cast(C, gbrow, W["gate_b"].unsqueeze(0),
                     deps=C.rowp_reg.begin_write())
    C.rowp_reg.end_write(t_gb)
    C.l0_gates = smcol(C, 128)
    C.l0_val2 = smcol(C, 16)
    tmp = smcol(C, 4)
    m1c, m2c, s1c, s2c = (tmp[:, i:i + 1] for i in range(4))
    zwr = wrC
    z_toks = []
    g_toks = []
    for tt in range(NTT):
        bi = tt % 2
        bankd = C.pb_reg[bi].begin_write()
        mml = None
        for dc in range(8):
            mm = C.pe(lambda t, bi=bi, dc=dc, tt=tt: t.matmul(
                C.pb[bi][:, 0:136], xnT_tile(C, dc, tt * 128, 128),
                dslab[:, dc, :], start=(dc == 0), stop=False),
                deps=(slab_evs + list(bankd) + xn_deps) if dc == 0
                else [slab_evs[-1]])
            mml = mm
        mmb = C.pe(lambda t, bi=bi: t.matmul(
            C.pb[bi][:, 128:136], C.onesrow[:], gbrow,
            start=False, stop=True), deps=[t_gb, C.t_onesr])
        C.pb_reg[bi].end_write(mmb)
        regD.add_reader(mmb)
        t_z = C.dve(lambda v, bi=bi, tt=tt: v.tensor_copy(
            ar(C, AR_C, tt * 128, 128), C.pb[bi][:, 0:128]),
            deps=[mmb] + (zwr if tt == 0 else []))
        z_toks.append(t_z)
        lg = C.pb[bi][:, 128:136]
        gcols = C.l0_gates[:, tt * 8:(tt + 1) * 8]
        t_m = C.dve(lambda v, lg=lg: v.reduce_max(m1c, lg, AX.X),
                    deps=[mmb])
        t_sub = C.dve(lambda v, lg=lg, gcols=gcols: v.tensor_scalar(
            gcols, lg, m1c, None, ALU.subtract), deps=[t_m])
        t_e2 = C.act(lambda s, gcols=gcols: s.activation(
            gcols, gcols, AF.Exp), deps=[t_sub])
        t_s = C.dve(lambda v, gcols=gcols: v.reduce_sum(s1c, gcols, AX.X),
                    deps=[t_e2])
        t_i = C.dve(lambda v: v.reciprocal(s1c, s1c), deps=[t_s])
        t_p = C.dve(lambda v, gcols=gcols: v.tensor_scalar(
            gcols, gcols, s1c, None, ALU.mult), deps=[t_i])
        C.pb_reg[bi].add_reader([t_z, t_sub])
        t_m1 = C.dve(lambda v, gcols=gcols: v.reduce_max(m1c, gcols, AX.X),
                     deps=[t_p])
        t_k = C.dve(lambda v, gcols=gcols: v.tensor_scalar(
            C.scr[:, 0:8], gcols, m1c, -1e30, ALU.is_ge, ALU.mult),
            deps=[t_m1] + C.scr_reg.begin_write())
        t_k2 = C.dve(lambda v, gcols=gcols: v.tensor_add(
            C.scr[:, 0:8], C.scr[:, 0:8], gcols), deps=[t_k])
        t_m2 = C.dve(lambda v: v.reduce_max(m2c, C.scr[:, 0:8], AX.X),
                     deps=[t_k2])
        C.scr_reg.end_write([t_k, t_k2])
        C.scr_reg.add_reader(t_m2)
        t_v2 = C.dve(lambda v, tt=tt: v.tensor_tensor(
            C.l0_val2[:, tt:tt + 1], m1c, m2c, ALU.add), deps=[t_m2, t_m1])
        t_gt = C.dve(lambda v, gcols=gcols, tt=tt: v.tensor_scalar(
            gcols, gcols, m2c, C.l0_val2[:, tt:tt + 1], ALU.is_ge, ALU.mult),
            deps=[t_v2])
        g_toks.append(t_gt)
    regC.end_write(z_toks)
    C.l0_gate_toks = g_toks
    C.l0_z_toks = z_toks
    dbg_dump(C, "l0gate", z_toks + g_toks)
    v2b = wb(C, 512, 16)
    t_c = C.dve(lambda v: v.tensor_copy(v2b, C.l0_val2),
                deps=g_toks + regs_begin_write(sregs))
    ev = tb_transpose_batch(
        C, [lambda: v2b], lambda nb: wb(C, 0, 128)[0:16, :],
        deps=[t_c], tbi=1, src_rows=128, src_cols=16)
    regs_end_write(sregs, [t_c, ev])
    t_o = C.dma_sync(lambda s: s.dma_start(
        W["rowscr"][:, :], wb(C, 0, 128)[0:16, :]), deps=[ev])
    v2row = C.rowp[0:1, 0:2048]
    t_v2r = C.dma_sync(lambda s: s.dma_start(
        v2row, W["rowscr"].rearrange("a b -> (a b)").unsqueeze(0)),
        deps=[t_o] + C.rowp_reg.begin_write())
    C.rowp_reg.end_write(t_v2r)
    C.l0_v2row = v2row
    C.l0_v2row_tok = t_v2r
    regs_add_reader(sregs, t_v2r)
    return z_toks


def emit_l0_ffn(C, W):
    xn_deps = C.xnT_reg.read_deps()
    regA, regB = C.ar_reg["A"], C.ar_reg["B"]
    wt8 = [wb(C, 0, 512), wb(C, 512, 512)]
    wtregs = [wb_slots(C, 0, 512), wb_slots(C, 512, 512)]
    bet = wb(C, 1024, 4096)
    betregs = wb_slots(C, 1024, 4096)
    ztp = ar(C, AR_D, 0, 1024)
    regD = C.ar_reg["D"]
    birow = C.rowp[0:1, 2048:2560]
    bireg = Region("l0bi")
    wiv = W["wiT"]
    wov = W["woT"]
    add_toks = []
    for sp in range(4):
        t0 = sp * 512
        wrA = regA.begin_write()
        base_toks = []
        for fg in range(8):
            t_bi = load_cast(C, birow,
                             W["bi"].unsqueeze(0)[:, fg * 512:(fg + 1) * 512],
                             deps=bireg.begin_write())
            bireg.end_write(t_bi)
            mml = [None] * 4
            for dc in range(8):
                wt = wt8[dc % 2]
                t_w = C.dma_sync(lambda s, wt=wt, dc=dc, fg=fg: s.dma_start(
                    wt, wiv[dc * 128:(dc + 1) * 128,
                            fg * 512:(fg + 1) * 512]),
                    deps=regs_begin_write(wtregs[dc % 2]))
                regs_end_write(wtregs[dc % 2], t_w)
                for tis in range(4):
                    bankd = C.pb_reg[tis].begin_write() if dc == 0 else ()
                    mm = C.pe(lambda t, wt=wt, tis=tis, dc=dc, t0=t0:
                              t.matmul(C.pb[tis][:, :],
                                       xnT_tile(C, dc, t0 + tis * 128, 128),
                                       wt, start=(dc == 0), stop=False),
                              deps=[t_w] + list(bankd) +
                              (xn_deps if dc == 0 and tis == 0 else []))
                    regs_add_reader(wtregs[dc % 2], mm)
                    mml[tis] = mm
            for tis in range(4):
                mmb = C.pe(lambda t, tis=tis: t.matmul(
                    C.pb[tis][:, :], C.onesrow[:], birow,
                    start=False, stop=True), deps=[t_bi, C.t_onesr])
                bireg.add_reader(mmb)
                C.pb_reg[tis].end_write(mmb)
                ev = C.dve(lambda v, tis=tis, fg=fg: v.tensor_copy(
                    C.arena[:, AR_A + tis * 4096 + fg * 512:
                            AR_A + tis * 4096 + fg * 512 + 512],
                    C.pb[tis][:, :]),
                    deps=[mmb] + (wrA if fg == 0 and tis == 0 else []))
                C.pb_reg[tis].add_reader(ev)
                base_toks.append(ev)
        regA.end_write(base_toks)
        if sp == 0:
            dbg_dump(C, "l0base0", base_toks)
        wrB = regB.begin_write()
        t_z0 = C.pool(lambda g: g.memset(ar(C, AR_B, 0, 16384), 0.0),
                      deps=wrB)
        acc_toks = [t_z0]
        for tis in range(4):
            tt = sp * 4 + tis
            tbt = C.tb[1]
            treg = C.tb_reg[1]
            wdeps = treg.begin_write() + C.l0_z_toks
            mms = []
            for e in range(8):
                mm = C.pe(lambda t, e=e, tt=tt: t.transpose(
                    tbt[0:16, e * 128:(e + 1) * 128],
                    ar(C, AR_C, tt * 128 + e * 16, 16), C.ident[:]),
                    deps=(wdeps if e == 0 else ()))
                mms.append(mm)
            treg.end_write(mms)
            t_zp = C.dve(lambda v: v.tensor_copy(ztp[0:16, :],
                                                 tbt[0:16, 0:1024]),
                         deps=[mms[-1]] + regD.begin_write())
            treg.add_reader(t_zp)
            regD.end_write(t_zp)
            for e in range(8):
                t_be = C.dma_sync(lambda s, e=e: s.dma_start(
                    bet[0:16, :], W["BT"][e]),
                    deps=regs_begin_write(betregs))
                regs_end_write(betregs, t_be)
                mmd = []
                for fs in range(8):
                    bankd = C.pb_reg[fs % 4].begin_write()
                    mm = C.pe(lambda t, fs=fs, e=e: t.matmul(
                        C.pb[fs % 4][:, :], ztp[0:16, e * 128:(e + 1) * 128],
                        bet[0:16, fs * 512:(fs + 1) * 512],
                        start=True, stop=True),
                        deps=[t_zp, t_be] + list(bankd))
                    regs_add_reader(betregs, mm)
                    regD.add_reader(mm)
                    C.pb_reg[fs % 4].end_write(mm)
                    mmd.append(mm)
                    sscr = C.scr[:, 0:512]
                    t_s = C.dve(lambda v, fs=fs, tis=tis: v.tensor_tensor(
                        sscr, C.pb[fs % 4][:, :],
                        C.arena[:, AR_A + tis * 4096 + fs * 512:
                                AR_A + tis * 4096 + fs * 512 + 512],
                        ALU.add),
                        deps=[mm, base_toks[-1]] + C.scr_reg.begin_write())
                    C.scr_reg.end_write(t_s)
                    C.pb_reg[fs % 4].add_reader(t_s)
                    t_u = C.act(lambda s, fs=fs: s.activation(
                        sscr, sscr, AF.Relu), deps=[t_s])
                    C.scr_reg.end_write([t_s, t_u])
                    t_ac = C.dve(lambda g, fs=fs, tis=tis, e=e, tt=tt:
                                  g.scalar_tensor_tensor(
                                      C.arena[:, AR_B + tis * 8192 + fs * 1024:
                                              AR_B + tis * 8192 + fs * 1024 + 1024]
                                      .bitcast(F32) if False else
                                      ar(C, AR_B, tis * 4096 + fs * 512, 512),
                                      sscr,
                                      C.l0_gates[:, tt * 8 + e:tt * 8 + e + 1],
                                      ar(C, AR_B, tis * 4096 + fs * 512, 512),
                                      ALU.mult, ALU.add),
                                  deps=[t_u, acc_toks[-1]] + C.l0_gate_toks[
                                      tt:tt + 1])
                    C.scr_reg.add_reader(t_ac)
                    acc_toks.append(t_ac)
        regB.end_write(acc_toks)
        if sp == 0:
            dbg_dump(C, "l0acc0", acc_toks)
        wrA2 = regA.begin_write()
        accT_evs = []
        for tis in range(4):
            for b0 in range(0, 32, 8):
                ev = tb_transpose_batch(
                    C,
                    [(lambda tis=tis, fc=fc: C.arena[
                        :, AR_B + tis * 4096 + fc * 128:
                        AR_B + tis * 4096 + fc * 128 + 128])
                     for fc in range(b0, b0 + 8)],
                    lambda nb, tis=tis, b0=b0: C.arena[
                        :, AR_A + (b0 // 8) * 4096 + tis * 1024:
                        AR_A + (b0 // 8) * 4096 + tis * 1024 + 1024],
                    deps=acc_toks[-1:] + (wrA2 if tis == 0 and b0 == 0
                                          else []), tbi=0)
                accT_evs.append(ev)
        regA.end_write(accT_evs)
        regB.add_reader(accT_evs)
        wo_t = [wb(C, 1024, 512), wb(C, 1536, 512)]
        woregs = [wb_slots(C, 1024, 512), wb_slots(C, 1536, 512)]
        for fh in range(2):
            mml2 = [None] * 4
            for fc in range(32):
                wt = wo_t[fc % 2]
                t_w = C.dma_sync(lambda s, wt=wt, fc=fc, fh=fh: s.dma_start(
                    wt, wov[fc * 128:(fc + 1) * 128,
                            fh * 512:(fh + 1) * 512]),
                    deps=regs_begin_write(woregs[fc % 2]))
                regs_end_write(woregs[fc % 2], t_w)
                for tis in range(4):
                    bankd = C.pb_reg[tis].begin_write() if fc == 0 else ()
                    mm = C.pe(lambda t, wt=wt, tis=tis, fc=fc: t.matmul(
                        C.pb[tis][:, :],
                        C.arena[:, AR_A + (fc // 8) * 4096 + tis * 1024 +
                                (fc % 8) * 128:
                                AR_A + (fc // 8) * 4096 + tis * 1024 +
                                (fc % 8) * 128 + 128],
                        wt, start=(fc == 0), stop=False),
                        deps=[t_w] + list(bankd) +
                        (accT_evs if fc == 0 and tis == 0 else []))
                    regs_add_reader(woregs[fc % 2], mm)
                    mml2[tis] = mm
            for tis in range(4):
                tt = sp * 4 + tis
                mmb = C.pe(lambda t, tis=tis, tt=tt, fh=fh: t.matmul(
                    C.pb[tis][:, :], C.l0_v2row[:, tt * 128:(tt + 1) * 128],
                    C.l0_borow2[:, fh * 512:(fh + 1) * 512],
                    start=False, stop=True),
                    deps=[C.l0_v2row_tok, C.l0_borow2_tok])
                C.pb_reg[tis].end_write(mmb)
                regA.add_reader(mmb)
                xd = C.xres_reg[tt].begin_write()
                sl = slice(tt * D + fh * 512, tt * D + (fh + 1) * 512)
                ev = C.dve(lambda v, tis=tis, sl=sl: v.tensor_add(
                    C.x_res[:, sl], C.x_res[:, sl], C.pb[tis][:, :]),
                    deps=[mmb] + xd)
                C.pb_reg[tis].add_reader(ev)
                C.xres_reg[tt].end_write(ev)
                add_toks.append(ev)
    return add_toks


def emit_layer0(C, W):
    parts = _os.environ.get("MK_L0_PARTS", "af")
    prep = emit_l0_prep(C, W) if "f" in parts else []
    emit_ln(C)
    if "a" in parts:
        wBfn, wB_toks = emit_l0_wpairs(C, W)
        qkv_toks = []
        for wname, bname, scrname in (("Wq", "bq", "qscr"),
                                      ("Wk", "bk", "kscr"),
                                      ("Wv", "bv", "vscr")):
            qkv_toks += emit_l0_proj_spill(C, W, wname, bname, scrname)
        ot_toks = emit_l0_attn_mix(C, W, wBfn, wB_toks, qkv_toks)
        emit_l0_attn_out(C, W, ot_toks)
    if "f" not in parts:
        layer_reset_pools(C)
        return
    emit_l0_gating(C, W)
    C.l0_borow2 = C.rowp[0:1, 2560:3584]
    t_b = load_cast(C, C.l0_borow2, W["bo2"].unsqueeze(0))
    C.l0_borow2_tok = C.dve(lambda v: v.tensor_single_scalar(
        C.l0_borow2, C.l0_borow2, 2.0, ALU.mult), deps=[t_b])
    emit_l0_ffn(C, W)
    layer_reset_pools(C)



def l2_declare(nc):
    W = {}
    for n in ("Wq", "Wk", "Wv"):
        W[n] = nc.dram_tensor(f"L2_attn_{n}", [16, D, 64], F32,
                              kind="ExternalInput").ap()
    W["Wp"] = nc.dram_tensor("L2_attn_Wp", [D, D], F32, kind="ExternalInput").ap()
    W["bp"] = nc.dram_tensor("L2_attn_bp", [D], F32, kind="ExternalInput").ap()
    W["rel"] = nc.dram_tensor("L2_attn_rel_table", [33, 16], F32,
                              kind="ExternalInput").ap()
    W["w_gate"] = nc.dram_tensor("L2_ffn_w_gate", [2816, D], F32,
                                 kind="ExternalInput").ap()
    W["w_up"] = nc.dram_tensor("L2_ffn_w_up", [2816, D], F32,
                               kind="ExternalInput").ap()
    W["w_down"] = nc.dram_tensor("L2_ffn_w_down", [D, 2816], F32,
                                 kind="ExternalInput").ap()
    dbg = {}
    if _os.environ.get("MK_DBG"):
        dbg = dict(kind="ExternalOutput")
    W["stripscr"] = nc.dram_tensor("l2_strip", [16, 1168], BF16, **dbg).ap()
    W["oscr"] = nc.dram_tensor("l2_oscr", [16, 64, T], BF16, **dbg).ap()
    W["wgqT"] = nc.dram_tensor("l2_wgqT", [D, 2816], BF16, **dbg).ap()
    W["wuqT"] = nc.dram_tensor("l2_wuqT", [D, 2816], BF16, **dbg).ap()
    W["wdqT"] = nc.dram_tensor("l2_wdqT", [2816, D], BF16, **dbg).ap()
    W["xqscr"] = nc.dram_tensor("l2_xqscr", [T, D], BF16, **dbg).ap()
    return W


def emit_quant_fp4(C, q, dst, nblk, deps, tmp_off=2048, qf32=False):
    n = nblk * 16
    assert nblk <= 32
    bm = C.qcols[:, 0:nblk]
    sc = C.qcols[:, 32:32 + nblk]
    rs = C.qcols[:, 64:64 + nblk]
    scrd = C.qreg.begin_write()
    qv = q.rearrange("p (b k) -> p b k", k=16)
    af = C.wbuf[:, tmp_off:tmp_off + 2 * n].bitcast(F32)
    rf = C.wbuf[:, tmp_off + 2 * n:tmp_off + 4 * n].bitcast(F32)
    sw = C.wbuf[:, tmp_off + 4 * n:tmp_off + 5 * n]
    gf = C.wbuf[:, tmp_off + 5 * n:tmp_off + 6 * n]
    tregs = wb_slots(C, tmp_off, 6 * n)
    twd = regs_begin_write(tregs)
    t1 = C.dve(lambda v: v.tensor_reduce(bm, qv, AX.X, ALU.max,
                                         apply_absolute_value=True),
               deps=deps + scrd)
    t2 = C.dve(lambda v: v.tensor_scalar(sc, bm, 1.0 / 6.0, 1e-12,
                                         ALU.mult, ALU.max), deps=[t1])
    t3 = C.dve(lambda v: v.reciprocal(rs, sc), deps=[t2])
    if qf32:
        t_s0 = C.dve(lambda v: v.tensor_scalar(sw, q, 0.0, -2.0, ALU.is_lt,
                                               ALU.mult), deps=deps + twd)
        t_s = C.dve(lambda v: v.tensor_scalar_add(sw, sw, 1.0), deps=[t_s0])
    else:
        t_s = C.dve(lambda v: v.tensor_single_scalar(
            sw.bitcast(mybir.dt.uint16), q.bitcast(mybir.dt.uint16),
            32768, ALU.bitwise_and), deps=deps + twd)
    t4 = C.dve(lambda v: v.tensor_tensor(
        af.rearrange("p (b k) -> p b k", k=16), qv,
        rs.unsqueeze(2).to_broadcast((128, nblk, 16)), ALU.mult),
        deps=[t3] + deps + twd)
    t5 = C.act(lambda s: s.activation(af, af, AF.Abs), deps=[t4])
    M = float(1.5 * 2 ** 23)
    t6 = C.dve(lambda v: v.tensor_scalar(rf, af, 2.0, 2.0, ALU.min, ALU.mult),
               deps=[t5])
    t7 = C.dve(lambda v: v.tensor_scalar(rf, rf, M, M, ALU.add, ALU.subtract),
               deps=[t6])
    tg = C.dve(lambda v: v.tensor_scalar(gf, af, 5.0, 2.0, ALU.is_gt,
                                         ALU.mult), deps=[t5])
    t8 = C.dve(lambda v: v.tensor_scalar(af, af, 2.0, 4.5, ALU.max, ALU.min),
               deps=[t5, tg])
    t9 = C.dve(lambda v: v.tensor_scalar(af, af, M, M + 2.0, ALU.add,
                                         ALU.subtract), deps=[t8])
    t10 = C.dve(lambda v: v.scalar_tensor_tensor(dst, rf, 0.5, af,
                                                 ALU.mult, ALU.add),
                deps=[t7, t9])
    t12 = C.dve(lambda v: v.tensor_tensor(dst, dst, gf, ALU.add),
                deps=[t10, tg])
    if qf32:
        t13 = C.dve(lambda v: v.tensor_tensor(dst, dst, sw, ALU.mult),
                    deps=[t12, t_s])
    else:
        t13 = C.dve(lambda v: v.tensor_tensor(
            dst.bitcast(mybir.dt.uint16), dst.bitcast(mybir.dt.uint16),
            sw.bitcast(mybir.dt.uint16), ALU.bitwise_or), deps=[t12, t_s])
    t14 = C.dve(lambda v: v.tensor_tensor(
        dst.rearrange("p (b k) -> p b k", k=16),
        dst.rearrange("p (b k) -> p b k", k=16),
        sc.unsqueeze(2).to_broadcast((128, nblk, 16)), ALU.mult),
        deps=[t13, t2])
    regs_end_write(tregs, [t_s, t4, t5, t6, t7, tg, t8, t9])
    regs_add_reader(tregs, [t10, t12, t13, t14])
    C.qreg.end_write([t1, t2, t3])
    C.qreg.add_reader([t4, t14])
    return t14


def emit_l2_weight_prep(C, W):
    toks = []
    stgf = wb(C, 0, 1024).bitcast(F32)
    sregs = wb_slots(C, 0, 1024)
    qdst = wb(C, 1024, 512)
    qregs = wb_slots(C, 1024, 512)
    ost = wb(C, 4608, 512)
    oregs = wb_slots(C, 4608, 512)
    for (src, dst, R, K2) in ((W["w_gate"], W["wgqT"], 2816, D),
                              (W["w_up"], W["wuqT"], 2816, D),
                              (W["w_down"], W["wdqT"], D, 2816)):
        dv = dst.rearrange("(kc p) r -> p kc r", p=128)
        for rb in range(R // 128):
            for hw0 in range(0, K2, 512):
                hwn = min(512, K2 - hw0)
                t_ld = C.dma_sync(lambda s, src=src, rb=rb, hw0=hw0, hwn=hwn:
                                  s.dma_start(stgf[:, 0:hwn],
                                              src[rb * 128:(rb + 1) * 128,
                                                  hw0:hw0 + hwn]),
                                  deps=regs_begin_write(sregs))
                regs_end_write(sregs, t_ld)
                qwd = regs_begin_write(qregs)
                t_q = emit_quant_fp4(C, stgf[:, 0:hwn], qdst[:, 0:hwn],
                                     hwn // 16, [t_ld] + qwd,
                                     tmp_off=1536, qf32=True)
                regs_add_reader(sregs, t_q)
                regs_end_write(qregs, t_q)
                nb = hwn // 128
                kg = hw0 // 128
                ev = tb_transpose_batch(
                    C,
                    [(lambda kc=kc: qdst[:, kc * 128:(kc + 1) * 128])
                     for kc in range(nb)],
                    lambda nb2: ost[:, 0:nb2 * 128],
                    deps=[t_q] + regs_begin_write(oregs), tbi=1)
                regs_end_write(oregs, ev)
                regs_add_reader(qregs, ev)
                t_o = C.dma_sync(lambda s, dv=dv, rb=rb, kg=kg, nb=nb:
                                 s.dma_start(
                                     dv[:, kg:kg + nb,
                                        rb * 128:(rb + 1) * 128],
                                     ost[:, 0:nb * 128].rearrange(
                                         "p (kc r) -> p kc r", r=128)),
                                 deps=[ev])
                regs_add_reader(oregs, t_o)
                toks.append(t_o)
    return toks


def l2_make_xq_hook(C, W):
    toks = []
    xnf = wb(C, 0, 2048).bitcast(F32)
    xfregs = wb_slots(C, 0, 2048)
    xqs = wb(C, 3584, 1024)
    xsregs = wb_slots(C, 3584, 1024)
    xqv = W["xqscr"].rearrange("(tt p) d -> p tt d", p=128)

    def hook(tt, xt, m_c, r_c, deps):
        wd = regs_begin_write(xfregs)
        t_x = C.dve(lambda v: v.tensor_scalar(
            xnf, xt, m_c, r_c, ALU.subtract, ALU.mult), deps=deps + wd)
        regs_end_write(xfregs, t_x)
        prev = regs_begin_write(xsregs)
        qts = []
        for q0 in range(0, 1024, 256):
            t_q = emit_quant_fp4(
                C, xnf[:, q0:q0 + 256],
                xqs[:, q0:q0 + 256], 16,
                [t_x] + qts[-1:] + (prev if q0 == 0 else []),
                tmp_off=2048, qf32=True)
            qts.append(t_q)
        regs_add_reader(xfregs, qts)
        regs_end_write(xsregs, qts)
        t_o = C.dma_sync(lambda s, tt=tt: s.dma_start(xqv[:, tt, :], xqs),
                         deps=qts[-1:])
        regs_add_reader(xsregs, t_o)
        toks.append(t_o)
    return hook, toks


def emit_l2_attn(C, W):
    xn_deps = C.xnT_reg.read_deps()
    regA, regB, regC, regD = (C.ar_reg[k] for k in "ABCD")
    t33 = C.scr[0:33, 0:16]
    scrd = C.scr_reg.begin_write()
    t_tb = C.dma_sync(lambda s: s.dma_start(t33, W["rel"]), deps=scrd)
    C.scr_reg.end_write(t_tb)
    striprow = wb(C, 0, 1168)
    spregs = wb_slots(C, 0, 1168)
    strip_toks = []
    for h in range(16):
        wd = regs_begin_write(spregs)
        t_z = C.pool(lambda g: g.memset(striprow, 0.0), deps=wd)
        t_f1 = C.dve(lambda v, h=h: v.tensor_scalar(
            striprow[0:1, 0:640], striprow[0:1, 0:640], 0.0,
            t33[0:1, h:h + 1], ALU.mult, ALU.add), deps=[t_tb, t_z])
        t_f2 = C.dma_pool(lambda g, h=h: g.dma_start(
            striprow[0:1, 623:640],
            W["rel"][0:17, h:h + 1].transpose([1, 0]),
            allow_slow_non_contiguous=True), deps=[t_z])
        t_f3 = C.dve(lambda v: v.tensor_scalar(
            striprow[0:1, 640:1168], striprow[0:1, 640:1168], 0.0,
            -30000.0, ALU.mult, ALU.add), deps=[t_z])
        t_o = C.dma_sync(lambda s, h=h: s.dma_start(
            W["stripscr"][h].unsqueeze(0), striprow[0:1, :]),
            deps=[t_f1, t_f2, t_f3])
        regs_end_write(spregs, [t_f1, t_f2, t_f3])
        regs_add_reader(spregs, t_o)
        strip_toks.append(t_o)
    C.scr_reg.add_reader(strip_toks)
    t0row = rowb_alloc(C, 16)
    t_t0 = C.dve(lambda v: v.tensor_copy(t0row, t33[0:1, :]), deps=[t_tb])
    C.scr_reg.add_reader(t_t0)
    tbl0col = smcol(C, 16)
    t_t0b = bcast_col(C, lambda: t0row, 16, tbl0col, [t_t0])

    s_tile = ar(C, AR_A, 0, 4096).bitcast(F32)
    p_tile = ar(C, AR_A, 4096, 2048)
    pT_sup = lambda off, n: ar(C, AR_B, off, n)
    qT = ar(C, AR_B, 8192, 2048)
    kT = ar(C, AR_B, 10240, 2048)
    vsl = ar(C, AR_B, 12288, 2048)
    Mh = ar(C, AR_D, 0, 1024)
    mstage = ar(C, AR_D, 1024, 1024)

    stat_m = smcol(C, 16)
    stat_r = smcol(C, 16)
    l2qkv = Region("l2qkv")
    C.l2qkv = l2qkv
    prev_sp = []

    wpair = [wb(C, 0, 512), wb(C, 512, 512)]
    wpregs = [wb_slots(C, 0, 512), wb_slots(C, 512, 512)]
    ostg = [wb(C, 1024, 512), wb(C, 1536, 512)]
    ostregs = [wb_slots(C, 1024, 512), wb_slots(C, 1536, 512)]

    for pair in range(8):
        wrB = l2qkv.begin_write()
        pair_toks = {}
        for wname, dst, scaleq in (("Wq", qT, True), ("Wk", kT, False)):
            evs = []
            mml = [None] * 4
            t_ws = []
            for cc in range(8):
                wt = wpair[cc % 2]
                t_w = load_cast(
                    C, wt[:, 0:128].rearrange("p (h d) -> p h d", d=64),
                    W[wname][2 * pair:2 * pair + 2,
                             cc * 128:(cc + 1) * 128, :].transpose([1, 0, 2]),
                    deps=regs_begin_write(wpregs[cc % 2]))
                regs_end_write(wpregs[cc % 2], t_w)
                t_ws.append(t_w)
                for ts4 in range(4):
                    bankd = C.pb_reg[ts4].begin_write() if cc == 0 else ()
                    mm = C.pe(lambda t, wt=wt, ts4=ts4, cc=cc: t.matmul(
                        C.pb[ts4][:, :],
                        wt[:, 0:128],
                        xnT_tile(C, cc, ts4 * 512, 512),
                        start=(cc == 0), stop=(cc == 7)),
                        deps=[t_w] + list(bankd) +
                        (xn_deps if cc == 0 and ts4 == 0 else []))
                    regs_add_reader(wpregs[cc % 2], mm)
                    mml[ts4] = mm
            for ts4 in range(4):
                C.pb_reg[ts4].end_write(mml[ts4])
                if scaleq:
                    ev = C.dve(lambda v, ts4=ts4, dst=dst: v.tensor_single_scalar(
                        dst[:, ts4 * 512:(ts4 + 1) * 512], C.pb[ts4][:, :],
                        0.125, ALU.mult),
                        deps=[mml[ts4]] + (wrB if ts4 == 0 else []))
                else:
                    ev = C.dve(lambda v, ts4=ts4, dst=dst: v.tensor_copy(
                        dst[:, ts4 * 512:(ts4 + 1) * 512], C.pb[ts4][:, :]),
                        deps=[mml[ts4]])
                C.pb_reg[ts4].add_reader(ev)
                evs.append(ev)
            pair_toks[wname] = evs
        v_evs = []
        t_ws = []
        vw = wb(C, 2048, 1024)
        vwregs = wb_slots(C, 2048, 1024)
        vwd = regs_begin_write(vwregs)
        for cc in range(8):
            t_w = load_cast(
                C, vw[:, cc * 128:(cc + 1) * 128]
                .rearrange("p (h d) -> p h d", d=64),
                W["Wv"][2 * pair:2 * pair + 2,
                        cc * 128:(cc + 1) * 128, :].transpose([1, 0, 2]),
                deps=vwd if cc == 0 else [])
            t_ws.append(t_w)
        regs_end_write(vwregs, t_ws)
        for tt in range(NTT):
            bi = 4 + (tt % 2)
            bankd = C.pb_reg[bi].begin_write()
            mml2 = None
            for cc in range(8):
                mm = C.pe(lambda t, bi=bi, cc=cc, tt=tt: t.matmul(
                    C.pb[bi][:, 0:128],
                    xnT_tile(C, cc, tt * 128, 128),
                    vw[:, cc * 128:(cc + 1) * 128],
                    start=(cc == 0), stop=(cc == 7)),
                    deps=([t_ws[-1]] + list(bankd)) if cc == 0 else [t_ws[-1]])
                regs_add_reader(vwregs, mm)
                mml2 = mm
            C.pb_reg[bi].end_write(mml2)
            ev = C.dve(lambda v, bi=bi, tt=tt: v.tensor_copy(
                vsl[:, tt * 128:(tt + 1) * 128], C.pb[bi][:, 0:128]),
                deps=[mml2])
            C.pb_reg[bi].add_reader(ev)
            v_evs.append(ev)
        l2qkv.end_write(pair_toks["Wq"] + pair_toks["Wk"] + v_evs)
        if pair == 0:
            dbg_dump(C, "l2v", pair_toks["Wq"] + pair_toks["Wk"] + v_evs)

        for s2 in range(2):
            h = pair * 2 + s2
            wrD = regD.begin_write()
            t_mp = C.dma_sync(lambda s, h=h: s.dma_start(
                mstage, _ovl_ap(W["stripscr"], h)),
                deps=[strip_toks[h]] + wrD)
            mm_j = []
            for half in range(2):
                bi = half
                bankd = C.pb_reg[bi].begin_write()
                mm = C.pe(lambda t, half=half, bi=bi: t.matmul(
                    C.pb[bi][:, :],
                    C.identJ[:],
                    mstage[:, half * 512:(half + 1) * 512],
                    start=True, stop=True),
                    deps=[t_mp, C.t_identJ] + list(bankd))
                C.pb_reg[bi].end_write(mm)
                ev = C.dve(lambda v, half=half, bi=bi: v.tensor_copy(
                    Mh[:, half * 512:(half + 1) * 512],
                    C.pb[bi][:, :]), deps=[mm])
                C.pb_reg[bi].add_reader(ev)
                mm_j.append(ev)
            regD.end_write(mm_j)
            regD.add_reader(t_mp)

            for isup in range(4):
                jext = (isup + 1) * 512
                njt = jext // 128
                wrB2 = []
                p_done = []
                for it in range(4):
                    itile = isup * 4 + it
                    iext = (itile + 1) * 128
                    njb = (iext + 511) // 512
                    wrA = regA.begin_write() if it == 0 and isup == 0 else []
                    sc_evs = []
                    for jb in range(njb):
                        bi = jb % 4
                        bankd = C.pb_reg[bi].begin_write()
                        mm = C.pe(lambda t, bi=bi, s2=s2, itile=itile, jb=jb:
                                  t.matmul(
                                      C.pb[bi][:, :],
                                      qT[s2 * 64:(s2 + 1) * 64,
                                         itile * 128:(itile + 1) * 128],
                                      kT[s2 * 64:(s2 + 1) * 64,
                                         jb * 512:(jb + 1) * 512],
                                      start=True, stop=True),
                                  deps=list(bankd) + pair_toks["Wq"] +
                                  pair_toks["Wk"])
                        C.pb_reg[bi].end_write(mm)
                        xoff = jb * 512 - itile * 128 + 512
                        if xoff >= 0:
                            assert xoff <= 512, (jb, itile)
                            ev = C.dve(lambda v, bi=bi, jb=jb, xoff=xoff:
                                       v.scalar_tensor_tensor(
                                           s_tile[:, jb * 512:(jb + 1) * 512],
                                           C.pb[bi][:, :], 1.0,
                                           Mh[:, xoff:xoff + 512],
                                           ALU.mult, ALU.add),
                                       deps=[mm, mm_j[-1]] + wrA + prev_sp)
                        else:
                            ev = C.dve(lambda v, bi=bi, jb=jb, h=h:
                                       v.tensor_scalar(
                                           s_tile[:, jb * 512:(jb + 1) * 512],
                                           C.pb[bi][:, :],
                                           tbl0col[:, h:h + 1], None,
                                           ALU.add),
                                       deps=[mm, t_t0b] + wrA + prev_sp)
                        C.pb_reg[bi].add_reader(ev)
                        regD.add_reader(ev)
                        sc_evs.append(ev)
                    mcol = stat_m[:, itile % 16:itile % 16 + 1]
                    rcol = stat_r[:, itile % 16:itile % 16 + 1]
                    t_m = C.dve(lambda v, njb=njb, mcol=mcol: v.reduce_max(
                        mcol, s_tile[:, 0:njb * 512], AX.X),
                        deps=sc_evs + prev_sp)
                    t_nm = C.dve(lambda v, mcol=mcol: v.tensor_single_scalar(
                        mcol, mcol, -1.0, ALU.mult), deps=[t_m])
                    t_e = C.act(lambda s, njb=njb, mcol=mcol, rcol=rcol:
                                s.activation(
                                    p_tile[:, 0:njb * 512],
                                    s_tile[:, 0:njb * 512], AF.Exp,
                                    bias=mcol, accum_out=rcol),
                                deps=[t_nm] + sc_evs)
                    t_r = C.dve(lambda v, rcol=rcol: v.reciprocal(rcol, rcol),
                                deps=[t_e])
                    t_p = C.dve(lambda v, njb=njb, rcol=rcol:
                                v.tensor_single_scalar(
                                    p_tile[:, 0:njb * 512],
                                    p_tile[:, 0:njb * 512], rcol, ALU.mult),
                                deps=[t_r])
                    if pair == 0 and s2 == 0 and isup == 0 and it == 0:
                        dbg_dump(C, "l2fl", [t_p, t_e, t_m] + sc_evs)
                    njt_i = njb * 4
                    wrB2 = regB.begin_write() if it == 0 else wrB2
                    for j0 in range(0, njt_i, 8):
                        nb = min(8, njt_i - j0)
                        ev = tb_transpose_batch(
                            C,
                            [(lambda jt=jt: p_tile[:, jt * 128:(jt + 1) * 128])
                             for jt in range(j0, j0 + nb)],
                            lambda nb2, j0=j0, it=it: _l2_ptdst(C, j0, nb2, it),
                            deps=[t_p] + (wrB2 if it == 0 and j0 == 0 else []),
                            tbi=(it + j0) % 2)
                        p_done.append(ev)
                    prev_sp = [t_e, t_p] + p_done[-1:]
                regB.end_write(p_done)
                bi = 4
                bankd = C.pb_reg[bi].begin_write()
                mml3 = None
                for jt in range(njt):
                    mm = C.pe(lambda t, bi=bi, jt=jt, s2=s2: t.matmul(
                        C.pb[bi][0:64, :],
                        vsl[:, jt * 128 + s2 * 64: jt * 128 + (s2 + 1) * 64],
                        pT_sup(jt * 512, 512),
                        start=(jt == 0), stop=(jt == njt - 1)),
                        deps=(list(bankd) + p_done + v_evs) if jt == 0
                        else [p_done[-1]])
                    mml3 = mm
                C.pb_reg[bi].end_write(mml3)
                regB.add_reader(mml3)
                l2qkv.add_reader(mml3)
                st = ostg[isup % 2]
                ev = C.dve(lambda v, bi=bi, st=st: v.tensor_copy(
                    st[0:64, :], C.pb[bi][0:64, :]),
                    deps=[mml3] + regs_begin_write(ostregs[isup % 2]))
                C.pb_reg[bi].add_reader(ev)
                t_os = C.dma_sync(lambda s, h=h, isup=isup, st=st: s.dma_start(
                    W["oscr"][h, :, isup * 512:(isup + 1) * 512], st[0:64, :]),
                    deps=[ev])
                regs_end_write(ostregs[isup % 2], [ev, t_os])
                if not hasattr(C, "l2_o_toks"):
                    C.l2_o_toks = []
                C.l2_o_toks.append(t_os)

    bprow = rowb_alloc(C, 1024)
    t_bp = load_cast(C, bprow, W["bp"].unsqueeze(0))
    wtile = [wb(C, 2048, 1024), wb(C, 3072, 1024)]
    wtregs = [wb_slots(C, 2048, 1024), wb_slots(C, 3072, 1024)]
    otile = [wb(C, 0, 512), wb(C, 512, 512)]
    otregs = [Region("l2o0"), Region("l2o1")]
    for ttg in range(4):
        mml = [[None, None] for _ in range(4)]
        for hc in range(8):
            wt = wtile[hc % 2]
            t_w = load_cast(C, wt, W["Wp"][hc * 128:(hc + 1) * 128, :],
                            deps=regs_begin_write(wtregs[hc % 2]))
            regs_end_write(wtregs[hc % 2], t_w)
            for t4 in range(4):
                tt = ttg * 4 + t4
                ot = otile[(hc * 4 + t4) % 2]
                t_ot = C.dma_sync(lambda s, hc=hc, tt=tt, ot=ot: s.dma_start(
                    ot[:, 0:128],
                    W["oscr"][2 * hc:2 * hc + 2, :,
                              tt * 128:(tt + 1) * 128]
                    .rearrange("h d t -> (h d) t")),
                    deps=otregs[(hc * 4 + t4) % 2].begin_write() +
                    (C.l2_o_toks if hc == 0 and t4 == 0 else []))
                otregs[(hc * 4 + t4) % 2].end_write(t_ot)
                for fh in range(2):
                    bi = t4 * 2 + fh
                    dst = C.pb[bi] if bi < 6 else None
                    if dst is None:
                        dst = C.tb[bi - 6][:, :].bitcast(F32)
                    bankd = (C.pb_reg[bi].begin_write() if bi < 6 else
                             C.tb_reg[bi - 6].begin_write()) if hc == 0 else ()
                    mm = C.pe(lambda t, dst=dst, ot=ot, wt=wt, fh=fh, hc=hc:
                              t.matmul(
                                  dst[:, 0:512], ot[:, 0:128],
                                  wt[:, fh * 512:(fh + 1) * 512],
                                  start=(hc == 0), stop=False),
                              deps=[t_ot, t_w] + list(bankd))
                    regs_add_reader(wtregs[hc % 2], mm)
                    otregs[(hc * 4 + t4) % 2].add_reader(mm)
                    mml[t4][fh] = mm
        for t4 in range(4):
            tt = ttg * 4 + t4
            for fh in range(2):
                bi = t4 * 2 + fh
                dst = C.pb[bi] if bi < 6 else C.tb[bi - 6][:, :].bitcast(F32)
                mmb = C.pe(lambda t, dst=dst, fh=fh: t.matmul(
                    dst[:, 0:512], C.onesrow[:],
                    bprow[:, fh * 512:(fh + 1) * 512],
                    start=False, stop=True), deps=[t_bp, C.t_onesr])
                if bi < 6:
                    C.pb_reg[bi].end_write(mmb)
                else:
                    C.tb_reg[bi - 6].end_write(mmb)
                xd = C.xres_reg[tt].begin_write()
                sl = slice(tt * D + fh * 512, tt * D + (fh + 1) * 512)
                ev = C.dve(lambda v, dst=dst, sl=sl: v.tensor_add(
                    C.x_res[:, sl], C.x_res[:, sl], dst[:, 0:512]),
                    deps=[mmb] + xd)
                if bi < 6:
                    C.pb_reg[bi].add_reader(ev)
                else:
                    C.tb_reg[bi - 6].add_reader(ev)
                C.xres_reg[tt].end_write(ev)
    C.l2_o_toks = []


def _ovl_ap(stripscr, h):
    from concourse.bass_types import AP as _AP
    base = stripscr[h]
    return _AP(tensor=base.tensor, offset=h * 1168, ap=[[1, 128], [1, 1024]])


def _l2_ptdst(C, j0, nb, it):
    v = C.arena[:, AR_B:AR_B + 8192].rearrange("p (jt i) -> p jt i", i=512)
    return v[:, j0:j0 + nb, it * 128:(it + 1) * 128]



def emit_l2_ffn(C, W, prep_toks, xq_toks):
    regA, regB, regC, regD = (C.ar_reg[k] for k in "ABCD")
    wrB = regB.begin_write()
    if hasattr(C, "l2qkv"):
        wrB = wrB + C.l2qkv.begin_write()
    xqT_v = ar(C, AR_B, 0, 16384).rearrange("p (dc t) -> p dc t", t=T)
    xqv = W["xqscr"].rearrange("(tt p) d -> p tt d", p=128)
    xstg = [wb(C, 0, 1024), wb(C, 1024, 1024)]
    xstregs = [wb_slots(C, 0, 1024), wb_slots(C, 1024, 1024)]
    xqt_toks = []
    for tt in range(NTT):
        st = xstg[tt % 2]
        t_l = C.dma_sync(lambda s, tt=tt, st=st: s.dma_start(
            st, xqv[:, tt, :]),
            deps=regs_begin_write(xstregs[tt % 2]) +
            (xq_toks if tt < 2 else []))
        regs_end_write(xstregs[tt % 2], t_l)
        ev = tb_transpose_batch(
            C, [(lambda dc=dc, st=st: st[:, dc * 128:(dc + 1) * 128])
                for dc in range(NDC)],
            lambda nb, tt=tt: xqT_v[:, :, tt * 128:(tt + 1) * 128],
            deps=[t_l] + (wrB if tt == 0 else []), tbi=tt % 2)
        regs_add_reader(xstregs[tt % 2], ev)
        xqt_toks.append(ev)
    regB.end_write(xqt_toks)
    xq_toks = xqt_toks

    IBS = [512, 512, 512, 512, 512, 256]
    IB0 = [0, 512, 1024, 1536, 2048, 2560]
    wt_g = [wb(C, 0, 512), wb(C, 512, 512)]
    wt_u = [wb(C, 1024, 512), wb(C, 1536, 512)]
    wgregs = [wb_slots(C, 0, 512), wb_slots(C, 512, 512)]
    wuregs = [wb_slots(C, 1024, 512), wb_slots(C, 1536, 512)]
    sg = wb(C, 2048, 1024).bitcast(F32)
    sgregs = wb_slots(C, 2048, 1024)
    hscr = wb(C, 3072, 1024).bitcast(F32)
    hsregs = wb_slots(C, 3072, 1024)
    hq_v = ar(C, AR_A, 0, 11264)
    hqT_v = ar(C, AR_C, 0, 5632)
    wrA_all = regA.begin_write()
    for ttg in range(4):
        hq_toks = []
        wrA = regA.begin_write() if ttg > 0 else wrA_all
        for ib in range(6):
            ibw = IBS[ib]
            mmg = [None] * 4
            mmu = [None] * 4
            for dc in range(8):
                wg = wt_g[dc % 2]
                wu = wt_u[dc % 2]
                t_wg = C.dma_sync(lambda s, wg=wg, dc=dc, ib=ib, ibw=ibw:
                                  s.dma_start(wg[:, 0:ibw],
                                              W["wgqT"][dc * 128:(dc + 1) * 128,
                                                        IB0[ib]:IB0[ib] + ibw]),
                                  deps=regs_begin_write(wgregs[dc % 2]) +
                                  (prep_toks if ttg == 0 and ib == 0 and dc < 2
                                   else []))
                regs_end_write(wgregs[dc % 2], t_wg)
                t_wu = C.dma_sync(lambda s, wu=wu, dc=dc, ib=ib, ibw=ibw:
                                  s.dma_start(wu[:, 0:ibw],
                                              W["wuqT"][dc * 128:(dc + 1) * 128,
                                                        IB0[ib]:IB0[ib] + ibw]),
                                  deps=regs_begin_write(wuregs[dc % 2]))
                regs_end_write(wuregs[dc % 2], t_wu)
                for t4 in range(4):
                    tt = ttg * 4 + t4
                    bg = C.pb_reg[t4].begin_write() if dc == 0 else ()
                    mm1 = C.pe(lambda t, wg=wg, t4=t4, dc=dc, tt=tt, ibw=ibw:
                               t.matmul(C.pb[t4][:, 0:ibw],
                                        xqT_v[:, dc, tt * 128:(tt + 1) * 128],
                                        wg[:, 0:ibw],
                                        start=(dc == 0), stop=(dc == 7)),
                               deps=[t_wg] + list(bg) +
                               (xq_toks if dc == 0 and t4 == 0 else []))
                    regs_add_reader(wgregs[dc % 2], mm1)
                    mmg[t4] = mm1
                    bi2 = 4 + t4 % 2
                    bu = (C.pb_reg[bi2].begin_write() if t4 < 2 else
                          C.tb_reg[t4 - 2].begin_write()) if dc == 0 else ()
                    dstu = (C.pb[bi2][:, 0:ibw] if t4 < 2 else
                            C.tb[t4 - 2][:, :].bitcast(F32)[:, 0:ibw])
                    mm2 = C.pe(lambda t, wu=wu, dstu=dstu, dc=dc, tt=tt,
                               ibw=ibw: t.matmul(
                                   dstu,
                                   xqT_v[:, dc, tt * 128:(tt + 1) * 128],
                                   wu[:, 0:ibw],
                                   start=(dc == 0), stop=(dc == 7)),
                               deps=[t_wu] + list(bu))
                    regs_add_reader(wuregs[dc % 2], mm2)
                    mmu[t4] = mm2
            for t4 in range(4):
                dstu = (C.pb[4 + t4 % 2][:, 0:IBS[ib]] if t4 < 2 else
                        C.tb[t4 - 2][:, :].bitcast(F32)[:, 0:IBS[ib]])
                ibw = IBS[ib]
                t_sg = C.act(lambda s, t4=t4, ibw=ibw: s.activation(
                    sg[:, 0:ibw], C.pb[t4][:, 0:ibw], AF.Silu),
                    deps=[mmg[t4]] + regs_begin_write(sgregs))
                C.pb_reg[t4].end_write(mmg[t4])
                C.pb_reg[t4].add_reader(t_sg)
                regs_end_write(sgregs, t_sg)
                t_h = C.dve(lambda v, dstu=dstu, ibw=ibw: v.tensor_tensor(
                    hscr[:, 0:ibw], dstu, sg[:, 0:ibw], ALU.mult),
                    deps=[t_sg, mmu[t4]] + regs_begin_write(hsregs))
                if t4 < 2:
                    C.pb_reg[4 + t4 % 2].end_write(mmu[t4])
                    C.pb_reg[4 + t4 % 2].add_reader(t_h)
                else:
                    C.tb_reg[t4 - 2].end_write(mmu[t4])
                    C.tb_reg[t4 - 2].add_reader(t_h)
                regs_add_reader(sgregs, t_h)
                regs_end_write(hsregs, t_h)
                prevq = emit_quant_fp4(
                    C, hscr[:, 0:ibw],
                    hq_v[:, t4 * 2816 + IB0[ib]:
                         t4 * 2816 + IB0[ib] + ibw],
                    ibw // 16,
                    [t_h] +
                    (wrA if ib == 0 and t4 == 0 else []),
                    tmp_off=0, qf32=True)
                regs_add_reader(hsregs, prevq)
                hq_toks.append(prevq)
        regA.end_write(hq_toks)
        if ttg == 0:
            dbg_dump(C, "l2hq", hq_toks)
        wdt = [wb(C, 0, 1024), wb(C, 1024, 1024)]
        wdregs = [wb_slots(C, 0, 1024), wb_slots(C, 1024, 1024)]
        for t4p in range(2):
            t4s = (t4p * 2, t4p * 2 + 1)
            wrC = regC.begin_write()
            hT_toks = []
            blocks = [(ic, t4) for ic in range(22) for t4 in t4s]
            for b0 in range(0, len(blocks), 8):
                grp = blocks[b0:b0 + 8]
                ev = tb_transpose_batch(
                    C,
                    [(lambda ic=ic, t4=t4: hq_v[
                        :, t4 * 2816 + ic * 128: t4 * 2816 + ic * 128 + 128])
                     for (ic, t4) in grp],
                    lambda nb, b0=b0: ar(C, AR_C, b0 * 128, nb * 128),
                    deps=hq_toks + (wrC if b0 == 0 else []), tbi=b0 // 8 % 2)
                hT_toks.append(ev)
            regC.end_write(hT_toks)
            mmld = [[None, None], [None, None]]
            for i0 in range(22):
                wd_ = wdt[i0 % 2]
                t_w = C.dma_sync(lambda s, wd_=wd_, i0=i0: s.dma_start(
                    wd_, W["wdqT"][i0 * 128:(i0 + 1) * 128, :]),
                    deps=regs_begin_write(wdregs[i0 % 2]))
                regs_end_write(wdregs[i0 % 2], t_w)
                for j in range(2):
                    for dmh in range(2):
                        bi = j * 2 + dmh
                        bankd = (C.pb_reg[bi].begin_write()
                                 if i0 == 0 else ())
                        mm = C.pe(lambda t, wd_=wd_, i0=i0, j=j, dmh=dmh:
                                  t.matmul(
                                      C.pb[bi][:, :] if False else
                                      C.pb[j * 2 + dmh][:, :],
                                      ar(C, AR_C, (i0 * 2 + j) * 128, 128),
                                      wd_[:, dmh * 512:(dmh + 1) * 512],
                                      start=(i0 == 0), stop=(i0 == 21)),
                                  deps=[t_w, hT_toks[-1]] + list(bankd))
                        regs_add_reader(wdregs[i0 % 2], mm)
                        regC.add_reader(mm)
                        mmld[j][dmh] = mm
            for j in range(2):
                t4 = t4s[j]
                tt = ttg * 4 + t4
                for dmh in range(2):
                    bi = j * 2 + dmh
                    C.pb_reg[bi].end_write(mmld[j][dmh])
                    xd = C.xres_reg[tt].begin_write()
                    sl = slice(tt * D + dmh * 512, tt * D + (dmh + 1) * 512)
                    ev = C.dve(lambda v, bi=bi, sl=sl: v.tensor_add(
                        C.x_res[:, sl], C.x_res[:, sl], C.pb[bi][:, :]),
                        deps=[mmld[j][dmh]] + xd)
                    C.pb_reg[bi].add_reader(ev)
                    C.xres_reg[tt].end_write(ev)


def _l2_hqtdst(C, grp, icbase):
    (ic0, t40) = grp[0]
    start = ((ic0 - icbase) * 4 + t40) * 128
    return ar(C, AR_C, start, len(grp) * 128)


def emit_layer2(C, W):
    parts = _os.environ.get("MK_L2_PARTS", "af")
    prep = emit_l2_weight_prep(C, W) if "f" in parts else []
    if "f" in parts:
        hook, xq_toks = l2_make_xq_hook(C, W)
        emit_ln(C, tile_hook=hook)
    else:
        emit_ln(C)
        xq_toks = []
    if "a" in parts:
        emit_l2_attn(C, W)
    if "f" in parts:
        emit_l2_ffn(C, W, prep, xq_toks)
    layer_reset_pools(C)




def _sap(base, off, dims):
    from concourse.bass_types import AP as _bAP
    return _bAP(tensor=base.tensor, offset=base.offset + off,
                ap=[base.ap[0]] + dims)


def l3_declare(nc):
    W = {}
    W["W_in"] = nc.dram_tensor("L3_attn_W_in", [4096, D], F32,
                               kind="ExternalInput").ap()
    W["conv_w"] = nc.dram_tensor("L3_attn_conv_w", [2048, 1, 4], F32,
                                 kind="ExternalInput").ap()
    W["conv_b"] = nc.dram_tensor("L3_attn_conv_b", [2048], F32,
                                 kind="ExternalInput").ap()
    W["W_x"] = nc.dram_tensor("L3_attn_W_x", [96, 2048], F32,
                              kind="ExternalInput").ap()
    W["W_dt"] = nc.dram_tensor("L3_attn_W_dt", [2048, 64], F32,
                               kind="ExternalInput").ap()
    W["b_dt"] = nc.dram_tensor("L3_attn_b_dt", [2048], F32,
                               kind="ExternalInput").ap()
    W["A_log"] = nc.dram_tensor("L3_attn_A_log", [2048, 16], F32,
                                kind="ExternalInput").ap()
    W["D_param"] = nc.dram_tensor("L3_attn_D_param", [2048], F32,
                                  kind="ExternalInput").ap()
    W["W_out"] = nc.dram_tensor("L3_attn_W_out", [D, 2048], F32,
                                kind="ExternalInput").ap()
    W["gate_w"] = nc.dram_tensor("L3_ffn_gate_w", [8, D], F32,
                                 kind="ExternalInput").ap()
    W["lb_bias"] = nc.dram_tensor("L3_ffn_lb_bias", [8], F32,
                                  kind="ExternalInput").ap()
    W["egw"] = nc.dram_tensor("L3_ffn_expert_gate_w", [8, 512, D], F32,
                              kind="ExternalInput").ap()
    W["euw"] = nc.dram_tensor("L3_ffn_expert_up_w", [8, 512, D], F32,
                              kind="ExternalInput").ap()
    W["edw"] = nc.dram_tensor("L3_ffn_expert_down_w", [8, D, 512], F32,
                              kind="ExternalInput").ap()
    W["sgw"] = nc.dram_tensor("L3_ffn_shared_gate_w", [1024, D], F32,
                              kind="ExternalInput").ap()
    W["suw"] = nc.dram_tensor("L3_ffn_shared_up_w", [1024, D], F32,
                              kind="ExternalInput").ap()
    W["sdw"] = nc.dram_tensor("L3_ffn_shared_down_w", [D, 1024], F32,
                              kind="ExternalInput").ap()
    _dbg_env = _os.environ.get("MK_DBG", "")

    def dbg_k(nm):
        if _dbg_env == "1" or nm in _dbg_env.split(","):
            return dict(kind="ExternalOutput")
        return {}
    W["WinT"] = nc.dram_tensor("l3_WinT", [D, 4096], BF16).ap()
    W["WxT"] = nc.dram_tensor("l3_WxT", [2048, 96], BF16).ap()
    W["WdtT"] = nc.dram_tensor("l3_WdtT", [64, 2048], BF16).ap()
    W["WoutT"] = nc.dram_tensor("l3_WoutT", [2048, D], BF16).ap()
    W["egwT"] = nc.dram_tensor("l3_egwT", [8, D, 512], BF16).ap()
    W["euwT"] = nc.dram_tensor("l3_euwT", [8, D, 512], BF16).ap()
    W["edwT"] = nc.dram_tensor("l3_edwT", [8, 512, D], BF16).ap()
    W["sgwT"] = nc.dram_tensor("l3_sgwT", [D, 1024], BF16).ap()
    W["suwT"] = nc.dram_tensor("l3_suwT", [D, 1024], BF16).ap()
    W["sdwT"] = nc.dram_tensor("l3_sdwT", [1024, D], BF16).ap()
    W["xcscr"] = nc.dram_tensor("l3_xcscr", [16, 128, T], BF16, **dbg_k("xcscr")).ap()
    W["zscr"] = nc.dram_tensor("l3_zscr", [16, 128, T], BF16, **dbg_k("zscr")).ap()
    W["xdbl"] = nc.dram_tensor("l3_xdbl", [T, 96], BF16, **dbg_k("xdbl")).ap()
    W["dscr"] = nc.dram_tensor("l3_dscr", [16, 128, T], F32, **dbg_k("dscr")).ap()
    W["y2scr"] = nc.dram_tensor("l3_y2scr", [16, 128, T], BF16, **dbg_k("y2scr")).ap()
    W["ccin"] = nc.dram_tensor("l3_ccin", [128, 128], F32).ap()
    W["ccout"] = nc.dram_tensor("l3_ccout", [128, 128], F32).ap()
    return W


def _l3t(C, src_ap, dst_ap, R, K2, toks):
    ping = K2 <= 1024
    if ping:
        stgs = [wb(C, 0, 1024), wb(C, 1024, 1024)]
        sregss = [wb_slots(C, 0, 1024), wb_slots(C, 1024, 1024)]
    else:
        stgs = [wb(C, 0, 2048)]
        sregss = [wb_slots(C, 0, 2048)]
    ost = wb(C, 2048, 1024)
    oregs = wb_slots(C, 2048, 1024)
    nkc = K2 // 128 if K2 >= 128 else 0
    dv = (dst_ap.rearrange("(kc p) r -> p kc r", p=128) if nkc else None)
    for rb in range((R + 127) // 128):
        rn = min(128, R - rb * 128)
        stg = stgs[rb % len(stgs)]
        sregs = sregss[rb % len(stgs)]
        t_ld = load_cast(C, stg[0:rn, 0:K2],
                         src_ap[rb * 128:rb * 128 + rn, :],
                         deps=regs_begin_write(sregs))
        regs_end_write(sregs, t_ld)
        if nkc:
            for k0 in range(0, nkc, 8):
                nb = min(8, nkc - k0)
                ev = tb_transpose_batch(
                    C, [(lambda kc=kc, rn=rn, stg=stg:
                         stg[0:rn, kc * 128:(kc + 1) * 128])
                        for kc in range(k0, k0 + nb)],
                    lambda nb2, rn=rn: ost[:, 0:nb2 * rn],
                    deps=[t_ld] + regs_begin_write(oregs), tbi=1,
                    src_rows=rn)
                regs_end_write(oregs, ev)
                regs_add_reader(sregs, ev)
                t_o = C.dma_sync(
                    lambda s, k0=k0, nb=nb, rb=rb, rn=rn: s.dma_start(
                        dv[:, k0:k0 + nb, rb * 128:rb * 128 + rn],
                        ost[:, 0:nb * rn].rearrange("p (kc r) -> p kc r",
                                                    r=rn)),
                    deps=[ev])
                regs_add_reader(oregs, t_o)
                toks.append(t_o)
        else:
            ev = tb_transpose_batch(
                C, [lambda rn=rn, stg=stg: stg[0:rn, 0:K2]],
                lambda nb2, rn=rn: ost[0:K2, 0:rn],
                deps=[t_ld] + regs_begin_write(oregs), tbi=1,
                src_rows=rn, src_cols=K2)
            regs_end_write(oregs, ev)
            regs_add_reader(sregs, ev)
            t_o = C.dma_sync(lambda s, rb=rb, rn=rn: s.dma_start(
                dst_ap[:, rb * 128:rb * 128 + rn], ost[0:K2, 0:rn]),
                deps=[ev])
            regs_add_reader(oregs, t_o)
            toks.append(t_o)


def emit_l3_prep_attn(C, W):
    P = {}
    for name, src, dst, R, K2 in (
            ("WinT", W["W_in"], W["WinT"], 4096, D),
            ("WxT", W["W_x"], W["WxT"], 96, 2048),
            ("WdtT", W["W_dt"], W["WdtT"], 2048, 64),
            ("WoutT", W["W_out"], W["WoutT"], D, 2048)):
        toks = []
        _l3t(C, src, dst, R, K2, toks)
        P[name] = toks
    return P


def emit_l3_prep_ffn(C, W):
    P = {}
    for e in range(8):
        for name, src, dst, R, K2 in (
                (f"egwT{e}", W["egw"][e], W["egwT"][e], 512, D),
                (f"euwT{e}", W["euw"][e], W["euwT"][e], 512, D),
                (f"edwT{e}", W["edw"][e], W["edwT"][e], D, 512)):
            toks = []
            _l3t(C, src, dst, R, K2, toks)
            P[name] = toks
    for name, src, dst, R, K2 in (
            ("sgwT", W["sgw"], W["sgwT"], 1024, D),
            ("suwT", W["suw"], W["suwT"], 1024, D),
            ("sdwT", W["sdw"], W["sdwT"], D, 1024)):
        toks = []
        _l3t(C, src, dst, R, K2, toks)
        P[name] = toks
    return P


def emit_l3_gating(C, W):
    xn_deps = C.xnT_reg.read_deps()
    stg = wb(C, 0, 1024)
    sregs = wb_slots(C, 0, 1024)
    t_g = load_cast(C, stg[0:8, 0:1024], W["gate_w"],
                    deps=regs_begin_write(sregs))
    regs_end_write(sregs, t_g)
    gsl = wb(C, 1024, 64)
    gregs = wb_slots(C, 1024, 64)
    ev_g = tb_transpose_batch(
        C, [(lambda dc=dc: stg[0:8, dc * 128:(dc + 1) * 128])
            for dc in range(8)],
        lambda nb: gsl[:, 0:64],
        deps=[t_g] + regs_begin_write(gregs), tbi=1, src_rows=8)
    regs_end_write(gregs, ev_g)
    regs_add_reader(sregs, ev_g)
    lbrow = rowb_alloc(C, 8)
    t_lb = load_cast(C, lbrow, W["lb_bias"].unsqueeze(0),
                     deps=C.rowp_reg.begin_write())
    C.rowp_reg.end_write(t_lb)
    C.l3_masks = smcol(C, 128)
    C.l3_w = smcol(C, 128)
    tmp4 = smcol(C, 4)
    m1c, m2c, s1c, _ = (tmp4[:, i:i + 1] for i in range(4))
    mb16 = wb(C, 1088, 16)
    mregs = wb_slots(C, 1088, 16)
    mask_toks = []
    cnt_mm = wsum_mm = None
    prev_sm = []
    for tt in range(NTT):
        bi = tt % 2
        bankd = C.pb_reg[bi].begin_write()
        for dc in range(8):
            mm = C.pe(lambda t, bi=bi, dc=dc, tt=tt: t.matmul(
                C.pb[bi][:, 0:8], xnT_tile(C, dc, tt * 128, 128),
                gsl[:, dc * 8:(dc + 1) * 8],
                start=(dc == 0), stop=False),
                deps=([ev_g] + list(bankd) + xn_deps) if dc == 0 else [])
            regs_add_reader(gregs, mm)
        mmb = C.pe(lambda t, bi=bi: t.matmul(
            C.pb[bi][:, 0:8], C.onesrow[:], lbrow,
            start=False, stop=True), deps=[t_lb, C.t_onesr])
        C.rowp_reg.add_reader(mmb)
        C.pb_reg[bi].end_write(mmb)
        pb8 = C.l3_w[:, tt * 8:(tt + 1) * 8]
        mb8 = C.l3_masks[:, tt * 8:(tt + 1) * 8]
        t_sg = C.act(lambda s, bi=bi, pb8=pb8: s.activation(
            pb8, C.pb[bi][:, 0:8], AF.Sigmoid), deps=[mmb] + prev_sm)
        C.pb_reg[bi].add_reader(t_sg)
        t_m1 = C.dve(lambda v, pb8=pb8: v.reduce_max(m1c, pb8, AX.X),
                     deps=[t_sg])
        t_k = C.dve(lambda v, pb8=pb8: v.tensor_scalar(
            C.qcols[:, 0:8], pb8, m1c, -1e30, ALU.is_ge, ALU.mult),
            deps=[t_m1] + C.qreg.begin_write())
        t_k2 = C.dve(lambda v, pb8=pb8: v.tensor_add(
            C.qcols[:, 0:8], C.qcols[:, 0:8], pb8), deps=[t_k])
        t_m2 = C.dve(lambda v: v.reduce_max(m2c, C.qcols[:, 0:8], AX.X),
                     deps=[t_k2])
        C.qreg.end_write([t_k, t_k2])
        C.qreg.add_reader(t_m2)
        t_mk = C.dve(lambda v, pb8=pb8, mb8=mb8: v.tensor_scalar(
            mb8, pb8, m2c, None, ALU.is_ge), deps=[t_m2])
        t_sm = C.dve(lambda v: v.tensor_tensor(s1c, m1c, m2c, ALU.add),
                     deps=[t_m2, t_m1])
        t_se = C.dve(lambda v: v.tensor_scalar(
            s1c, s1c, 1e-8, None, ALU.add), deps=[t_sm])
        t_ri = C.dve(lambda v: v.reciprocal(s1c, s1c), deps=[t_se])
        t_w = C.dve(lambda v, pb8=pb8: v.tensor_scalar(
            pb8, pb8, s1c, None, ALU.mult), deps=[t_ri])
        t_w2 = C.dve(lambda v, pb8=pb8, mb8=mb8: v.tensor_tensor(
            pb8, pb8, mb8, ALU.mult), deps=[t_w, t_mk])
        prev_sm = [t_w2, t_mk]
        mask_toks.append(t_mk)
        t_cm = C.dve(lambda v, mb8=mb8: v.tensor_copy(mb16[:, 0:8], mb8),
                     deps=[t_mk] + regs_begin_write(mregs))
        t_cw = C.dve(lambda v, pb8=pb8: v.tensor_copy(mb16[:, 8:16], pb8),
                     deps=[t_w2, t_cm])
        regs_end_write(mregs, [t_cm, t_cw])
        bd4 = C.pb_reg[4].begin_write() if tt == 0 else ()
        cnt_mm = C.pe(lambda t, tt=tt: t.matmul(
            C.pb[4][0:1, 0:8], C.onescol[:], mb16[:, 0:8],
            start=(tt == 0), stop=(tt == 15)),
            deps=[t_cm, C.t_ones] + list(bd4))
        bd5 = C.pb_reg[5].begin_write() if tt == 0 else ()
        wsum_mm = C.pe(lambda t, tt=tt: t.matmul(
            C.pb[5][0:1, 0:8], C.onescol[:], mb16[:, 8:16],
            start=(tt == 0), stop=(tt == 15)),
            deps=[t_cw, C.t_ones] + list(bd5))
        regs_add_reader(mregs, [cnt_mm, wsum_mm])
    C.pb_reg[4].end_write(cnt_mm)
    C.pb_reg[5].end_write(wsum_mm)
    scrd = C.scr_reg.begin_write()
    ccst = C.scr[:, 0:128]
    t_z = C.pool(lambda g: g.memset(ccst, 0.0), deps=scrd)
    t_c1 = C.dve(lambda v: v.tensor_copy(ccst[0:1, 0:8], C.pb[4][0:1, 0:8]),
                 deps=[t_z, cnt_mm])
    t_c2 = C.dve(lambda v: v.tensor_copy(ccst[0:1, 8:16], C.pb[5][0:1, 0:8]),
                 deps=[t_z, wsum_mm])
    C.pb_reg[4].add_reader(t_c1)
    C.pb_reg[5].add_reader(t_c2)
    C.scr_reg.end_write([t_z, t_c1, t_c2])
    t_st = C.dma_sync(lambda s: s.dma_start(W["ccin"], ccst),
                      deps=[t_c1, t_c2])
    C.scr_reg.add_reader(t_st)
    t_cc = C.pool(lambda g: g.collective_compute(
        "AllReduce", ALU.add, replica_groups=[list(range(NCORES))],
        ins=[W["ccin"].opt()], outs=[W["ccout"].opt()]), deps=[t_st])
    C.l3_cc_tok = t_cc
    C.l3_mask_toks = mask_toks


def emit_l3_gates_final(C, W):
    rows = rowf_alloc(C, 16)
    t_ld = C.dma_sync(lambda s: s.dma_start(rows, W["ccout"][0:1, 0:16]),
                      deps=[C.l3_cc_tok])
    t_mx = C.dve(lambda v: v.tensor_scalar(
        rows[:, 0:8], rows[:, 0:8], 1.0, None, ALU.max), deps=[t_ld])
    t_ri = C.dve(lambda v: v.reciprocal(rows[:, 0:8], rows[:, 0:8]),
                 deps=[t_mx])
    t_wm = C.dve(lambda v: v.tensor_tensor(
        rows[:, 8:16], rows[:, 8:16], rows[:, 0:8], ALU.mult), deps=[t_ri])
    wmb = rowb_alloc(C, 8)
    t_cb = C.dve(lambda v: v.tensor_copy(wmb, rows[:, 8:16]), deps=[t_wm])
    wmB = smcol(C, 8)
    t_b = bcast_col(C, lambda: wmb, 8, wmB, [t_cb])
    g_toks = []
    for tt in range(NTT):
        gb = C.l3_w[:, tt * 8:(tt + 1) * 8]
        t_gt = C.dve(lambda v, gb=gb, tt=tt: v.tensor_tensor(
            gb, C.l3_masks[:, tt * 8:(tt + 1) * 8], wmB, ALU.mult),
            deps=[t_b, C.l3_mask_toks[tt]])
        g_toks.append(t_gt)
    C.l3_gates = C.l3_w
    C.l3_gate_toks = g_toks


def emit_l3_attn(C, W, P):
    xn_deps = C.xnT_reg.read_deps()
    regA = C.ar_reg["A"]
    regB = C.ar_reg["B"]
    regC = C.ar_reg["C"]
    regD = C.ar_reg["D"]
    baseb = C.arena[:, :]
    basef = C.arena[:, :].bitcast(F32)
    scrv = C.scr[:, :]
    scrb = C.scr[:, :].bitcast(BF16)
    wbv = C.wbuf[:, :]
    wbf = C.wbuf[:, :].bitcast(F32)
    sdeps = C.scr_reg.begin_write()
    scr_toks = []

    t_Al = C.dma_sync(lambda s: s.dma_start(
        C.scr[:, 0:256].rearrange("p (dc n) -> p dc n", n=16),
        W["A_log"].rearrange("(dc p) n -> p dc n", p=128)), deps=sdeps)
    t_Ae = C.act(lambda s: s.activation(
        C.scr[:, 0:256], C.scr[:, 0:256], AF.Exp), deps=[t_Al])
    t_An = C.dve(lambda v: v.tensor_single_scalar(
        C.scr[:, 0:256], C.scr[:, 0:256], -1.0, ALU.mult), deps=[t_Ae])
    t_Dl = C.dma_sync(lambda s: s.dma_start(
        C.scr[:, 256:272],
        W["D_param"].rearrange("(dc p) -> p dc", p=128),
        allow_slow_non_contiguous=True), deps=sdeps)
    t_bdt = C.dma_sync(lambda s: s.dma_start(
        C.scr[:, 336:352],
        W["b_dt"].rearrange("(dc p) -> p dc", p=128),
        allow_slow_non_contiguous=True), deps=sdeps)
    scr_toks += [t_An, t_Dl, t_bdt]

    Adeps = regA.begin_write()
    Cdeps = regC.begin_write()
    stagef = [basef[:, 0:2051], basef[:, 2052:4103]]
    accf = basef[:, 4104:6152]
    xcst = [ar(C, AR_C, 0, 2048), ar(C, AR_C, 2048, 2048)]
    winb = [wb(C, 0, 1024), wb(C, 1024, 1024)]
    winregs = [wb_slots(C, 0, 1024), wb_slots(C, 1024, 1024)]
    rgSt = [Region("l3st0"), Region("l3st1")]
    rgAcc = Region("l3acc")
    rgXc = [Region("l3xc0"), Region("l3xc1")]
    rgWc = [Region("l3wc0"), Region("l3wc1")]
    winv = W["WinT"].rearrange("(dc p) e -> p dc e", p=128)
    cwv = W["conv_w"].rearrange("(ec p) a k -> p ec (a k)", p=128)
    cbv = W["conv_b"].rearrange("(ec p) -> p ec", p=128)
    t_p0 = C.pool(lambda g: g.memset(stagef[0][:, 0:3], 0.0), deps=Adeps)
    t_p1 = C.pool(lambda g: g.memset(stagef[1][:, 0:3], 0.0), deps=Adeps)
    for rg in rgSt:
        rg.end_write([t_p0, t_p1])
    rgAcc.end_write(Adeps)
    for rg in rgXc:
        rg.end_write(Cdeps)
    for rg in rgWc:
        rg.end_write(sdeps)
    xc_store = [None] * 16
    z_store = [None] * 16
    aA_toks = [t_p0, t_p1]
    cA_toks = []
    for ec in range(32):
        pp = ec % 2
        t_w = C.dma_sync(lambda s, ec=ec, pp=pp: s.dma_start(
            winb[pp][:, :].rearrange("p (dc e) -> p dc e", e=128),
            winv[:, :, ec * 128:(ec + 1) * 128]),
            deps=[P["WinT"][ec]] + regs_begin_write(winregs[pp]))
        regs_end_write(winregs[pp], t_w)
        xpath = ec < 16
        if xpath:
            t_wc = C.dma_sync(lambda s, ec=ec, pp=pp: s.dma_start(
                C.scr[:, 272 + pp * 10:276 + pp * 10], cwv[:, ec, :]),
                deps=rgWc[pp].begin_write())
            t_cb = C.dma_sync(lambda s, ec=ec, pp=pp: s.dma_start(
                C.scr[:, 276 + pp * 10:277 + pp * 10], cbv[:, ec:ec + 1],
                allow_slow_non_contiguous=True), deps=[t_wc])
            rgWc[pp].end_write([t_wc, t_cb])
        ev4 = []
        for ts4 in range(4):
            bi = (ec * 4 + ts4) % 6
            bankd = C.pb_reg[bi].begin_write()
            mml = None
            for dc in range(8):
                mml = C.pe(lambda t, bi=bi, dc=dc, ts4=ts4, pp=pp: t.matmul(
                    C.pb[bi][:, :], winb[pp][:, dc * 128:(dc + 1) * 128],
                    xnT_tile(C, dc, ts4 * 512, 512),
                    start=(dc == 0), stop=(dc == 7)),
                    deps=([t_w] + list(bankd) + (xn_deps if ec == 0 else []))
                    if dc == 0 else [])
                regs_add_reader(winregs[pp], mml)
            C.pb_reg[bi].end_write(mml)
            if xpath:
                ev = C.dve(lambda v, bi=bi, ts4=ts4, pp=pp: v.tensor_copy(
                    stagef[pp][:, 3 + ts4 * 512:3 + ts4 * 512 + 512],
                    C.pb[bi][:, :]),
                    deps=[mml] + (rgSt[pp].begin_write() if ts4 == 0 else []))
            else:
                ev = C.dve(lambda v, bi=bi, ts4=ts4, pp=pp: v.tensor_copy(
                    xcst[pp][:, ts4 * 512:(ts4 + 1) * 512], C.pb[bi][:, :]),
                    deps=[mml] + (rgXc[pp].begin_write() if ts4 == 0 else []))
            C.pb_reg[bi].add_reader(ev)
            ev4.append(ev)
        if xpath:
            rgSt[pp].end_write(ev4)
            aA_toks += ev4
            wc = C.scr[:, 272 + pp * 10:277 + pp * 10]
            ad = rgAcc.begin_write()
            t_c = C.dve(lambda v, pp=pp, wc=wc: v.tensor_scalar(
                accf, stagef[pp][:, 0:2048], wc[:, 0:1], None, ALU.mult),
                deps=ev4 + [t_wc] + ad)
            for k in (1, 2, 3):
                t_c = C.dve(lambda v, pp=pp, wc=wc, k=k:
                            v.scalar_tensor_tensor(
                                accf, stagef[pp][:, k:2048 + k],
                                wc[:, k:k + 1], accf, ALU.mult, ALU.add),
                            deps=[t_c])
            rgAcc.end_write(t_c)
            rgSt[pp].add_reader(t_c)
            rgWc[pp].add_reader(t_c)
            aA_toks.append(t_c)
            xcw = rgXc[pp].begin_write()
            t_si = C.act(lambda s, pp=pp, wc=wc: s.activation(
                xcst[pp][:, :], accf, AF.Silu, bias=wc[:, 4:5]),
                deps=[t_c, t_cb] + xcw)
            rgAcc.add_reader(t_si)
            rgWc[pp].add_reader(t_si)
            rgXc[pp].end_write(t_si)
            aA_toks.append(t_si)
            cA_toks.append(t_si)
            scr_toks.append(t_si)
            t_o = C.dma_sync(lambda s, ec=ec, pp=pp: s.dma_start(
                W["xcscr"][ec], xcst[pp][:, :]), deps=[t_si])
            xc_store[ec] = t_o
        else:
            rgXc[pp].end_write(ev4)
            cA_toks += ev4
            t_o = C.dma_sync(lambda s, ec=ec, pp=pp: s.dma_start(
                W["zscr"][ec - 16], xcst[pp][:, :]), deps=ev4)
            z_store[ec - 16] = t_o
        rgXc[pp].add_reader(t_o)
        cA_toks.append(t_o)
    regA.end_write(aA_toks)
    regC.end_write(cA_toks)
    if _os.environ.get("MK_L3_PH", "ABCD") == "A":
        C.sync(lambda s: s.nop(), deps=cA_toks + scr_toks + sum(P.values(), []))
        C.scr_reg.end_write(scr_toks)
        return

    Adeps = regA.begin_write()
    Cdeps = regC.begin_write()
    bcut = _os.environ.get("MK_L3_B_CUT", "")
    wdts = ar(C, AR_A, 0, 2048)
    t_wdt = None
    if bcut != "x" and bcut != "t":
        t_wdt = C.dma_sync(lambda s: s.dma_start(
            wdts[0:64, :], W["WdtT"][:, :]), deps=P["WdtT"] + Adeps)
    wxs = wb(C, 0, 1536)
    wxregs = wb_slots(C, 0, 1536)
    t_wx = C.dma_sync(lambda s: s.dma_start(
        wxs[:, :].rearrange("p (ec n) -> p ec n", n=96),
        W["WxT"].rearrange("(ec p) n -> p ec n", p=128)),
        deps=P["WxT"] + regs_begin_write(wxregs))
    regs_end_write(wxregs, t_wx)
    xdst = [ar(C, AR_C, 0, 96), ar(C, AR_C, 96, 96)]
    dtlst = [ar(C, AR_C, 192, 128), ar(C, AR_C, 320, 128)]
    tmp1 = basef[:, 16640:17152]
    tmp2 = basef[:, 17152:17664]
    dstg = basef[:, 17664:18176]
    xcb = wb(C, 2048, 2048)
    wsl2 = regs_begin_write([C.wslot[2], C.wslot[3]])
    rgXd = [Region("l3xd0"), Region("l3xd1")]
    rgDtl = [Region("l3dt0"), Region("l3dt1")]
    rgT1 = Region("l3t1")
    rgT2 = Region("l3t2")
    rgDs = Region("l3ds")
    rgXcb = [Region("l3xb0"), Region("l3xb1")]
    for rg in (rgXd[0], rgXd[1], rgDtl[0], rgDtl[1], rgT1, rgT2, rgDs):
        rg.end_write(Cdeps)
    rgXcb[0].end_write(wsl2)
    dscr_store = [[] for _ in range(NTT)]
    xdbl_store = [None] * NTT
    cB_toks = []
    xcb_all = []
    delta_mms = []
    dvv = W["dscr"].rearrange("dc p t -> p dc t")
    xcvB = W["xcscr"].rearrange("dc p t -> p dc t")
    for tt in range(NTT):
        pq = tt % 2
        bi = pq
        bankd = C.pb_reg[bi].begin_write()
        t_xt = C.dma_sync(lambda s, tt=tt: s.dma_start(
            xcb[:, :].rearrange("p (dc t) -> p dc t", t=128),
            xcvB[:, :, tt * 128:(tt + 1) * 128]),
            deps=xc_store + rgXcb[0].begin_write())
        rgXcb[0].end_write(t_xt)
        xcb_all.append(t_xt)
        mml = None
        for ec in range(16):
            mml = C.pe(lambda t, bi=bi, ec=ec: t.matmul(
                C.pb[bi][:, 0:96], xcb[:, ec * 128:(ec + 1) * 128],
                wxs[:, ec * 96:(ec + 1) * 96],
                start=(ec == 0), stop=False),
                deps=[t_xt, t_wx] + (list(bankd) if ec == 0 else []))
            rgXcb[0].add_reader(mml)
            regs_add_reader(wxregs, mml)
        xcb_all.append(mml)
        mml = mm_close(C, C.pb[bi][:, 0:96], 96, [mml])
        C.pb_reg[bi].end_write(mml)
        t_xe = C.dve(lambda v, bi=bi, pq=pq: v.tensor_copy(
            xdst[pq][:, :], C.pb[bi][:, 0:96]),
            deps=[mml] + rgXd[pq].begin_write())
        C.pb_reg[bi].add_reader(t_xe)
        rgXd[pq].end_write(t_xe)
        t_xo = C.dma_sync(lambda s, tt=tt, pq=pq: s.dma_start(
            W["xdbl"][tt * 128:(tt + 1) * 128, :], xdst[pq][:, :]),
            deps=[t_xe])
        xdbl_store[tt] = t_xo
        rgXd[pq].add_reader(t_xo)
        if bcut == "x":
            cB_toks += [t_xe, t_xo]
            continue
        ev_t = tb_transpose_batch(
            C, [lambda pq=pq: xdst[pq][:, 0:64]],
            lambda nb, pq=pq: dtlst[pq][0:64, 0:128],
            deps=[t_xe] + rgDtl[pq].begin_write(),
            tbi=0, src_rows=128, src_cols=64)
        rgXd[pq].add_reader(ev_t)
        rgDtl[pq].end_write(ev_t)
        cB_toks += [t_xe, t_xo, ev_t]
        if bcut == "t":
            continue
        if tt >= int(_os.environ.get("MK_L3_B_NTT", "16")):
            continue
        for dsg in range(4):
            bj = 2 + dsg % 2
            bankd2 = C.pb_reg[bj].begin_write()
            mms = []
            for ds4 in range(4):
                ds = dsg * 4 + ds4
                mm = C.pe(lambda t, bj=bj, ds=ds, ds4=ds4, pq=pq: t.matmul(
                    C.pb[bj][:, ds4 * 128:(ds4 + 1) * 128],
                    wdts[0:64, ds * 128:(ds + 1) * 128],
                    dtlst[pq][0:64, :], start=True, stop=False),
                    deps=[ev_t, t_wdt] + (list(bankd2) if ds4 == 0 else []))
                rgDtl[pq].add_reader(mm)
                delta_mms.append(mm)
                mm = mm_close(C, C.pb[bj][:, ds4 * 128:(ds4 + 1) * 128],
                              128, [mm])
                mms.append(mm)
                delta_mms.append(mm)
            C.pb_reg[bj].end_write(mms)
            t1d = rgT1.begin_write()
            if bcut == "d1":
                tcp = C.dve(lambda v, bj=bj: v.tensor_copy(
                    tmp1, C.pb[bj][:, :]), deps=mms + t1d)
                C.pb_reg[bj].add_reader(tcp)
                rgT1.end_write(tcp)
                cB_toks.append(tcp)
                continue
            if bcut == "d6":
                tcp = C.dve(lambda v, bj=bj: v.tensor_copy(
                    tmp1, C.pb[bj][:, :]), deps=mms + t1d)
                C.pb_reg[bj].add_reader(tcp)
                t_cl = C.dve(lambda v: v.tensor_scalar(
                    dstg, tmp1, 1e-4, 10.0, ALU.add, ALU.min),
                    deps=[tcp] + rgDs.begin_write())
                rgDs.end_write(t_cl)
                rgT1.end_write([tcp, t_cl])
                cB_toks.append(t_cl)
                continue
            tls = []
            for ds4 in range(4):
                bcol = C.scr[:, 336 + dsg * 4 + ds4:337 + dsg * 4 + ds4]
                if _os.environ.get("MK_L3_TSIMM"):
                    tl = C.dve(lambda v, bj=bj, ds4=ds4:
                               v.tensor_scalar(
                                   tmp1[:, ds4 * 128:(ds4 + 1) * 128],
                                   C.pb[bj][:, ds4 * 128:(ds4 + 1) * 128],
                                   0.5, None, ALU.add),
                               deps=[mms[-1], t_bdt] +
                               (t1d if ds4 == 0 else []))
                else:
                    tl = C.dve(lambda v, bj=bj, ds4=ds4, bcol=bcol:
                               v.tensor_scalar(
                                   tmp1[:, ds4 * 128:(ds4 + 1) * 128],
                                   C.pb[bj][:, ds4 * 128:(ds4 + 1) * 128],
                                   bcol, None, ALU.add),
                               deps=[mms[-1], t_bdt] +
                               (t1d if ds4 == 0 else []))
                C.pb_reg[bj].add_reader(tl)
                tls.append(tl)
            if bcut == "d5":
                t_cl = C.dve(lambda v: v.tensor_copy(dstg, tmp1),
                             deps=tls + rgDs.begin_write())
                rgT1.add_reader(t_cl)
                rgDs.end_write(t_cl)
                cB_toks.append(t_cl)
                rgT1.end_write(tls)
                continue
            if bcut == "d3":
                t_cl = C.dve(lambda v: v.tensor_scalar(
                    dstg, tmp1, 1e-4, 10.0, ALU.add, ALU.min),
                    deps=tls + rgDs.begin_write())
                _unused = 0
                rgT1.add_reader(t_cl)
                rgDs.end_write(t_cl)
                cB_toks.append(t_cl)
                rgT1.end_write(tls)
                continue
            t_ab = C.act(lambda s: s.activation(tmp2, tmp1, AF.Abs),
                         deps=tls + rgT2.begin_write())
            t_ex = C.act(lambda s: s.activation(tmp2, tmp2, AF.Exp,
                                                scale=-1.0), deps=[t_ab])
            t_ln = C.act(lambda s: s.activation(tmp2, tmp2, AF.Ln,
                                                bias=1.0), deps=[t_ex])
            rgT2.end_write(t_ln)
            t_rl = C.dve(lambda v: v.tensor_scalar(
                tmp1, tmp1, 0.0, None, ALU.max), deps=[t_ab])
            t_ad = C.dve(lambda v: v.tensor_add(tmp1, tmp1, tmp2),
                         deps=[t_rl, t_ln])
            rgT1.end_write(tls + [t_rl, t_ad])
            rgT2.add_reader(t_ad)
            t_cl = C.dve(lambda v: v.tensor_scalar(
                dstg, tmp1, 1e-4, 10.0, ALU.add, ALU.min),
                deps=[t_ad] + rgDs.begin_write())
            rgT1.add_reader(t_cl)
            rgDs.end_write(t_cl)
            if bcut == "d2":
                cB_toks.append(t_cl)
                continue
            t_sp = C.dma_sync(lambda s, tt=tt, dsg=dsg: s.dma_start(
                dvv[:, dsg * 4:(dsg + 1) * 4, tt * 128:(tt + 1) * 128],
                dstg.rearrange("p (ds t) -> p ds t", t=128)), deps=[t_cl])
            rgDs.add_reader(t_sp)
            dscr_store[tt].append(t_sp)
            cB_toks += [t_ln, t_ad, t_cl, t_sp]
    regA.end_write([t_wdt] if t_wdt is not None else [])
    regA.add_reader(delta_mms)
    regC.end_write(cB_toks)
    regs_end_write([C.wslot[2]], xcb_all)
    if bcut in ("", "d2", "d3", "d5", "d6"):
        scr_toks += [t_cl]
    if _os.environ.get("MK_L3_PH", "ABCD") == "AB":
        C.sync(lambda s: s.nop(), deps=cB_toks + scr_toks + sum(P.values(), []))
        C.scr_reg.end_write(scr_toks)
        return

    C0f, TMPf, DELf = 16384, 19456, 20480
    XC0, Z0 = AR_C + 4096, AR_C + 5120
    Bdeps = regB.begin_write()
    Cdeps = regC.begin_write()
    Ddeps = regD.begin_write()
    rgDel = Region("l3del")
    rgXcc = Region("l3xcc")
    rgZc = Region("l3zc")
    rgDx = Region("l3dx")
    rgHs = [Region("l3hs0"), Region("l3hs1")]
    rgTmp = [Region("l3tp0"), Region("l3tp1")]
    rgDel.end_write(Ddeps)
    for rg in (rgXcc, rgZc, rgHs[0], rgHs[1], rgTmp[0], rgTmp[1]):
        rg.end_write(Cdeps)
    rowsc = rowb_alloc(C, 2048)
    brow = rowsc[0:1, 0:1024]
    crow = rowsc[0:1, 1024:2048]
    rgBr = Region("l3br")
    rgCr = Region("l3cr")
    rowd = C.rowp_reg.begin_write()
    rgBr.end_write(rowd)
    rgCr.end_write(rowd)
    xcv = W["xcscr"].rearrange("dc p t -> p dc t")
    zv = W["zscr"].rearrange("dc p t -> p dc t")
    y2v = W["y2scr"].rearrange("dc p t -> p dc t")
    prev_tok = [None, None]
    y2_store = [None] * 32
    slot_regs = [[C.wslot[0]], [C.wslot[1]], [C.wslot[2], C.wslot[3]],
                 [C.wslot[4]]]
    cC_toks = []
    t_dl_last = None
    fence = None
    use_fence = not _os.environ.get("MK_L3_NOFENCE")
    chunk_last = []
    for c in range(32):
        t0 = c * 64
        tt2 = c // 2
        if use_fence and c > 0:
            fence = C.sync(lambda s: s.nop(), deps=chunk_last)
            chunk_last = []
        fdep = [fence] if (use_fence and fence is not None) else []
        t_dl = C.dma_sync(lambda s, t0=t0: s.dma_start(
            basef[:, DELf:DELf + 1024].rearrange("p (dc t) -> p dc t", t=64),
            dvv[:, :, t0:t0 + 64]),
            deps=dscr_store[tt2] + rgDel.begin_write() + fdep)
        rgDel.end_write(t_dl)
        t_dl_last = t_dl
        t_xc = C.dma_sync(lambda s, t0=t0: s.dma_start(
            baseb[:, XC0:XC0 + 1024].rearrange("p (dc t) -> p dc t", t=64),
            xcv[:, :, t0:t0 + 64]),
            deps=xc_store + rgXcc.begin_write() + fdep)
        rgXcc.end_write(t_xc)
        t_zl = C.dma_sync(lambda s, t0=t0: s.dma_start(
            baseb[:, Z0:Z0 + 1024].rearrange("p (dc t) -> p dc t", t=64),
            zv[:, :, t0:t0 + 64]),
            deps=z_store + rgZc.begin_write())
        t_br = C.dma_sync(lambda s, t0=t0: s.dma_start(
            brow.rearrange("a (t n) -> a t n", n=16),
            W["xdbl"][t0:t0 + 64, 64:80].unsqueeze(0)),
            deps=[xdbl_store[tt2]] + rgBr.begin_write())
        rgBr.end_write(t_br)
        t_cr = C.dma_sync(lambda s, t0=t0: s.dma_start(
            crow.rearrange("a (t n) -> a t n", n=16),
            W["xdbl"][t0:t0 + 64, 80:96].unsqueeze(0)),
            deps=[xdbl_store[tt2]] + rgCr.begin_write())
        rgCr.end_write(t_cr)
        bb_toks, cb_toks = [], []
        for src_row, rg, dst0, out, t_r in ((brow, rgBr, 0, bb_toks, t_br),
                                            (crow, rgCr, 1024, cb_toks,
                                             t_cr)):
            wsl = regs_begin_write(slot_regs[dst0 // 1024])
            for q in (0, 1):
                wd5 = C.pb_reg[5].begin_write()
                mm = C.pe(lambda t, src_row=src_row, q=q: t.matmul(
                    C.pb[5][:, :], C.onesrow[:],
                    src_row[0:1, q * 512:(q + 1) * 512],
                    start=True, stop=True),
                    deps=wd5 + [t_r, C.t_onesr])
                C.pb_reg[5].end_write(mm)
                rg.add_reader(mm)
                ev = C.dve(lambda v, q=q, dst0=dst0: v.tensor_copy(
                    wbv[:, dst0 + q * 512:dst0 + (q + 1) * 512],
                    C.pb[5][:, :]), deps=[mm] + (wsl if q == 0 else []))
                C.pb_reg[5].add_reader(ev)
                out.append(ev)
            regs_end_write(slot_regs[dst0 // 1024], out)
        exp_toks = []
        aw = regA.begin_write()
        bw = regB.begin_write() if c > 0 else list(Bdeps)
        t_prea = None
        for hf in range(2):
            in0 = _sap(basef, DELf + 32 * hf, [[1, 32], [64, 16], [0, 16]])
            in1 = _sap(scrv, 0, [[0, 32], [16, 16], [1, 16]])
            pre = _sap(basef, 8192, [[256, 32], [16, 16], [1, 16]])
            t_prea = C.dve(lambda g, in0=in0, in1=in1, pre=pre:
                           g.tensor_tensor(pre, in0, in1, ALU.mult),
                           deps=[t_dl, t_An] +
                           (bw if hf == 0 else [exp_toks[0]]))
            rgDel.add_reader(t_prea)
            tex = C.act(lambda s, hf=hf: s.activation(
                baseb[:, AR_A + hf * 8192:AR_A + (hf + 1) * 8192],
                basef[:, 8192:16384],
                AF.Exp), deps=[t_prea] + (aw if hf == 0 else [])
                + [C.t_zmm])
            exp_toks.append(tex)
        regA.end_write(exp_toks)
        t_dx = C.dve(lambda v: v.tensor_tensor(
            scrb[:, 704:1728], basef[:, DELf:DELf + 1024],
            baseb[:, XC0:XC0 + 1024], ALU.mult),
            deps=[t_dl, t_xc] + rgDx.begin_write() +
            (sdeps if c == 0 else []))
        rgDx.end_write(t_dx)
        rgDel.add_reader(t_dx)
        rgXcc.add_reader(t_dx)
        in0 = _sap(scrb, 704, [[1, 64], [64, 16], [0, 16]])
        in1 = _sap(wbv, 0, [[16, 64], [0, 16], [1, 16]])
        buo = _sap(baseb, AR_B, [[256, 64], [16, 16], [1, 16]])
        t_bu = C.dve(lambda g, in0=in0, in1=in1, buo=buo:
                     g.tensor_tensor(buo, in0, in1, ALU.mult),
                     deps=[t_dx] + bb_toks + exp_toks)
        regs_add_reader(slot_regs[0], t_bu)
        rgDx.add_reader(t_bu)
        t_bc = C.dve(lambda g: g.tensor_scalar(
            baseb[:, AR_B:AR_B + 16384], baseb[:, AR_B:AR_B + 16384],
            -10.0, 10.0, ALU.max, ALU.min), deps=[t_bu])
        regB.end_write([t_prea, t_bu, t_bc])
        ybw = regs_begin_write(slot_regs[2])
        yb_toks = []
        for s in range(16):
            S = c * 16 + s
            b = S % 2
            subw = rgHs[b].begin_write()
            sub_o3 = []
            for tl in range(4):
                t = 4 * s + tl
                cur = basef[:, C0f + b * 1024 + tl * 256:
                            C0f + b * 1024 + (tl + 1) * 256]
                a_t = baseb[:, AR_A + t * 256:AR_A + (t + 1) * 256]
                bu_t = baseb[:, AR_B + t * 256:AR_B + (t + 1) * 256]
                dd = (subw if tl == 0 else [])
                if S == 0 and tl == 0:
                    o1 = C.dve(lambda e, cur=cur, bu_t=bu_t:
                               e.tensor_copy(cur, bu_t),
                               deps=[t_bc] + dd)
                else:
                    pb_ = (S - 1) % 2 if tl == 0 else b
                    ptl = 3 if tl == 0 else tl - 1
                    prev = basef[:, C0f + pb_ * 1024 + ptl * 256:
                                 C0f + pb_ * 1024 + (ptl + 1) * 256]
                    o1 = C.dver(lambda e, cur=cur, prev=prev, a_t=a_t:
                                e.tensor_tensor(cur, prev, a_t, ALU.mult),
                                deps=[prev_tok[0],
                                      exp_toks[t // 32], t_bc] + dd)
                    if tl == 0:
                        rgHs[pb_].add_reader(o1)
                    o1 = C.dver(lambda e, cur=cur, bu_t=bu_t:
                                e.tensor_tensor(cur, cur, bu_t, ALU.add),
                                deps=[o1])
                o3 = C.dver(lambda e, cur=cur: e.tensor_scalar(
                    cur, cur, -20.0, 20.0, ALU.max, ALU.min),
                    deps=[o1])
                prev_tok[0] = o3
                sub_o3.append(o3)
            rgHs[b].end_write(sub_o3)
            regA.add_reader(sub_o3)
            regB.add_reader(sub_o3)
            hsin = _sap(basef, C0f + b * 1024, [[16, 16], [256, 4], [1, 16]])
            cbin = _sap(wbv, 1024 + 64 * s, [[0, 16], [16, 4], [1, 16]])
            ytmp = _sap(basef, TMPf, [[64, 16], [16, 4], [1, 16]])
            t_yt = C.dve(lambda v, hsin=hsin, cbin=cbin, ytmp=ytmp:
                         v.tensor_tensor(ytmp, hsin, cbin, ALU.mult),
                         deps=[sub_o3[-1]] + cb_toks +
                         rgTmp[0].begin_write() + rgTmp[1].begin_write())
            rgHs[b].add_reader(t_yt)
            regs_add_reader(slot_regs[1], t_yt)
            t_yr = C.dve(lambda v, ytmp=ytmp, s=s: v.reduce_sum(
                _sap(wbf, 1024 + 4 * s, [[64, 16], [1, 4]]),
                ytmp, AX.X), deps=[t_yt] + (ybw if s == 0 else []))
            rgTmp[0].end_write(t_yt)
            rgTmp[1].end_write(t_yt)
            rgTmp[0].add_reader(t_yr)
            rgTmp[1].add_reader(t_yr)
            yb_toks.append(t_yr)
            cC_toks += sub_o3
        regs_end_write(slot_regs[2], yb_toks)
        t_sz = C.act(lambda s: s.activation(
            baseb[:, Z0:Z0 + 1024], baseb[:, Z0:Z0 + 1024], AF.Silu),
            deps=[t_zl])
        rgZc.end_write([t_zl, t_sz])
        tq = rgTmp[0].begin_write() + rgTmp[1].begin_write()
        in0 = _sap(baseb, XC0, [[64, 16], [1, 64]])
        in1 = _sap(scrv, 256, [[1, 16], [0, 64]])
        y2f = _sap(basef, TMPf, [[64, 16], [1, 64]])
        t_q1 = C.dve(lambda g, in0=in0, in1=in1, y2f=y2f: g.tensor_tensor(
            y2f, in0, in1, ALU.mult), deps=[t_xc, t_Dl] + tq)
        rgXcc.add_reader(t_q1)
        t_q2 = C.dve(lambda g, y2f=y2f: g.tensor_tensor(
            y2f, y2f, _sap(wbf, 1024, [[64, 16], [1, 64]]), ALU.add),
            deps=[t_q1] + yb_toks)
        regs_add_reader(slot_regs[2], t_q2)
        t_q3 = C.dve(lambda g: g.tensor_scalar(
            basef[:, TMPf:TMPf + 1024], basef[:, TMPf:TMPf + 1024],
            -50.0, 50.0, ALU.max, ALU.min), deps=[t_q2])
        y2w = regs_begin_write(slot_regs[3])
        t_q4 = C.dve(lambda g: g.tensor_tensor(
            wbv[:, 4096:5120], basef[:, TMPf:TMPf + 1024],
            baseb[:, Z0:Z0 + 1024], ALU.mult), deps=[t_q3, t_sz] + y2w)
        rgZc.add_reader(t_q4)
        rgTmp[0].end_write(t_q4)
        rgTmp[1].end_write(t_q4)
        regs_end_write(slot_regs[3], t_q4)
        t_ys = C.dma_sync(lambda s, t0=t0: s.dma_start(
            y2v[:, :, t0:t0 + 64],
            wbv[:, 4096:5120].rearrange("p (dc t) -> p dc t", t=64)),
            deps=[t_q4])
        regs_add_reader(slot_regs[3], t_ys)
        y2_store[c] = t_ys
        chunk_last = [t_ys, t_q4, prev_tok[0], yb_toks[-1],
                      exp_toks[-1], t_bc, t_sz]
        cC_toks += [t_xc, t_zl, t_sz, t_q4, t_ys]
    regC.end_write(cC_toks)
    regD.end_write([t_dl_last])
    regD.add_reader([t_dx])
    scr_toks += [t_dx]
    if _os.environ.get("MK_L3_PH", "ABCD") == "ABC":
        C.sync(lambda s: s.nop(), deps=cC_toks + y2_store + scr_toks + sum(P.values(), []))
        C.scr_reg.end_write(scr_toks)
        return

    Adeps = regA.begin_write()
    wov = ar(C, AR_A, 0, 16384).rearrange("p (dc e) -> p dc e", e=1024)
    t_wo = C.dma_sync(lambda s: s.dma_start(
        wov, W["WoutT"].rearrange("(dc p) e -> p dc e", p=128)),
        deps=P["WoutT"] + Adeps)
    ylb = [wb(C, 0, 2048), wb(C, 2048, 2048)]
    ylregs = [wb_slots(C, 0, 2048), wb_slots(C, 2048, 2048)]
    wo_reads = []
    for tt in range(NTT):
        pp = tt % 2
        t_yl = C.dma_sync(lambda s, tt=tt, pp=pp: s.dma_start(
            ylb[pp][:, :].rearrange("p (dc t) -> p dc t", t=128),
            y2v[:, :, tt * 128:(tt + 1) * 128]),
            deps=[y2_store[2 * tt], y2_store[2 * tt + 1]] +
            regs_begin_write(ylregs[pp]))
        regs_end_write(ylregs[pp], t_yl)
        xd = C.xres_reg[tt].begin_write()
        evs = []
        for dmh in range(2):
            bi = (tt * 2 + dmh) % 4
            bankd = C.pb_reg[bi].begin_write()
            mml = None
            for dc in range(16):
                mml = C.pe(lambda t, bi=bi, dc=dc, dmh=dmh, pp=pp: t.matmul(
                    C.pb[bi][:, :], ylb[pp][:, dc * 128:(dc + 1) * 128],
                    wov[:, dc, dmh * 512:(dmh + 1) * 512],
                    start=(dc == 0), stop=(dc == 15)),
                    deps=[t_yl, t_wo] + (list(bankd) if dc == 0 else []))
                regs_add_reader(ylregs[pp], mml)
            C.pb_reg[bi].end_write(mml)
            wo_reads.append(mml)
            sl = slice(tt * D + dmh * 512, tt * D + (dmh + 1) * 512)
            ev = C.dve(lambda v, bi=bi, sl=sl: v.tensor_add(
                C.x_res[:, sl], C.x_res[:, sl], C.pb[bi][:, :]),
                deps=[mml] + xd)
            C.pb_reg[bi].add_reader(ev)
            evs.append(ev)
        C.xres_reg[tt].end_write(evs)
    regA.end_write([t_wo])
    regA.add_reader(wo_reads)
    C.scr_reg.end_write(scr_toks)


def emit_l3_ffn(C, W, P):
    xn_deps = C.xnT_reg.read_deps()
    regA = C.ar_reg["A"]
    regB = C.ar_reg["B"]
    regC = C.ar_reg["C"]
    Adeps = regA.begin_write()
    Bdeps = regB.begin_write()
    Cdeps = regC.begin_write()
    rgGU = [Region("l3gu0"), Region("l3gu1")]
    rgDN = [Region("l3dn0"), Region("l3dn1")]
    rgH = Region("l3h")
    for rg in rgGU:
        rg.end_write(Bdeps)
    for rg in rgDN:
        rg.end_write(Adeps)
    rgH.end_write(Cdeps)
    sgf = [wb(C, 0, 1024).bitcast(F32), wb(C, 1024, 1024).bitcast(F32)]
    sgregs = [wb_slots(C, 0, 1024), wb_slots(C, 1024, 1024)]
    all_mms = []
    for u in range(10):
        e = u if u < 8 else None
        h2 = u - 8
        pp = u % 2
        guv = ar(C, AR_B, pp * 8192, 8192).rearrange(
            "p (g dc i) -> p g dc i", g=2, i=512)
        dnv = ar(C, AR_A, pp * 4096, 4096).rearrange(
            "p (ic h) -> p ic h", h=1024)
        if e is not None:
            src_g = W["egwT"][e].rearrange("(dc p) i -> p dc i", p=128)
            src_u = W["euwT"][e].rearrange("(dc p) i -> p dc i", p=128)
            src_d = W["edwT"][e].rearrange("(ic p) h -> p ic h", p=128)
            dep_g, dep_u, dep_d = (P[f"egwT{e}"], P[f"euwT{e}"],
                                   P[f"edwT{e}"])
        else:
            src_g = W["sgwT"].rearrange("(dc p) i -> p dc i", p=128)[
                :, :, h2 * 512:(h2 + 1) * 512]
            src_u = W["suwT"].rearrange("(dc p) i -> p dc i", p=128)[
                :, :, h2 * 512:(h2 + 1) * 512]
            src_d = W["sdwT"].rearrange("(ic p) h -> p ic h", p=128)[
                :, h2 * 4:(h2 + 1) * 4, :]
            dep_g, dep_u, dep_d = P["sgwT"], P["suwT"], P["sdwT"]
        t_lg = C.dma_sync(lambda s, guv=guv, src_g=src_g: s.dma_start(
            guv[:, 0], src_g), deps=dep_g + rgGU[pp].begin_write())
        t_lu = C.dma_sync(lambda s, guv=guv, src_u=src_u: s.dma_start(
            guv[:, 1], src_u), deps=dep_u + [t_lg])
        rgGU[pp].end_write([t_lg, t_lu])
        t_ld = C.dma_sync(lambda s, dnv=dnv, src_d=src_d: s.dma_start(
            dnv, src_d), deps=dep_d + rgDN[pp].begin_write())
        rgDN[pp].end_write(t_ld)
        hd = rgH.begin_write()
        h_toks = []
        for ic in range(4):
            for ts4 in range(4):
                bg = ts4 % 2
                bu_ = 2 + ts4 % 2
                bdg = C.pb_reg[bg].begin_write()
                bdu = C.pb_reg[bu_].begin_write()
                mmg = mmu = None
                for dc in range(8):
                    mmg = C.pe(lambda t, bg=bg, dc=dc, ic=ic, ts4=ts4,
                               guv=guv: t.matmul(
                                   C.pb[bg][:, :],
                                   guv[:, 0, dc, ic * 128:(ic + 1) * 128],
                                   xnT_tile(C, dc, ts4 * 512, 512),
                                   start=(dc == 0), stop=(dc == 7)),
                               deps=([t_lg] + list(bdg) +
                                     (xn_deps if u == 0 else []))
                               if dc == 0 else [])
                    mmu = C.pe(lambda t, bu_=bu_, dc=dc, ic=ic, ts4=ts4,
                               guv=guv: t.matmul(
                                   C.pb[bu_][:, :],
                                   guv[:, 1, dc, ic * 128:(ic + 1) * 128],
                                   xnT_tile(C, dc, ts4 * 512, 512),
                                   start=(dc == 0), stop=(dc == 7)),
                               deps=[t_lu] + (list(bdu) if dc == 0 else []))
                    all_mms += [mmg, mmu]
                rgGU[pp].add_reader([mmg, mmu])
                C.pb_reg[bg].end_write(mmg)
                C.pb_reg[bu_].end_write(mmu)
                sp = ts4 % 2
                t_sl = C.act(lambda s, bg=bg, sp=sp: s.activation(
                    sgf[sp][:, :], C.pb[bg][:, :], AF.Silu),
                    deps=[mmg] + regs_begin_write(sgregs[sp]))
                regs_end_write(sgregs[sp], t_sl)
                C.pb_reg[bg].add_reader(t_sl)
                t_h = C.dve(lambda v, bu_=bu_, sp=sp, ic=ic, ts4=ts4:
                            v.tensor_tensor(
                                ar(C, AR_C, ic * 2048 + ts4 * 512, 512),
                                C.pb[bu_][:, :], sgf[sp][:, :], ALU.mult),
                            deps=[t_sl, mmu] +
                            (hd if ic == 0 and ts4 == 0 else []))
                C.pb_reg[bu_].add_reader(t_h)
                regs_add_reader(sgregs[sp], t_h)
                h_toks.append(t_h)
        rgH.end_write(h_toks)
        for tt in range(NTT):
            for dmh in range(2):
                bi = 4 + (tt * 2 + dmh) % 2
                bankd = C.pb_reg[bi].begin_write()
                mml = None
                for ic in range(4):
                    mml = C.pe(lambda t, bi=bi, ic=ic, tt=tt, dmh=dmh,
                               dnv=dnv: t.matmul(
                                   C.pb[bi][:, :],
                                   ar(C, AR_C, ic * 2048 + tt * 128, 128),
                                   dnv[:, ic, dmh * 512:(dmh + 1) * 512],
                                   start=(ic == 0), stop=(ic == 3)),
                               deps=[t_ld, h_toks[4 * ic + tt // 4]] +
                               (list(bankd) if ic == 0 else []))
                    rgDN[pp].add_reader(mml)
                    rgH.add_reader(mml)
                    all_mms.append(mml)
                C.pb_reg[bi].end_write(mml)
                xd = C.xres_reg[tt].begin_write()
                sl = slice(tt * D + dmh * 512, tt * D + (dmh + 1) * 512)
                if e is not None:
                    gcol = C.l3_gates[:, tt * 8 + e:tt * 8 + e + 1]
                    ev = C.dve(lambda v, bi=bi, sl=sl, gcol=gcol:
                               v.scalar_tensor_tensor(
                                   C.x_res[:, sl], C.pb[bi][:, :], gcol,
                                   C.x_res[:, sl], ALU.mult, ALU.add),
                               deps=[mml, C.l3_gate_toks[tt]] + xd)
                else:
                    ev = C.dve(lambda v, bi=bi, sl=sl: v.tensor_add(
                        C.x_res[:, sl], C.x_res[:, sl], C.pb[bi][:, :]),
                        deps=[mml] + xd)
                C.pb_reg[bi].add_reader(ev)
                C.xres_reg[tt].end_write(ev)
    regA.end_write(all_mms[-1:])
    regB.end_write(all_mms[-1:])
    regC.end_write(all_mms[-1:])


def emit_layer3(C, W):
    parts = _os.environ.get("MK_L3_PARTS", "af")
    P = {}
    if "a" in parts:
        P.update(emit_l3_prep_attn(C, W))
    emit_ln(C)
    if "f" in parts:
        emit_l3_gating(C, W)
    if "a" in parts:
        emit_l3_attn(C, W, P)
    if "f" in parts:
        P.update(emit_l3_prep_ffn(C, W))
        emit_l3_gates_final(C, W)
        emit_l3_ffn(C, W, P)
    layer_reset_pools(C)



LAYER_DECL = {0: l0_declare, 1: l1_declare, 2: l2_declare,
              3: l3_declare}
LAYER_EMIT = {0: emit_layer0, 1: emit_layer1, 2: emit_layer2,
              3: emit_layer3}

LAYER_INPUT_NAMES = {
    0: ["L0_attn_phase", "L0_attn_Wq", "L0_attn_bq", "L0_attn_Wk",
        "L0_attn_bk", "L0_attn_Wv", "L0_attn_bv", "L0_attn_Wo", "L0_attn_bo",
        "L0_ffn_wi", "L0_ffn_bi", "L0_ffn_wo", "L0_ffn_bo", "L0_ffn_lora_A",
        "L0_ffn_lora_B", "L0_ffn_gate_w", "L0_ffn_gate_b"],
    1: ["L1_attn_W_qkv", "L1_attn_W_out", "L1_attn_b_out", "L1_ffn_k0",
        "L1_ffn_k1", "L1_ffn_k2", "L1_ffn_s0", "L1_ffn_s1", "L1_ffn_s2"],
    2: ["L2_attn_Wq", "L2_attn_Wk", "L2_attn_Wv", "L2_attn_Wp", "L2_attn_bp",
        "L2_attn_rel_table", "L2_ffn_w_gate", "L2_ffn_w_up", "L2_ffn_w_down"],
    3: ["L3_attn_W_in", "L3_attn_conv_w", "L3_attn_conv_b", "L3_attn_W_x",
        "L3_attn_W_dt", "L3_attn_b_dt", "L3_attn_A_log", "L3_attn_D_param",
        "L3_attn_W_out", "L3_ffn_gate_w", "L3_ffn_lb_bias",
        "L3_ffn_expert_gate_w", "L3_ffn_expert_up_w", "L3_ffn_expert_down_w",
        "L3_ffn_shared_gate_w", "L3_ffn_shared_up_w", "L3_ffn_shared_down_w"],
}

_PROG_CACHE = {}


def build_program(layers):
    key = tuple(layers)
    if key in _PROG_CACHE:
        return _PROG_CACHE[key]
    nc = bass.Bass("TRN2", target_bir_lowering=False)
    x_in = nc.dram_tensor("x_in", [T, D], F32, kind="ExternalInput").ap()
    x_out = nc.dram_tensor("x_out", [T, D], F32, kind="ExternalOutput").ap()
    with ExitStack() as stack:
        C = make_ctx(nc, stack)
        Ws = {l: LAYER_DECL[l](nc) for l in layers}
        emit_load_x(C, x_in)
        for l in layers:
            LAYER_EMIT[l](C, Ws[l])
        emit_store_x(C, x_out)
        C.prog.finalize(nc, stack)
    names = []
    for l in layers:
        names += LAYER_INPUT_NAMES[l]
    _PROG_CACHE[key] = (nc, names)
    return nc, names


LAUNCH_PLAN = [[0, 1, 2, 3]]


def _input_for(name, arr):
    a = np.asarray(arr)
    if a.dtype == np.bool_:
        a = a.astype(np.uint8)
    elif a.dtype != np.float32:
        a = a.astype(np.float32)
    if a.ndim == 0:
        a = a.reshape(1)
    return np.ascontiguousarray(a)


def kernel(**inputs):
    x = np.ascontiguousarray(np.asarray(inputs["L0_attn_x"], np.float32))
    cur = [x[c] for c in range(NCORES)]
    for layers in LAUNCH_PLAN:
        nc, names = build_program(layers)
        in_maps = []
        for c in range(NCORES):
            m = {"x_in": cur[c]}
            for n in names:
                m[n] = _input_for(n, inputs[n])
            in_maps.append(m)
        res = run_bass_kernel_spmd(nc, in_maps, core_ids=list(range(NCORES)))
        cur = [np.asarray(res.results[c]["x_out"]) for c in range(NCORES)]
    return np.stack(cur, axis=0)
